# version 13
# baseline (speedup 1.0000x reference)
import sys
sys.path.insert(0, "/opt/trn_rl_repo")
from contextlib import ExitStack

import numpy as np
import ml_dtypes
import concourse.bass as bass
import concourse.mybir as mybir
from concourse.bass_utils import run_bass_kernel_spmd

F32 = mybir.dt.float32
BF16 = mybir.dt.bfloat16
FP16 = mybir.dt.float16
AF = mybir.ActivationFunctionType
ALU = mybir.AluOpType

H = 1024
DIN = 2048
DH = 4096
NST = 16
RNK = 64
L = 4096
T = 512
NCH = L // T
HK = H // 128
GT = DIN // 128
MT = DH // 128
WMT = (2 * DIN) // 128
KM = 4
KF = 3
MAGIC = float(np.float32(1.5 * 2 ** 23))
EPS = 1e-5


class Sem:
    def __init__(self, stack, nc, name, step=1):
        self.h = stack.enter_context(nc.semaphore(name))
        self.v = 0
        self.step = step

    def inc(self, inst, k=None):
        k = self.step if k is None else k
        inst.then_inc(self.h, k)
        self.v += k
        return self.v


def t3(ap, k):
    return ap.rearrange("p (k t) -> p k t", k=k)


def build(debug=False, n_chunks=NCH, stop_after=None):
    nc = bass.Bass("TRN2", target_bir_lowering=False, debug=False,
                   detect_race_conditions=False)

    x_h = nc.dram_tensor("x", [H, L], F32, kind="ExternalInput").ap()
    winT_h = nc.dram_tensor("winT", [H, 2 * DIN], F32, kind="ExternalInput").ap()
    wxT_h = nc.dram_tensor("wxT", [DIN, 96], F32, kind="ExternalInput").ap()
    wdtT_h = nc.dram_tensor("wdtT", [RNK, DIN], F32, kind="ExternalInput").ap()
    woutT_h = nc.dram_tensor("woutT", [DIN, H], F32, kind="ExternalInput").ap()
    w1T_h = nc.dram_tensor("w1T", [H, DH], F32, kind="ExternalInput").ap()
    w2T_h = nc.dram_tensor("w2T", [DH, H], F32, kind="ExternalInput").ap()
    bdt_h = nc.dram_tensor("bdt", [128, GT], F32, kind="ExternalInput").ap()
    alog_h = nc.dram_tensor("alog", [128, GT * NST], F32, kind="ExternalInput").ap()
    dvec_h = nc.dram_tensor("dvec", [128, GT], F32, kind="ExternalInput").ap()
    cw1_h = nc.dram_tensor("cw1", [128, GT * KM], F32, kind="ExternalInput").ap()
    cw2_h = nc.dram_tensor("cw2", [128, MT * KF], F32, kind="ExternalInput").ap()
    cb2_h = nc.dram_tensor("cb2", [128, MT], F32, kind="ExternalInput").ap()

    out_h = nc.dram_tensor("out", [H, L], F32, kind="ExternalOutput").ap()

    wbf_h = nc.dram_tensor("wbf", [WMT, H, 128], BF16, kind="Internal").ap()
    wob_h = nc.dram_tensor("wob", [HK, DIN, 128], BF16, kind="Internal").ap()
    wq1_h = nc.dram_tensor("wq1", [MT, H, 128], BF16, kind="Internal").ap()
    wq2_h = nc.dram_tensor("wq2", [MT, 128, H], BF16, kind="Internal").ap()
    bcr_h = nc.dram_tensor("bcr", [32, T], BF16, kind="Internal").ap()
    h2sp_h = nc.dram_tensor("h2sp", [MT, 128, T], BF16, kind="Internal").ap()
    xsp_h = nc.dram_tensor("xsp", [H, L], F32, kind="Internal").ap()

    dbg_handles = {}

    def dbg_out(name, shape, dtype):
        h = nc.dram_tensor(name, shape, dtype, kind="ExternalOutput").ap()
        dbg_handles[name] = h
        return h

    with ExitStack() as st:
        def sb(name, shape, dt=F32):
            return st.enter_context(nc.sbuf_tensor(name, shape, dt))[:]

        def psm(name, shape, dt=F32):
            return st.enter_context(nc.psum_tensor(name, shape, dt))[:]

        s_pe = Sem(st, nc, "s_pe")
        s_act = Sem(st, nc, "s_act")
        s_dve = Sem(st, nc, "s_dve")
        s_gps = Sem(st, nc, "s_gps")
        s_spd = Sem(st, nc, "s_spd", 16)
        s_gpd = Sem(st, nc, "s_gpd", 16)
        s_sp = [Sem(st, nc, "s_sp0", 16), Sem(st, nc, "s_sp1", 16)]
        s_gp = [Sem(st, nc, "s_gp0", 16), Sem(st, nc, "s_gp1", 16)]
        s_spx = s_sp + [Sem(st, nc, "s_sp2", 16), Sem(st, nc, "s_sp3", 16)]
        s_gpx = s_gp + [Sem(st, nc, "s_gp2", 16), Sem(st, nc, "s_gp3", 16)]

        xf = sb("xf", [128, HK * T], BF16)
        scf = sb("scf", [128, HK * T])
        nb1 = sb("nb1", [128, HK * T], BF16)
        hp = [sb(f"hp{i}", [128, T + KM - 1], BF16) for i in range(2)]
        mt3 = sb("mt3", [128, GT * (KM - 1)], BF16)
        hsb = sb("hsb", [128, GT * T], BF16)
        resb = sb("resb", [128, GT * T], BF16)
        dlg = [sb(f"dlg{i}", [128, T], FP16) for i in range(2)]
        dxg = sb("dxg", [128, T], BF16)
        xdbq = sb("xdbq", [96, T], BF16)
        Brep = sb("Brep", [128, NST * T], BF16)
        Crep = sb("Crep", [128, NST * T], BF16)
        dAq = [sb(f"dAq{i}", [128, 2 * T], BF16) for i in range(3)]
        dBq = [sb(f"dBq{i}", [128, 2 * T], BF16) for i in range(3)]
        NSL = 3
        zq = sb("zq", [128, 2 * T], BF16)
        carry = sb("carry", [128, GT * NST])
        yb = sb("yb", [128, GT * T], BF16)
        sp_a = sb("sp_a", [128, T])
        sp_r = [sb(f"sp_r{i}", [128, T]) for i in range(2)]
        sp_u = [sb(f"sp_u{i}", [128, T]) for i in range(2)]
        xpr = sb("xpr", [128, HK * T])
        xq1 = [sb(f"xq1_{i}", [128, HK * (T + 2)], BF16) for i in range(2)]
        mxsq = sb("mxsq", [128, T], BF16)
        zrot = sb("zrot", [128, 128], BF16)
        ftail = sb("ftail", [128, MT], BF16)
        h2t = [sb(f"h2t_{i}", [128, T], BF16) for i in range(2)]
        mx2 = sb("mx2", [128, T], BF16)
        ofb = nb1
        wpA = [sb(f"wpA{i}", [128, HK * 128], BF16) for i in range(2)]
        wpB = [sb("wpB0", [128, GT * 128], BF16)]
        wpC = [wpA[0][:, 0:H], wpA[1][:, 0:H]]
        wpC4 = [wpA[0][:, 0:512], wpA[0][:, 512:1024],
                wpA[1][:, 0:512], wpA[1][:, 512:1024]]
        W1S = [wpA[0], wpA[1],
               wpB[0][:, 0:HK * 128], wpB[0][:, HK * 128:2 * HK * 128]]
        H2S = [h2t[0], h2t[1],
               scf.bitcast(BF16)[:, 3584:4096], scf.bitcast(BF16)[:, 4096:4608]]
        dxgs = [dxg, scf.bitcast(BF16)[:, 3072:3584]]
        hp2 = [hp[0][:, 0:T + KF], hp[1][:, 0:T + KF],
               scf.bitcast(BF16)[:, 4608:4608 + T + KF],
               scf.bitcast(BF16)[:, 5124:5124 + T + KF]]
        h2A = [xpr.bitcast(BF16)[:, i * 512:(i + 1) * 512] for i in range(16)]
        ones_a = sb("ones_a", [128, 1])
        ones_b = sb("ones_b", [128, 1])
        onesh_a = sb("onesh_a", [128, 1], BF16)
        onesh_b = sb("onesh_b", [128, 1], BF16)
        ones1 = sb("ones1", [1, 128])
        idnt = [sb(f"idnt{i}", [128, 128], BF16) for i in range(2)]
        iof = sb("iof", [128, 128])
        iop = sb("iop", [128, 1])
        maskd = sb("maskd", [128, 128])
        A_sb = sb("A_sb", [128, GT * NST])
        D_sb = sb("D_sb", [128, GT])
        bdt_sb = sb("bdt_sb", [128, GT])
        cw1_sb = sb("cw1_sb", [128, GT * KM])
        cw2_sb = sb("cw2_sb", [128, MT * KF])
        cb2_sb = sb("cb2_sb", [128, MT])
        diag = [sb(f"diag{i}", [128, 128], BF16) for i in range(KM * 2)]
        dgw = diag + [scf.bitcast(BF16)[:, 5640 + i * 128:5768 + i * 128]
                      for i in range(10)]
        wxq = sb("wxq", [128, GT * 96], BF16)
        wdtq = sb("wdtq", [RNK, GT * 128], BF16)
        swrow = sb("swrow", [1, 256])
        swb = sb("swb", [128, 2])
        r_m = Crep.bitcast(F32)[0:1, 2048:2560]
        r_r = Crep.bitcast(F32)[0:1, 2560:3072]
        r_t2 = Crep.bitcast(F32)[0:1, 3072:3584]
        r_is = Crep.bitcast(F32)[0:1, 3584:4096]
        r_t1 = sb("r_t1", [1, T])
        r_mx = sb("r_mx", [1, T])
        r_s = sb("r_s", [1, T])
        mxq = sb("mxq", [128, 8])

        pb = [psm(f"pb{i}", [128, 512]) for i in range(8)]

        block = st.enter_context(nc.Block())

        def PE(f): block.tensor(f)
        def VE(f): block.vector(f)
        def SC(f): block.scalar(f)
        def GP(f): block.gpsimd(f)
        def SY(f): block.sync(f)

        def barrier():
            all_s = [(s_pe,), (s_act,), (s_dve,), (s_gps,), (s_spd,), (s_gpd,),
                     (s_sp[0],), (s_sp[1],), (s_gp[0],), (s_gp[1],),
                     (s_spx[2],), (s_spx[3],), (s_gpx[2],), (s_gpx[3],)]
            vals = [(s, s.v) for (s,) in all_s]

            def mk(skip):
                def f(eng):
                    for s, v in vals:
                        if s is not skip and v > 0:
                            eng.wait_ge(s.h, v)
                return f
            PE(mk(s_pe))
            VE(mk(s_dve))
            SC(mk(s_act))
            GP(mk(s_gps))
            SY(mk(None))

        def dbg_tap(name, ap_in, rows, cols, dtype):
            if not debug:
                return
            h = dbg_out(name, [rows, cols], dtype)
            barrier()

            def f2(sy):
                s_spd.inc(sy.dma_start(out=h, in_=ap_in))
            SY(f2)
            barrier()

        def setup_sync(sy):
            s_spd.inc(sy.dma_start(out=bdt_sb, in_=bdt_h))
            s_spd.inc(sy.dma_start(out=A_sb, in_=alog_h))
            s_spd.inc(sy.dma_start(out=D_sb, in_=dvec_h))
            s_spd.inc(sy.dma_start(out=cw1_sb, in_=cw1_h))
            s_spd.inc(sy.dma_start(out=cw2_sb, in_=cw2_h))
            s_spd.inc(sy.dma_start(out=cb2_sb, in_=cb2_h))
        SY(setup_sync)

        def setup_gps(g):
            g.iota(iof, [[1, 128]], channel_multiplier=0,
                   allow_small_or_imprecise_dtypes=True)
            g.iota(iop, [[0, 1]], channel_multiplier=1,
                   allow_small_or_imprecise_dtypes=True)
            g.memset(swrow, 1.0)
            g.memset(zrot, 0.0)
            g.memset(xq1[0], 0.0)
            g.memset(xq1[1], 0.0)
            g.memset(ones_a, 1.0)
            g.memset(ones_b, 1.0)
            g.memset(onesh_a, 1.0)
            g.memset(onesh_b, 1.0)
            s_gps.inc(g.memset(ones1, 1.0))
        GP(setup_gps)

        def setup_ve(v):
            v.wait_ge(s_gps.h, s_gps.v)
            v.tensor_scalar(out=maskd, in0=iof, scalar1=iop, scalar2=None,
                            op0=ALU.is_equal)
            v.tensor_copy(out=idnt[0], in_=maskd)
            s_dve.inc(v.tensor_copy(out=idnt[1], in_=maskd))
        VE(setup_ve)

        def setup_sc(s):
            s.wait_ge(s_spd.h, s_spd.v)
            s_act.inc(s.activation(out=A_sb, in_=A_sb, func=AF.Exp))
        SC(setup_sc)

        def setup_ve2(v):
            v.wait_ge(s_act.h, s_act.v)
            s_dve.inc(v.tensor_scalar(out=A_sb, in0=A_sb, scalar1=-1.0,
                                      scalar2=None, op0=ALU.mult))
        VE(setup_ve2)
        barrier()

        def res_w_sync(sy):
            for k in range(GT):
                s_spd.inc(sy.dma_start(out=scf[:, k * 96:(k + 1) * 96],
                                       in_=wxT_h[k * 128:(k + 1) * 128, :]))
            for k in range(GT):
                s_spd.inc(sy.dma_start(
                    out=scf[0:RNK, GT * 96 + k * 128:GT * 96 + (k + 1) * 128],
                    in_=wdtT_h[:, k * 128:(k + 1) * 128]))
        SY(res_w_sync)

        def res_w_ve(v):
            v.wait_ge(s_spd.h, s_spd.v)
            v.tensor_copy(out=wxq, in_=scf[:, 0:GT * 96])
            s_dve.inc(v.tensor_copy(
                out=wdtq, in_=scf[0:RNK, GT * 96:GT * 96 + GT * 128]))
        VE(res_w_ve)
        barrier()

        def conv_weight(src_h, rows, n_m, dst_h, quant=False):
            kb_n = rows // 128
            width = n_m * 128
            for kb in range(kb_n):
                def f_sy(sy, kb=kb):
                    sy.wait_ge(s_dve.h, s_dve.v)
                    s_spd.inc(sy.dma_start(
                        out=scf[:, 0:width],
                        in_=src_h[kb * 128:(kb + 1) * 128, 0:width]))
                SY(f_sy)

                def f_ve(v, kb=kb):
                    v.wait_ge(s_spd.h, s_spd.v)
                    v.wait_ge(s_gpd.h, s_gpd.v)
                    if not quant:
                        s_dve.inc(v.tensor_copy(out=nb1[:, 0:width],
                                                in_=scf[:, 0:width]))
                    else:
                        v.tensor_scalar(out=scf[:, 0:width], in0=scf[:, 0:width],
                                        scalar1=swb[:, 0:1], scalar2=None,
                                        op0=ALU.mult)
                        v.tensor_scalar(out=scf[:, 0:width], in0=scf[:, 0:width],
                                        scalar1=MAGIC, scalar2=MAGIC,
                                        op0=ALU.add, op1=ALU.subtract)
                        v.tensor_scalar(out=scf[:, 0:width], in0=scf[:, 0:width],
                                        scalar1=1.0, scalar2=-1.0,
                                        op0=ALU.min, op1=ALU.max)
                        s_dve.inc(v.tensor_scalar(out=nb1[:, 0:width],
                                                  in0=scf[:, 0:width],
                                                  scalar1=swb[:, 1:2],
                                                  scalar2=None, op0=ALU.mult))
                VE(f_ve)

                def f_gp(g, kb=kb):
                    g.wait_ge(s_dve.h, s_dve.v)
                    dst = dst_h[:, kb * 128:(kb + 1) * 128, :]
                    dst = dst.rearrange("m p c -> p m c")
                    s_gpd.inc(g.dma_start(
                        out=dst,
                        in_=nb1[:, 0:width].rearrange("p (m c) -> p m c", c=128)))
                GP(f_gp)
            barrier()

        def wsum_pass(src_h, rows, width, total):
            kb_n = rows // 128

            def z_ve(v):
                s_dve.inc(v.memset(xpr[:, 0:1], 0.0))
            VE(z_ve)
            for kb in range(kb_n):
                def f_sy(sy, kb=kb):
                    sy.wait_ge(s_act.h, s_act.v)
                    s_spd.inc(sy.dma_start(
                        out=scf[:, 0:width],
                        in_=src_h[kb * 128:(kb + 1) * 128, 0:width]))
                SY(f_sy)

                def f_sc(s, kb=kb):
                    s.wait_ge(s_spd.h, s_spd.v)
                    s.wait_ge(s_dve.h, s_dve.v)
                    s_act.inc(s.activation(out=nb1[:, 0:width],
                                           in_=scf[:, 0:width], func=AF.Abs,
                                           accum_out=sp_a[:, 0:1]))
                SC(f_sc)

                def f_ve(v, kb=kb):
                    v.wait_ge(s_act.h, s_act.v)
                    s_dve.inc(v.tensor_tensor(out=xpr[:, 0:1], in0=xpr[:, 0:1],
                                              in1=sp_a[:, 0:1], op=ALU.add))
                VE(f_ve)

            def f_pe(t):
                t.wait_ge(s_dve.h, s_dve.v)
                s_pe.inc(t.matmul(pb[6][0:1, 0:1], xpr[:, 0:1], ones_a,
                                  start=True, stop=True))
            PE(f_pe)

            def f_ve2(v):
                v.wait_ge(s_pe.h, s_pe.v)
                s_dve.inc(v.tensor_scalar(out=swrow[:, 0:1], in0=pb[6][0:1, 0:1],
                                          scalar1=1.0 / total, scalar2=1e-5,
                                          op0=ALU.mult, op1=ALU.max))
            VE(f_ve2)

            def f_sc_r(s):
                s.wait_ge(s_dve.h, s_dve.v)
                s.activation(out=swrow[:, 1:2], in_=swrow[:, 0:1], func=AF.Ln)
                s_act.inc(s.activation(out=swrow[:, 128:129], in_=swrow[:, 1:2],
                                       func=AF.Exp, scale=-1.0))
            SC(f_sc_r)

            def f_pe2(t):
                t.wait_ge(s_dve.h, s_dve.v)
                t.wait_ge(s_act.h, s_act.v)
                t.matmul(pb[6][:, 2:3], ones1, swrow[:, 128:129], start=True, stop=True)
                s_pe.inc(t.matmul(pb[6][:, 3:4], ones1, swrow[:, 0:1],
                                  start=True, stop=True))
            PE(f_pe2)

            def f_ve3(v):
                v.wait_ge(s_pe.h, s_pe.v)
                v.tensor_copy(out=swb[:, 0:1], in_=pb[6][:, 2:3])
                s_dve.inc(v.tensor_copy(out=swb[:, 1:2], in_=pb[6][:, 3:4]))
            VE(f_ve3)
            barrier()

        conv_weight(winT_h, H, WMT, wbf_h)
        conv_weight(woutT_h, DIN, HK, wob_h)
        if stop_after != "front":
            wsum_pass(w1T_h, H, DH, H * DH)
            dbg_tap("d_swb1", swb[:, 0:2], 128, 2, F32)
            conv_weight(w1T_h, H, MT, wq1_h, quant=True)
            wsum_pass(w2T_h, DH, H, DH * H)
            for kb in range(DH // 128):
                def f_sy(sy, kb=kb):
                    sy.wait_ge(s_dve.h, s_dve.v)
                    s_spd.inc(sy.dma_start(out=scf[:, 0:H],
                                           in_=w2T_h[kb * 128:(kb + 1) * 128, :]))
                SY(f_sy)

                def f_ve(v, kb=kb):
                    v.wait_ge(s_spd.h, s_spd.v)
                    v.wait_ge(s_gpd.h, s_gpd.v)
                    v.tensor_scalar(out=scf[:, 0:H], in0=scf[:, 0:H],
                                    scalar1=swb[:, 0:1], scalar2=None, op0=ALU.mult)
                    v.tensor_scalar(out=scf[:, 0:H], in0=scf[:, 0:H],
                                    scalar1=MAGIC, scalar2=MAGIC,
                                    op0=ALU.add, op1=ALU.subtract)
                    v.tensor_scalar(out=scf[:, 0:H], in0=scf[:, 0:H],
                                    scalar1=1.0, scalar2=-1.0,
                                    op0=ALU.min, op1=ALU.max)
                    s_dve.inc(v.tensor_scalar(out=nb1[:, 0:H], in0=scf[:, 0:H],
                                              scalar1=swb[:, 1:2], scalar2=None,
                                              op0=ALU.mult))
                VE(f_ve)

                def f_gp(g, kb=kb):
                    g.wait_ge(s_dve.h, s_dve.v)
                    s_gpd.inc(g.dma_start(out=wq2_h[kb, :, :], in_=nb1[:, 0:H]))
                GP(f_gp)
            barrier()

        NPAIR = NST // 2

        def merged_stage(c, w, n_chunks):
            cheap = (w is not None and w == n_chunks)
            WC = 2 if cheap else T
            V = {}
            ops = {"pe": [], "act": [], "dve": [], "gps": []}

            win = {"pe": {}, "act": {}, "dve": {}, "gps": {}}
            if w is not None:
                def at(e, q, item):
                    win[e].setdefault(q, []).append(item)
                for m in range(MT):
                    qa = min(6, m // 5)
                    at("pe", qa, ("w1", m, 1))
                    at("act", qa, ("wev", m, 1))
                    at("dve", qa, ("wdiag", m, 1))
                    at("pe", qa, ("wconv", m, 1))
                    at("act", qa, ("wsilu", m, 1))
                    at("dve", min(7, qa + 1), ("wmx", m, 1))
                at("pe", 7, ("btr1", 0, 1))
                at("dve", 7, ("bred", 0, 1))
                at("pe", 7, ("btr2", 0, 1))
                at("act", 7, ("bcp", 0, 1))
                at("act", 7, ("bsqrt", 0, 1))
                at("dve", 7, ("bscale", 0, 1))
                at("pe", 7, ("bbc", 0, 1))
                at("dve", 7, ("bcopy", 0, 1))
                for quad in range(2):
                    for k in range(MT):
                        idx = quad * MT + k
                        qc = 8 + idx // 8
                        if quad == 0:
                            at("gps", qc, ("wmul", (0, k), 1))
                            at("dve", qc, ("wxq", (0, k), 1))
                        at("pe", qc, ("w2", (quad, k), 1))
                    at("act", min(15, 8 + (quad * MT + MT - 1) // 8),
                       ("wofev", quad, 1))


            if c is not None:
                ops["pe"] += [("wdt", 0, 1), ("wdt", 1, 1)]
                ops["act"] += ([("sp", 0, 1)]
                               + [("dA", (0, p), 1) for p in range(NSL)])
                ops["dve"] += [("dlt", 0, 1), ("dx", 0, 1)]
                for g in range(GT):
                    scan_pe = ([("ymm", (g, p), 1) for p in range(NPAIR)]
                               + [("hsd", g, 1)])
                    if g + 2 < GT:
                        scan_pe.append(("wdt", g + 2, 1))
                    ops["pe"] += scan_pe + win["pe"].get(g - 1, [])
                    av = [("dA", (g, p), 1) for p in range(NSL, NPAIR)]
                    if g + 1 < GT:
                        av += ([("sp", g + 1, 1)]
                               + [("dA", (g + 1, p), 1) for p in range(NSL)])
                    av.append(("hsdc", g, 1))
                    ops["act"] += av + win["act"].get(g - 1, [])
                    dv = []
                    for p in range(NPAIR):
                        if p == 6 and g + 1 < GT:
                            dv += [("dlt", g + 1, 1), ("dx", g + 1, 1)]
                        if c == 0:
                            dv.append(("cz", (g, p), 1))
                        dv += [("dbq", (g, p), 1), ("scan", (g, p), 1),
                               ("pq", (g, p), 1)]
                    dv.append(("yev", g, 1))
                    ops["dve"] += dv + win["dve"].get(g - 1, [])
                    ops["gps"] += win["gps"].get(g - 1, [])
                for e in ("pe", "act", "dve", "gps"):
                    for q in sorted(win[e]):
                        if q >= GT - 1:
                            ops[e] += win[e][q]
            else:
                for e in ("pe", "act", "dve", "gps"):
                    for q in sorted(win[e]):
                        ops[e] += win[e][q]
            full = ops

            base = {"pe": s_pe.v, "act": s_act.v, "dve": s_dve.v,
                    "gps": s_gps.v}
            for e in ("pe", "act", "dve", "gps"):
                vv = base[e]
                for kind, key, n in full[e]:
                    vv += n
                    V[(kind, key)] = vv
            if w is not None:
                gp_base = [s_gp[0].v, s_gp[1].v]
                cnt4 = [s.v for s in s_spx]
                wp_done = {}
                for m in range(MT):
                    sl = m % 4
                    cnt4[sl] += 16
                    wp_done[m] = (sl, cnt4[sl])
                wld_done = {}
                for quad in range(2):
                    for k in range(MT):
                        sl = (quad * MT + k) % 4
                        cnt4[sl] += 16
                        wld_done[(quad, k)] = (sl, cnt4[sl])
                cntg = [s.v for s in s_gpx]
                spill_done = {}
                for m in range(MT):
                    cntg[m % 2] += 16
                    spill_done[m] = (m % 2, cntg[m % 2])
                ldh2_done, splq_done, ldxq_done = {}, {}, {}
                for k in range(MT):
                    p = k % 4
                    cntg[p] += 16
                    ldh2_done[k] = (p, cntg[p])
                    cntg[p] += 16
                    splq_done[k] = (p, cntg[p])
                for k in range(MT):
                    p = k % 4
                    cntg[p] += 16
                    ldxq_done[k] = (p, cntg[p])
                xq_cur = xq1[w % 2] if not cheap else None
                xq_prv = xq1[(w - 1) % 2] if w > 0 else None
                if cheap:
                    xq_prv = xq1[(w - 1) % 2]
            TE = T + 2

            def pe_emit(t):
                for kind, key, n in full["pe"]:
                    if kind == "wdt":
                        g = key
                        if g >= 2:
                            t.wait_ge(s_act.h, V[("sp", g - 2)])
                        inst = t.matmul(pb[g % 2][:, 0:T],
                                        wdtq[0:RNK, g * 128:(g + 1) * 128],
                                        xdbq[0:RNK, 0:T], start=True, stop=True)
                        inst.then_inc(s_pe.h, 1)
                    elif kind == "ymm":
                        g, p = key
                        t.wait_ge(s_dve.h, V[("pq", (g, p))])
                        if p == 0 and g >= 2:
                            t.wait_ge(s_dve.h, V[("yev", g - 2)])
                        sl = (g * NPAIR + p) % NSL
                        bank = pb[4 + g % 2]
                        for j in range(2):
                            inst = t.matmul(bank[:, 0:T], idnt[j],
                                            dBq[sl][:, j * T:(j + 1) * T],
                                            start=(p == 0 and j == 0), stop=False)
                        inst.then_inc(s_pe.h, 1)
                    elif kind == "hsd":
                        g = key
                        t.wait_ge(s_act.h, V[("hsdc", g)])
                        inst = t.matmul(pb[4 + g % 2][:, 0:T], idnt[0],
                                        dlg[g % 2].bitcast(BF16)[:, 0:T],
                                        start=False, stop=True)
                        inst.then_inc(s_pe.h, 1)
                    elif kind == "w1":
                        m = key
                        sl, vv2 = wp_done[m]
                        t.wait_ge(s_spx[sl].h, vv2)
                        if c is None:
                            bmn, bmt = 2 + 2 * (m % 2), 3 + 2 * (m % 2)
                            if m >= 2:
                                t.wait_ge(s_act.h, V[("wev", m - 2)])
                        else:
                            bmn, bmt = 2, 3
                            if m >= 1:
                                t.wait_ge(s_act.h, V[("wev", m - 1)])
                        if not cheap:
                            for k in range(HK):
                                inst = t.matmul(
                                    pb[bmn][:, 0:512],
                                    W1S[sl][:, k * 128:(k + 1) * 128],
                                    xq_cur[:, k * TE:k * TE + 512],
                                    start=(k == 0), stop=(k == HK - 1))
                            for k in range(HK):
                                inst = t.matmul(
                                    pb[bmt][:, 0:2],
                                    W1S[sl][:, k * 128:(k + 1) * 128],
                                    xq_cur[:, k * TE + 512:k * TE + 514],
                                    start=(k == 0), stop=(k == HK - 1))
                        else:
                            for k in range(HK):
                                inst = t.matmul(
                                    pb[bmn][:, 0:2],
                                    W1S[sl][:, k * 128:(k + 1) * 128],
                                    xq_prv[:, k * TE + 512:k * TE + 514],
                                    start=(k == 0), stop=(k == HK - 1))
                        inst.then_inc(s_pe.h, 1)
                    elif kind == "wconv":
                        m = key
                        t.wait_ge(s_dve.h, V[("wdiag", m)])
                        t.wait_ge(s_act.h, V[("wev", m)])
                        bcv = 6 + (m % 2) if c is None else 6
                        if c is None:
                            if m >= 2:
                                t.wait_ge(s_act.h, V[("wsilu", m - 2)])
                        elif m >= 1:
                            t.wait_ge(s_act.h, V[("wsilu", m - 1)])
                        for tap in range(KF):
                            inst = t.matmul(pb[bcv][:, 0:WC],
                                            dgw[(m % 6) * 3 + tap],
                                            hp2[m % 4][:, tap:tap + WC],
                                            start=(tap == 0), stop=(tap == KF - 1))
                        inst.then_inc(s_pe.h, 1)
                    elif kind == "btr1":
                        t.wait_ge(s_dve.h, V[("wmx", MT - 1)])
                        for a in range(4):
                            inst = t.transpose(
                                pb[7].bitcast(BF16)[:, a * 128:(a + 1) * 128],
                                mx2[:, a * 128:(a + 1) * 128], idnt[0])
                        inst.then_inc(s_pe.h, 1)
                    elif kind == "btr2":
                        t.wait_ge(s_dve.h, V[("bred", 0)])
                        for a in range(4):
                            inst = t.transpose(pb[7][0:1, a * 128:(a + 1) * 128],
                                               mxq[:, 4 + a:5 + a], maskd)
                        inst.then_inc(s_pe.h, 1)
                    elif kind == "bbc":
                        t.wait_ge(s_dve.h, V[("bscale", 0)])
                        inst = t.matmul(pb[7][:, 0:T], ones1, r_s,
                                        start=True, stop=True)
                        inst.then_inc(s_pe.h, 1)
                    elif kind == "w2":
                        quad, k = key
                        sl, vv2 = wld_done[(quad, k)]
                        t.wait_ge(s_spx[sl].h, vv2)
                        if quad == 0:
                            t.wait_ge(s_dve.h, V[("wxq", (0, k))])
                        else:
                            pg, vg = ldxq_done[k]
                            t.wait_ge(s_gpx[pg].h, vg)
                            if k == 0:
                                t.wait_ge(s_act.h, V[("wofev", 0)])
                        for mi in range(4):
                            bkz = (2, 3, 6, 7)[mi]
                            inst = t.matmul(
                                pb[bkz][:, 0:WC],
                                wpC4[sl][:, mi * 128:(mi + 1) * 128],
                                H2S[k % 4][:, 0:WC],
                                start=(k == 0), stop=(k == MT - 1))
                        inst.then_inc(s_pe.h, 1)

            def act_emit(s):
                for kind, key, n in full["act"]:
                    if kind == "sp":
                        g = key
                        s.wait_ge(s_pe.h, V[("wdt", g)])
                        if g >= 2:
                            s.wait_ge(s_dve.h, V[("dlt", g - 2)])
                        bank = pb[g % 2][:, 0:T]
                        bcol = bdt_sb[:, g:g + 1]
                        gs = g % 2
                        s.activation(out=sp_a, in_=bank, func=AF.Abs,
                                     bias=bcol)
                        s.activation(out=sp_r[gs], in_=bank, func=AF.Relu,
                                     bias=bcol)
                        s.activation(out=sp_u[gs], in_=sp_a, func=AF.Exp,
                                     scale=-1.0)
                        inst = s.activation(out=sp_u[gs], in_=sp_u[gs],
                                            func=AF.Ln, bias=1.0)
                        inst.then_inc(s_act.h, 1)
                    elif kind == "dA":
                        g, p = key
                        s.wait_ge(s_dve.h, V[("dlt", g)])
                        i = g * NPAIR + p
                        sl = i % NSL
                        if i >= NSL:
                            prev = divmod(i - NSL, NPAIR)
                            s.wait_ge(s_dve.h, V[("scan", prev)])
                        for j in range(2):
                            nn2 = p * 2 + j
                            inst = s.activation(
                                out=dAq[sl][:, j * T:(j + 1) * T],
                                in_=dlg[g % 2], func=AF.Exp,
                                scale=A_sb[:, g * NST + nn2:g * NST + nn2 + 1])
                        inst.then_inc(s_act.h, 1)
                    elif kind == "hsdc":
                        g = key
                        s.wait_ge(s_dve.h, V[("dx", g)])
                        if g >= 1:
                            s.wait_ge(s_pe.h, V[("hsd", g - 1)])
                        inst = s.activation(out=dlg[g % 2].bitcast(BF16)[:, 0:T],
                                            in_=hsb[:, g * T:(g + 1) * T],
                                            func=AF.Copy, scale=D_sb[:, g:g + 1])
                        inst.then_inc(s_act.h, 1)
                    elif kind == "wev":
                        m = key
                        s.wait_ge(s_pe.h, V[("w1", m)])
                        if m >= 4:
                            s.wait_ge(s_pe.h, V[("wconv", m - 4)])
                        bmn = 2 + 2 * (m % 2) if c is None else 2
                        bmt = bmn + 1
                        if not cheap:
                            s.activation(out=hp2[m % 4][:, 0:512],
                                         in_=pb[bmn][:, 0:512], func=AF.Copy)
                            inst = s.activation(out=hp2[m % 4][:, 512:514],
                                                in_=pb[bmt][:, 0:2], func=AF.Copy)
                        else:
                            inst = s.activation(out=hp2[m % 4][:, 0:2],
                                                in_=pb[bmn][:, 0:2], func=AF.Copy)
                        inst.then_inc(s_act.h, 1)
                    elif kind == "wsilu":
                        m = key
                        s.wait_ge(s_pe.h, V[("wconv", m)])
                        if m >= 16:
                            sl2, vv3 = spill_done[m - 16]
                            s.wait_ge(s_gp[sl2].h, vv3)
                            s.wait_ge(s_dve.h, V[("wmx", m - 16)])
                        bcv = 6 + (m % 2) if c is None else 6
                        inst = s.activation(out=h2A[m % 16][:, 0:WC],
                                            in_=pb[bcv][:, 0:WC], func=AF.Silu,
                                            bias=cb2_sb[:, m:m + 1])
                        inst.then_inc(s_act.h, 1)
                    elif kind == "bcp":
                        s.wait_ge(s_pe.h, V[("btr2", 0)])
                        for a in range(4):
                            inst = s.activation(
                                out=r_mx[:, a * 128:(a + 1) * 128],
                                in_=pb[7][0:1, a * 128:(a + 1) * 128],
                                func=AF.Copy)
                        inst.then_inc(s_act.h, 1)
                    elif kind == "bsqrt":
                        s.activation(out=r_t1, in_=r_mx, func=AF.Ln)
                        inst = s.activation(out=r_mx, in_=r_t1, func=AF.Exp,
                                            scale=0.5)
                        inst.then_inc(s_act.h, 1)
                    elif kind == "wofev":
                        quad = key
                        s.wait_ge(s_pe.h, V[("w2", (quad, MT - 1))])
                        for mi in range(4):
                            m = quad * 4 + mi
                            bkz = (2, 3, 6, 7)[mi]
                            inst = s.activation(
                                out=ofb[:, m * T:m * T + WC],
                                in_=pb[bkz][:, 0:WC], func=AF.Copy)
                        inst.then_inc(s_act.h, 1)

            def dve_emit(v):
                for kind, key, n in full["dve"]:
                    if kind == "dlt":
                        g = key
                        v.wait_ge(s_act.h, V[("sp", g)])
                        if g >= 2:
                            v.wait_ge(s_pe.h, V[("hsd", g - 2)])
                        inst = v.tensor_tensor(out=dlg[g % 2], in0=sp_r[g % 2],
                                               in1=sp_u[g % 2], op=ALU.add)
                        inst.then_inc(s_dve.h, 1)
                    elif kind == "dx":
                        g = key
                        inst = v.tensor_tensor(out=dxgs[g % 2], in0=dlg[g % 2],
                                               in1=hsb[:, g * T:(g + 1) * T],
                                               op=ALU.mult)
                        inst.then_inc(s_dve.h, 1)
                    elif kind == "cz":
                        g, p = key
                        inst = v.memset(
                            carry[:, g * NST + p * 2:g * NST + p * 2 + 2], 0.0)
                        inst.then_inc(s_dve.h, 1)
                        v.wait_ge(s_dve.h, V[("cz", (g, p))])
                    elif kind == "dbq":
                        g, p = key
                        i = g * NPAIR + p
                        sl = i % NSL
                        if i >= NSL:
                            v.wait_ge(s_pe.h, V[("ymm", divmod(i - NSL, NPAIR))])
                        inst = v.tensor_tensor(
                            out=dBq[sl].rearrange("q (n t) -> q n t", n=2),
                            in0=dxgs[g % 2].unsqueeze(1).broadcast_to(
                                [128, 2, T]),
                            in1=Brep.rearrange("q (n t) -> q n t", n=NST)
                            [:, p * 2:p * 2 + 2, :], op=ALU.mult)
                        inst.then_inc(s_dve.h, 1)
                    elif kind == "scan":
                        g, p = key
                        sl = (g * NPAIR + p) % NSL
                        v.wait_ge(s_act.h, V[("dA", (g, p))])
                        for j in range(2):
                            nn2 = p * 2 + j
                            inst = v.tensor_tensor_scan(
                                out=zq[:, j * T:(j + 1) * T],
                                data0=dAq[sl][:, j * T:(j + 1) * T],
                                data1=dBq[sl][:, j * T:(j + 1) * T],
                                initial=carry[:, g * NST + nn2:g * NST + nn2 + 1],
                                op0=ALU.mult, op1=ALU.add)
                        inst.then_inc(s_dve.h, 1)
                        v.wait_ge(s_dve.h, V[("scan", (g, p))])
                    elif kind == "pq":
                        g, p = key
                        sl = (g * NPAIR + p) % NSL
                        for j in range(2):
                            nn2 = p * 2 + j
                            v.tensor_copy(
                                out=carry[:, g * NST + nn2:g * NST + nn2 + 1],
                                in_=zq[:, j * T + T - 1:j * T + T])
                        inst = v.tensor_tensor(
                            out=dBq[sl], in0=zq,
                            in1=Crep[:, p * 2 * T:(p * 2 + 2) * T], op=ALU.mult)
                        inst.then_inc(s_dve.h, 1)
                    elif kind == "yev":
                        g = key
                        v.wait_ge(s_pe.h, V[("hsd", g)])
                        inst = v.tensor_tensor(out=yb[:, g * T:(g + 1) * T],
                                               in0=pb[4 + g % 2][:, 0:T],
                                               in1=resb[:, g * T:(g + 1) * T],
                                               op=ALU.mult)
                        inst.then_inc(s_dve.h, 1)
                    elif kind == "wdiag":
                        m = key
                        if m >= 6:
                            v.wait_ge(s_pe.h, V[("wconv", m - 6)])
                        if m == 0:
                            v.memset(mx2, 1.0 if cheap else 0.0)
                        if cheap:
                            v.wait_ge(s_act.h, V[("wev", m)])
                            v.memset(hp2[m % 4][:, 2:4], 0.0)
                        for tap in range(KF):
                            inst = v.tensor_scalar(
                                out=dgw[(m % 6) * 3 + tap], in0=maskd,
                                scalar1=cw2_sb[:, m * KF + tap:m * KF + tap + 1],
                                scalar2=None, op0=ALU.mult)
                        inst.then_inc(s_dve.h, 1)
                    elif kind == "wmx":
                        m = key
                        v.wait_ge(s_act.h, V[("wsilu", m)])
                        v.tensor_tensor(out=mxsq[:, 0:WC],
                                        in0=h2A[m % 16][:, 0:WC],
                                        in1=h2A[m % 16][:, 0:WC], op=ALU.mult)
                        inst = v.tensor_tensor(out=mx2[:, 0:WC],
                                               in0=mx2[:, 0:WC],
                                               in1=mxsq[:, 0:WC], op=ALU.max)
                        inst.then_inc(s_dve.h, 1)
                    elif kind == "bred":
                        v.wait_ge(s_pe.h, V[("btr1", 0)])
                        inst = v.tensor_reduce(
                            out=mxq[:, 4:8],
                            in_=pb[7].bitcast(BF16)[:, 0:512].rearrange(
                                "p (a q) -> p a q", a=4),
                            axis=mybir.AxisListType.X, op=ALU.max)
                        inst.then_inc(s_dve.h, 1)
                    elif kind == "bscale":
                        v.wait_ge(s_act.h, V[("bsqrt", 0)])
                        v.tensor_scalar(out=r_t1, in0=r_mx, scalar1=1e-5,
                                        scalar2=None, op0=ALU.max)
                        v.reciprocal(out=r_s, in_=r_t1)
                        inst = v.tensor_scalar(out=r_s, in0=r_s, scalar1=127.0,
                                               scalar2=None, op0=ALU.mult)
                        inst.then_inc(s_dve.h, 1)
                    elif kind == "bcopy":
                        v.wait_ge(s_pe.h, V[("bbc", 0)])
                        inst = v.tensor_copy(out=scf[:, 0:T], in_=pb[7][:, 0:T])
                        inst.then_inc(s_dve.h, 1)
                    elif kind == "wxq":
                        _, k = key
                        v.wait_ge(s_gps.h, V[("wmul", (0, k))])
                        inst = v.tensor_scalar(
                            out=H2S[k % 4][:, 0:WC],
                            in0=scf[:, 512 + (k % 2) * 512:
                                    512 + (k % 2) * 512 + WC],
                            scalar1=MAGIC, scalar2=MAGIC,
                            op0=ALU.add, op1=ALU.subtract)
                        inst.then_inc(s_dve.h, 1)

            PE(pe_emit)
            SC(act_emit)
            VE(dve_emit)
            for e, sem in (("pe", s_pe), ("act", s_act), ("dve", s_dve),
                           ("gps", s_gps)):
                tot = base[e] + sum(n for _, _, n in full[e])
                sem.v = tot

            if w is not None:
                def f_sy(sy):
                    for m in range(MT):
                        if m >= 4:
                            sy.wait_ge(s_pe.h, V[("w1", m - 4)])
                        s_spx[m % 4].inc(sy.dma_start(
                            out=W1S[m % 4][:, 0:HK * 128].rearrange(
                                "p (k c2) -> p k c2", c2=128),
                            in_=wq1_h[m, :, :].rearrange("(k p) c2 -> p k c2",
                                                         p=128)))
                    for quad in range(2):
                        for k in range(MT):
                            idx = quad * MT + k
                            sl = idx % 4
                            if idx >= 4:
                                pidx = idx - 4
                                sy.wait_ge(s_pe.h,
                                           V[("w2", (pidx // MT, pidx % MT))])
                            else:
                                sy.wait_ge(s_pe.h, V[("w1", MT - 1)])
                            s_spx[sl].inc(sy.dma_start(
                                out=wpC4[sl],
                                in_=wq2_h[k, :, quad * 512:(quad + 1) * 512]))
                SY(f_sy)

                def f_gp(g2):
                    for m in range(MT):
                        g2.wait_ge(s_dve.h, V[("wmx", m)])
                        if m >= 2:
                            g2.wait_ge(s_gp[m % 2].h, spill_done[m - 2][1])
                        s_gp[m % 2].inc(g2.dma_start(
                            out=h2sp_h[m, :, 0:WC],
                            in_=h2A[m % 16][:, 0:WC],
                            allow_slow_non_contiguous=(WC < 8)))
                    for k in range(MT):
                        p = k % 4
                        g2.wait_ge(s_gp[k % 2].h, spill_done[k][1])
                        if k >= 4:
                            g2.wait_ge(s_pe.h, V[("w2", (0, k - 4))])
                            g2.wait_ge(s_gpx[p].h, splq_done[k - 4][1])
                        s_gpx[p].inc(g2.dma_start(
                            out=H2S[p][:, 0:WC], in_=h2sp_h[k, :, 0:WC],
                            allow_slow_non_contiguous=(WC < 8)))
                        if k >= 1:
                            pp = (k - 1) % 4
                            g2.wait_ge(s_dve.h, V[("wxq", (0, k - 1))])
                            s_gpx[pp].inc(g2.dma_start(
                                out=h2sp_h[k - 1, :, 0:WC],
                                in_=H2S[pp][:, 0:WC],
                                allow_slow_non_contiguous=(WC < 8)))
                        g2.wait_ge(s_gpx[p].h, ldh2_done[k][1])
                        g2.wait_ge(s_dve.h, V[("bcopy", 0)])
                        if k >= 2:
                            g2.wait_ge(s_dve.h, V[("wxq", (0, k - 2))])
                        inst = g2.tensor_tensor(
                            out=scf[:, 512 + (k % 2) * 512:
                                    512 + (k % 2) * 512 + WC],
                            in0=H2S[p][:, 0:WC], in1=scf[:, 0:WC],
                            op=ALU.mult)
                        inst.then_inc(s_gps.h, 1)
                    g2.wait_ge(s_dve.h, V[("wxq", (0, MT - 1))])
                    s_gpx[(MT - 1) % 4].inc(g2.dma_start(
                        out=h2sp_h[MT - 1, :, 0:WC],
                        in_=H2S[(MT - 1) % 4][:, 0:WC],
                        allow_slow_non_contiguous=(WC < 8)))
                    for k in range(MT):
                        p = k % 4
                        if k >= 4:
                            g2.wait_ge(s_pe.h, V[("w2", (1, k - 4))])
                        else:
                            g2.wait_ge(s_gpx[p].h, splq_done[28 + p][1])
                            g2.wait_ge(s_pe.h, V[("w2", (0, 28 + p))])
                        s_gpx[p].inc(g2.dma_start(
                            out=H2S[p][:, 0:WC], in_=h2sp_h[k, :, 0:WC],
                            allow_slow_non_contiguous=(WC < 8)))
                GP(f_gp)

        def window_tail(w, n_chunks, bar=True):
            cheap = (w == n_chunks)
            if debug and w == 0:
                dbg_tap("d_of0", ofb[:, 0:HK * T], 128, HK * T, BF16)
                hh = dbg_out("d_h2w0", [MT, 128, T], BF16)
                barrier()

                def f2t(sy):
                    s_spd.inc(sy.dma_start(out=hh, in_=h2sp_h))
                SY(f2t)
                barrier()
            ln_quant_stage(ofb[:, 0:HK * T], True, 'out')
            dve_ln = s_dve.v
            gpd0x = s_gpd.v
            dve0x = s_dve.v
            gp_base = [s_gp[0].v, s_gp[1].v]
            if w == 0:
                src0, dst0, width = 0, 1, T - 1
            elif cheap:
                src0, dst0, width = w * T - 1, 0, 1
            else:
                src0, dst0, width = w * T - 1, 0, T

            def f_gp3(g2):
                g2.wait_ge(s_dve.h, dve_ln)
                for k in range(HK):
                    if k >= 1:
                        g2.wait_ge(s_dve.h, dve0x + k)
                    s_gp[k % 2].inc(g2.dma_start(
                        out=(sp_a if k % 2 == 0 else sp_r[0])[:, dst0:dst0 + width],
                        in_=xsp_h[k * 128:(k + 1) * 128, src0:src0 + width],
                        allow_slow_non_contiguous=(width < 8)))
            GP(f_gp3)

            def f_ve10(v):
                for k in range(HK):
                    v.wait_ge(s_gp[k % 2].h, gp_base[k % 2] + 16 * (k // 2 + 1))
                    s_dve.inc(v.tensor_tensor(
                        out=xpr[:, k * T:(k + 1) * T],
                        in0=scf[:, k * T:(k + 1) * T],
                        in1=sp_a if k % 2 == 0 else sp_r[0], op=ALU.add))
            VE(f_ve10)

            def f_gp4(g2):
                for k in range(HK):
                    g2.wait_ge(s_dve.h, dve0x + k + 1)
                    s_gpd.inc(g2.dma_start(
                        out=out_h[k * 128:(k + 1) * 128, src0:src0 + width],
                        in_=xpr[:, k * T + dst0:k * T + dst0 + width],
                        allow_slow_non_contiguous=(width < 8)))
            GP(f_gp4)
            if bar:
                barrier()

        def pbc(bank, k):
            return pb[bank][:, 0:T].unsqueeze(1).broadcast_to([128, k, T])

        def ln_quant_stage(src_ap, src_is_bf, mode, c=None):
            pe0, act0, dve0 = s_pe.v, s_act.v, s_dve.v
            n_k = HK
            D = n_k * 128
            has_cast = not src_is_bf
            dve_cast = dve0 + 1 if has_cast else dve0
            dve_stat = dve_cast + 1
            act_sq = lambda k: act0 + k + 1
            act_rstd = act0 + n_k + 1
            pe_sum = lambda k: pe0 + k + 1
            pe_done_sums = pe0 + n_k

            def f_sc(s):
                for k in range(n_k):
                    if k >= 2:
                        s.wait_ge(s_pe.h, pe_sum(k - 2))
                    if has_cast and k == 0:
                        s.wait_ge(s_dve.h, dve_cast)
                    srck = (src_ap[:, k * T:(k + 1) * T] if src_is_bf
                            else nb1[:, k * T:(k + 1) * T])
                    s_act.inc(s.activation(out=h2t[k % 2][:, 0:T], in_=srck,
                                           func=AF.Square))
                s.wait_ge(s_dve.h, dve_stat)
                s.activation(out=r_t2, in_=r_t1, func=AF.Ln)
                s_act.inc(s.activation(out=r_r, in_=r_t2, func=AF.Exp,
                                       scale=-0.5))
            SC(f_sc)
            assert s_act.v == act_rstd

            if has_cast:
                def f_vc(v):
                    s_dve.inc(v.tensor_copy(out=nb1[:, 0:n_k * T], in_=src_ap))
                VE(f_vc)

            def f_pe(t):
                for k in range(n_k):
                    t.wait_ge(s_act.h, act_sq(k))
                    o = onesh_a if k % 2 == 0 else onesh_b
                    rhs = (src_ap[:, k * T:(k + 1) * T] if src_is_bf
                           else nb1[:, k * T:(k + 1) * T])
                    t.matmul(pb[6][0:1, 0:T], o, rhs,
                             start=(k == 0), stop=(k == n_k - 1))
                    o2 = onesh_b if k % 2 == 0 else onesh_a
                    inst = t.matmul(pb[7][0:1, 0:T], o2, h2t[k % 2][:, 0:T],
                                    start=(k == 0), stop=(k == n_k - 1))
                    s_pe.inc(inst)
                t.wait_ge(s_dve.h, dve_stat)
                s_pe.inc(t.matmul(pb[4][:, 0:T], ones1, r_m,
                                  start=True, stop=True))
                t.wait_ge(s_act.h, act_rstd)
                s_pe.inc(t.matmul(pb[5][:, 0:T], ones1, r_r,
                                  start=True, stop=True))
            PE(f_pe)
            pe_bc_m = pe_done_sums + 1
            pe_bc_r = pe_done_sums + 2

            def f_ve(v):
                if has_cast:
                    pass
                v.wait_ge(s_pe.h, pe_done_sums)
                v.tensor_scalar(out=r_m, in0=pb[6][0:1, 0:T], scalar1=1.0 / D,
                                scalar2=None, op0=ALU.mult)
                v.tensor_scalar(out=r_t1, in0=pb[7][0:1, 0:T], scalar1=1.0 / D,
                                scalar2=None, op0=ALU.mult)
                v.tensor_tensor(out=r_t2, in0=r_m, in1=r_m, op=ALU.mult)
                v.tensor_tensor(out=r_t1, in0=r_t1, in1=r_t2, op=ALU.subtract)
                s_dve.inc(v.tensor_scalar(out=r_t1, in0=r_t1, scalar1=0.0,
                                          scalar2=EPS, op0=ALU.max, op1=ALU.add))
                assert s_dve.v == dve_stat
                if mode == 'xn1':
                    v.wait_ge(s_pe.h, pe_bc_r)
                    v.tensor_tensor(out=t3(scf[:, 0:HK * T], HK),
                                    in0=t3(src_ap, HK), in1=pbc(4, HK),
                                    op=ALU.subtract)
                    s_dve.inc(v.tensor_tensor(out=t3(nb1[:, 0:HK * T], HK),
                                              in0=t3(scf[:, 0:HK * T], HK),
                                              in1=pbc(5, HK), op=ALU.mult))
                elif mode == 'resid':
                    v.wait_ge(s_pe.h, pe_bc_r)
                    v.tensor_tensor(out=t3(scf[:, 0:HK * T], HK),
                                    in0=t3(src_ap, HK), in1=pbc(4, HK),
                                    op=ALU.subtract)
                    v.tensor_tensor(out=t3(scf[:, 0:HK * T], HK),
                                    in0=t3(scf[:, 0:HK * T], HK),
                                    in1=pbc(5, HK), op=ALU.mult)
                    s_dve.inc(v.tensor_tensor(out=xpr[:, 0:HK * T],
                                              in0=scf[:, 0:HK * T],
                                              in1=xf[:, 0:HK * T], op=ALU.add))
                elif mode == 'out':
                    v.wait_ge(s_pe.h, pe_bc_r)
                    v.tensor_tensor(out=t3(scf[:, 0:HK * T], HK),
                                    in0=t3(src_ap, HK), in1=pbc(4, HK),
                                    op=ALU.subtract)
                    s_dve.inc(v.tensor_tensor(out=t3(scf[:, 0:HK * T], HK),
                                              in0=t3(scf[:, 0:HK * T], HK),
                                              in1=pbc(5, HK), op=ALU.mult))
            VE(f_ve)

            if mode == 'quant':
                quant1_tail(src_ap, c)

        def quant1_tail(src_ap, c):
            pe0, act0, dve0 = s_pe.v, s_act.v, s_dve.v
            pe_ready = s_pe.v

            def f_ve(v):
                v.wait_ge(s_pe.h, pe_ready)
                for k in range(HK):
                    v.tensor_tensor(out=h2t[0][:, 0:T],
                                    in0=src_ap[:, k * T:(k + 1) * T],
                                    in1=pb[4][:, 0:T], op=ALU.subtract)
                    v.tensor_tensor(out=h2t[1][:, 0:T], in0=h2t[0][:, 0:T],
                                    in1=h2t[0][:, 0:T], op=ALU.mult)
                    if k == 0:
                        v.tensor_copy(out=mx2, in_=h2t[1][:, 0:T])
                    else:
                        v.tensor_tensor(out=mx2, in0=mx2, in1=h2t[1][:, 0:T],
                                        op=ALU.max)
                s_dve.inc(v.engine_nop())
            VE(f_ve)
            dve_tree = dve0 + 1
            colmax_to_row(0, dve_tree)
            act_mx = s_act.v
            dve_stat2 = s_dve.v + 1

            def f_sc2(s):
                s.activation(out=r_t1, in_=r_mx, func=AF.Ln)
                s_act.inc(s.activation(out=r_mx, in_=r_t1, func=AF.Exp,
                                       scale=0.5))
            SC(f_sc2)
            act_sqrt = s_act.v

            def f_ve3(v):
                v.wait_ge(s_act.h, act_sqrt)
                v.tensor_tensor(out=r_t1, in0=r_r, in1=r_mx, op=ALU.mult)
                v.tensor_scalar(out=r_t1, in0=r_t1, scalar1=1e-5, scalar2=None,
                                op0=ALU.max)
                v.reciprocal(out=r_s, in_=r_t1)
                v.tensor_scalar(out=r_s, in0=r_s, scalar1=127.0, scalar2=None,
                                op0=ALU.mult)
                v.tensor_tensor(out=r_t2, in0=r_r, in1=r_s, op=ALU.mult)
                v.tensor_tensor(out=r_mx, in0=r_m, in1=r_t2, op=ALU.mult)
                v.tensor_scalar(out=r_mx, in0=r_mx, scalar1=-1.0, scalar2=None,
                                op0=ALU.mult)
                s_dve.inc(v.reciprocal(out=r_is, in_=r_s))
            VE(f_ve3)
            dve_sc = s_dve.v

            def f_pe4(t):
                t.wait_ge(s_dve.h, dve_sc)
                t.matmul(pb[4][:, 0:T], ones1, r_t2, start=True, stop=True)
                t.matmul(pb[5][:, 0:T], ones1, r_mx, start=True, stop=True)
                s_pe.inc(t.matmul(pb[6][:, 0:T], ones1, r_is,
                                  start=True, stop=True))
            PE(f_pe4)
            pe_bc = s_pe.v

            def f_ve4(v):
                v.wait_ge(s_pe.h, pe_bc)
                v.tensor_tensor(out=t3(scf[:, 0:HK * T], HK),
                                in0=t3(src_ap, HK), in1=pbc(4, HK), op=ALU.mult)
                v.tensor_tensor(out=t3(scf[:, 0:HK * T], HK),
                                in0=t3(scf[:, 0:HK * T], HK), in1=pbc(5, HK),
                                op=ALU.add)
                v.tensor_scalar(out=scf[:, 0:HK * T], in0=scf[:, 0:HK * T],
                                scalar1=MAGIC, scalar2=MAGIC,
                                op0=ALU.add, op1=ALU.subtract)
                xq_view = xq1[c % 2].rearrange("p (k t) -> p k t", k=HK)[:, :, 2:]
                v.tensor_tensor(out=xq_view, in0=t3(scf[:, 0:HK * T], HK),
                                in1=pbc(6, HK), op=ALU.mult)
                if c > 0:
                    prev = xq1[(c - 1) % 2]
                    for k in range(HK):
                        inst = v.tensor_copy(
                            out=xq1[c % 2][:, k * (T + 2):k * (T + 2) + 2],
                            in_=prev[:, k * (T + 2) + 512:k * (T + 2) + 514])
                s_dve.inc(v.engine_nop())
            VE(f_ve4)

        def colmax_to_row(mxq_off, dve_ready):
            pe0 = s_pe.v

            def f_pe(t):
                t.wait_ge(s_dve.h, dve_ready)
                for a in range(4):
                    inst = t.transpose(
                        pb[7].bitcast(BF16)[:, a * 128:(a + 1) * 128],
                        mx2[:, a * 128:(a + 1) * 128], idnt[0])
                s_pe.inc(inst)
            PE(f_pe)

            def f_ve(v):
                v.wait_ge(s_pe.h, pe0 + 1)
                s_dve.inc(v.tensor_reduce(
                    out=mxq[:, mxq_off:mxq_off + 4],
                    in_=pb[7].bitcast(BF16)[:, 0:512].rearrange(
                        "p (a q) -> p a q", a=4),
                    axis=mybir.AxisListType.X, op=ALU.max))
            VE(f_ve)
            dve_red = s_dve.v

            def f_pe2(t):
                t.wait_ge(s_dve.h, dve_red)
                for a in range(4):
                    inst = t.transpose(pb[6][0:1, a * 128:(a + 1) * 128],
                                       mxq[:, mxq_off + a:mxq_off + a + 1],
                                       maskd)
                s_pe.inc(inst)
            PE(f_pe2)
            pe_tr2 = s_pe.v

            def f_sc(s):
                s.wait_ge(s_pe.h, pe_tr2)
                for a in range(4):
                    inst = s.activation(out=r_mx[:, a * 128:(a + 1) * 128],
                                        in_=pb[6][0:1, a * 128:(a + 1) * 128],
                                        func=AF.Copy)
                s_act.inc(inst)
            SC(f_sc)

        for c in range(n_chunks):
            def f_gpx(g2, c=c):
                for k in range(HK):
                    s_gpd.inc(g2.dma_start(out=xf[:, k * T:(k + 1) * T],
                                           in_=x_h[k * 128:(k + 1) * 128,
                                                   c * T:(c + 1) * T]))
            GP(f_gpx)
            barrier()

            ln_quant_stage(xf[:, 0:HK * T], True, 'xn1')
            barrier()
            if c == 0:
                dbg_tap("d_xn1", nb1[:, 0:HK * T], 128, HK * T, BF16)

            pe0, act0, dve0, spd0 = s_pe.v, s_act.v, s_dve.v, s_spd.v
            mm_done = {}
            conv_done = {}
            pe_order = []
            for m in range(WMT):
                pe_order.append(("mm", m))
                if m >= 2 and m - 2 < GT:
                    pe_order.append(("conv", m - 2))
            vv = pe0
            for kind, m in pe_order:
                vv += 1
                (mm_done if kind == "mm" else conv_done)[m] = vv
            evac_done = {}
            silu_done = {}
            act_order = []
            for m in range(WMT):
                act_order.append(("evac", m))
                if m >= 1 and m - 1 < GT:
                    act_order.append(("silu", m - 1))
            vv = act0
            for kind, m in act_order:
                vv += 1
                (evac_done if kind == "evac" else silu_done)[m] = vv
            diag_done = {m: dve0 + m + 1 for m in range(GT)}

            sp_base = [s_sp[0].v, s_sp[1].v]

            def f_sy(sy, c=c):
                for m in range(WMT):
                    if m >= 2:
                        sy.wait_ge(s_pe.h, mm_done[m - 2])
                    s_sp[m % 2].inc(sy.dma_start(
                        out=wpA[m % 2][:, 0:HK * 128].rearrange(
                            "p (k c2) -> p k c2", c2=128),
                        in_=wbf_h[m, :, :].rearrange("(k p) c2 -> p k c2",
                                                     p=128)))
            SY(f_sy)

            def f_pe(t, c=c):
                for kind, m in pe_order:
                    if kind == "mm":
                        t.wait_ge(s_sp[m % 2].h, sp_base[m % 2] + 16 * (m // 2 + 1))
                        if m >= 2:
                            t.wait_ge(s_act.h, evac_done[m - 2])
                        bank = pb[m % 2]
                        for k in range(HK):
                            inst = t.matmul(bank[:, 0:T],
                                            wpA[m % 2][:, k * 128:(k + 1) * 128],
                                            nb1[:, k * T:(k + 1) * T],
                                            start=(k == 0), stop=(k == HK - 1))
                        s_pe.inc(inst)
                    else:
                        t.wait_ge(s_dve.h, diag_done[m])
                        if m >= 2:
                            t.wait_ge(s_act.h, silu_done[m - 2])
                        bank = pb[2 + m % 2]
                        for tap in range(KM):
                            inst = t.matmul(bank[:, 0:T],
                                            diag[(m % 2) * KM + tap],
                                            hp[m % 2][:, tap:tap + T],
                                            start=(tap == 0),
                                            stop=(tap == KM - 1))
                        s_pe.inc(inst)
            PE(f_pe)

            def f_sc(s, c=c):
                for kind, m in act_order:
                    if kind == "evac":
                        s.wait_ge(s_pe.h, mm_done[m])
                        if m < GT:
                            if m >= 2:
                                s.wait_ge(s_pe.h, conv_done[m - 2])
                                s.wait_ge(s_dve.h, diag_done[m - 2])
                            s_act.inc(s.activation(
                                out=hp[m % 2][:, KM - 1:KM - 1 + T],
                                in_=pb[m % 2][:, 0:T], func=AF.Copy))
                        else:
                            g = m - GT
                            s_act.inc(s.activation(
                                out=resb[:, g * T:(g + 1) * T],
                                in_=pb[m % 2][:, 0:T], func=AF.Silu))
                    else:
                        s.wait_ge(s_pe.h, conv_done[m])
                        s_act.inc(s.activation(out=hsb[:, m * T:(m + 1) * T],
                                               in_=pb[2 + m % 2][:, 0:T],
                                               func=AF.Silu))
            SC(f_sc)

            def f_ve(v, c=c):
                for m in range(GT):
                    v.wait_ge(s_act.h, evac_done[m])
                    if m >= 2:
                        v.wait_ge(s_pe.h, conv_done[m - 2])
                    if c == 0:
                        v.memset(hp[m % 2][:, 0:KM - 1], 0.0)
                    else:
                        v.tensor_copy(out=hp[m % 2][:, 0:KM - 1],
                                      in_=mt3[:, m * (KM - 1):(m + 1) * (KM - 1)])
                    v.tensor_copy(out=mt3[:, m * (KM - 1):(m + 1) * (KM - 1)],
                                  in_=hp[m % 2][:, T:T + KM - 1])
                    for tap in range(KM):
                        inst = v.tensor_scalar(
                            out=diag[(m % 2) * KM + tap], in0=maskd,
                            scalar1=cw1_sb[:, m * KM + tap:m * KM + tap + 1],
                            scalar2=None, op0=ALU.mult)
                    s_dve.inc(inst)
            VE(f_ve)
            barrier()
            if c == 0:
                dbg_tap("d_hs", hsb[:, 0:GT * T], 128, GT * T, BF16)
                dbg_tap("d_res", resb[:, 0:GT * T], 128, GT * T, BF16)

            pe0 = s_pe.v

            def f_pe(t):
                for k in range(GT):
                    inst = t.matmul(pb[4][0:96, 0:T], wxq[:, k * 96:(k + 1) * 96],
                                    hsb[:, k * T:(k + 1) * T],
                                    start=(k == 0), stop=(k == GT - 1))
                s_pe.inc(inst)
            PE(f_pe)

            def f_ve(v):
                v.wait_ge(s_pe.h, pe0 + 1)
                s_dve.inc(v.tensor_copy(out=xdbq[0:96, 0:T],
                                        in_=pb[4][0:96, 0:T]))
            VE(f_ve)
            dve_xdb = s_dve.v

            def f_gp(g2):
                g2.wait_ge(s_dve.h, dve_xdb)
                s_gpd.inc(g2.dma_start(out=bcr_h, in_=xdbq[64:96, 0:T]))
                g2.wait_ge(s_gpd.h, s_gpd.v)
                for n in range(NST):
                    s_gpd.inc(g2.dma_start(
                        out=Brep[:, n * T:(n + 1) * T],
                        in_=bcr_h[n:n + 1, :].broadcast_to([128, T])))
                    s_gpd.inc(g2.dma_start(
                        out=Crep[:, n * T:(n + 1) * T],
                        in_=bcr_h[NST + n:NST + n + 1, :].broadcast_to([128, T])))
            GP(f_gp)
            barrier()

            ddbg = None
            if debug and c == 0:
                ddbg = dbg_out("d_delta", [128, GT * T], FP16)
                dbg_tap("d_Brep", Brep[:, 0:NST * T], 128, NST * T, BF16)


            merged_stage(c, (c - 1) if c > 0 else None, n_chunks)
            barrier()
            if c > 0:
                window_tail(c - 1, n_chunks, bar=False)
            if c == 0 and debug:
                dbg_tap("d_y", yb[:, 0:GT * T], 128, GT * T, BF16)

            pe0, act0, spd0 = s_pe.v, s_act.v, s_spd.v
            mm_done = {m: pe0 + m + 1 for m in range(HK)}
            evac_done = {m: act0 + m + 1 for m in range(HK)}
            sp_base = [s_sp[0].v, s_sp[1].v]
            spd_done = {}
            for m in range(HK):
                if m % 2 == 0:
                    spd_done[m] = (0, sp_base[0] + 16 * (m // 2 + 1))
                else:
                    spd_done[m] = (1, sp_base[1] + 32 * (m // 2 + 1))

            def lhs_wout(m, k):
                if m % 2 == 0:
                    return wpB[0][:, k * 128:(k + 1) * 128]
                return wpA[k // HK][:, (k % HK) * 128:(k % HK + 1) * 128]

            def f_sy(sy):
                for m in range(HK):
                    if m >= 2:
                        sy.wait_ge(s_pe.h, mm_done[m - 2])
                    if m % 2 == 0:
                        s_sp[0].inc(sy.dma_start(
                            out=wpB[0][:, 0:GT * 128].rearrange(
                                "p (k c2) -> p k c2", c2=128),
                            in_=wob_h[m, :, :].rearrange("(k p) c2 -> p k c2",
                                                         p=128)))
                    else:
                        for half in range(2):
                            s_sp[1].inc(sy.dma_start(
                                out=wpA[half][:, 0:HK * 128].rearrange(
                                    "p (k c2) -> p k c2", c2=128),
                                in_=wob_h[m, half * HK * 128:(half + 1) * HK * 128,
                                          :].rearrange("(k p) c2 -> p k c2",
                                                       p=128)))
            SY(f_sy)

            def f_pe(t):
                for m in range(HK):
                    sl, vv2 = spd_done[m]
                    t.wait_ge(s_sp[sl].h, vv2)
                    if m >= 2:
                        t.wait_ge(s_act.h, evac_done[m - 2])
                    bank = pb[m % 2]
                    for k in range(GT):
                        inst = t.matmul(bank[:, 0:T], lhs_wout(m, k),
                                        yb[:, k * T:(k + 1) * T],
                                        start=(k == 0), stop=(k == GT - 1))
                    s_pe.inc(inst)
            PE(f_pe)

            def f_sc(s):
                for m in range(HK):
                    s.wait_ge(s_pe.h, mm_done[m])
                    s_act.inc(s.activation(out=Brep[:, m * T:(m + 1) * T],
                                           in_=pb[m % 2][:, 0:T], func=AF.Copy))
            SC(f_sc)
            barrier()

            ln_quant_stage(Brep[:, 0:HK * T], True, 'resid')
            barrier()
            if c == 0:
                dbg_tap("d_xp", xpr[:, 0:HK * T], 128, HK * T, F32)

            def f_gp(g2, c=c):
                for k in range(HK):
                    s_gpd.inc(g2.dma_start(
                        out=xsp_h[k * 128:(k + 1) * 128, c * T:(c + 1) * T],
                        in_=xpr[:, k * T:(k + 1) * T]))
            GP(f_gp)

            ln_quant_stage(xpr[:, 0:HK * T], False, 'quant', c=c)
            barrier()
            if c == 0:
                dbg_tap("d_xq1",
                        xq1[0].rearrange("p (k t) -> p k t", k=HK)[:, :, 2:],
                        128, HK * T, BF16)


        merged_stage(None, n_chunks - 1, n_chunks)
        barrier()
        window_tail(n_chunks - 1, n_chunks)
        merged_stage(None, n_chunks, n_chunks)
        barrier()
        window_tail(n_chunks, n_chunks)


        def fin(sy):
            sy.wait_ge(s_gpd.h, s_gpd.v)
            sy.wait_ge(s_spd.h, s_spd.v)
        SY(fin)

    return nc, dbg_handles



def prep_inputs(inputs, core):
    x = inputs["L0_attn_hidden_states"]
    d = {}
    d["x"] = np.ascontiguousarray(np.asarray(x)[core].T)
    d["winT"] = np.ascontiguousarray(np.asarray(inputs["L0_attn_w_in"]).T)
    d["wxT"] = np.ascontiguousarray(np.asarray(inputs["L0_attn_w_xproj"]).T)
    d["wdtT"] = np.ascontiguousarray(np.asarray(inputs["L0_attn_w_dt"]).T)
    d["woutT"] = np.ascontiguousarray(np.asarray(inputs["L0_attn_w_out"]).T)
    d["w1T"] = np.ascontiguousarray(np.asarray(inputs["L0_ffn_w1"]).T)
    d["w2T"] = np.ascontiguousarray(np.asarray(inputs["L0_ffn_w2"]).T)
    d["bdt"] = np.ascontiguousarray(
        np.asarray(inputs["L0_attn_b_dt"]).reshape(GT, 128).T)
    d["alog"] = np.ascontiguousarray(
        np.asarray(inputs["L0_attn_A_log"]).reshape(GT, 128, NST)
        .transpose(1, 0, 2).reshape(128, GT * NST))
    d["dvec"] = np.ascontiguousarray(
        np.asarray(inputs["L0_attn_D"]).reshape(GT, 128).T)
    d["cw1"] = np.ascontiguousarray(
        np.asarray(inputs["L0_attn_w_conv"]).reshape(GT, 128, KM)
        .transpose(1, 0, 2).reshape(128, GT * KM))
    d["cw2"] = np.ascontiguousarray(
        np.asarray(inputs["L0_ffn_conv_w"]).reshape(MT, 128, KF)
        .transpose(1, 0, 2).reshape(128, MT * KF))
    d["cb2"] = np.ascontiguousarray(
        np.asarray(inputs["L0_ffn_conv_b"]).reshape(MT, 128).T)
    return {k: np.ascontiguousarray(np.asarray(v, dtype=np.float32))
            for k, v in d.items()}


def kernel(**inputs):
    nc, _ = build(debug=False)
    in_maps = [prep_inputs(inputs, core) for core in range(8)]
    res = run_bass_kernel_spmd(nc, in_maps, core_ids=list(range(8)))
    outs = [np.asarray(r["out"], np.float32).T for r in res.results]
    return np.stack(outs, axis=0)


# version 14
# speedup vs baseline: 1.0465x; 1.0465x over previous
import sys
sys.path.insert(0, "/opt/trn_rl_repo")
from contextlib import ExitStack

import numpy as np
import ml_dtypes
import concourse.bass as bass
import concourse.mybir as mybir
from concourse.bass_utils import run_bass_kernel_spmd

F32 = mybir.dt.float32
BF16 = mybir.dt.bfloat16
FP16 = mybir.dt.float16
AF = mybir.ActivationFunctionType
ALU = mybir.AluOpType

H = 1024
DIN = 2048
DH = 4096
NST = 16
RNK = 64
L = 4096
T = 512
NCH = L // T
HK = H // 128
GT = DIN // 128
MT = DH // 128
WMT = (2 * DIN) // 128
KM = 4
KF = 3
MAGIC = float(np.float32(1.5 * 2 ** 23))
EPS = 1e-5


class Sem:
    def __init__(self, stack, nc, name, step=1):
        self.h = stack.enter_context(nc.semaphore(name))
        self.v = 0
        self.step = step

    def inc(self, inst, k=None):
        k = self.step if k is None else k
        inst.then_inc(self.h, k)
        self.v += k
        return self.v


def t3(ap, k):
    return ap.rearrange("p (k t) -> p k t", k=k)


def build(debug=False, n_chunks=NCH, stop_after=None):
    nc = bass.Bass("TRN2", target_bir_lowering=False, debug=False,
                   detect_race_conditions=False)

    x_h = nc.dram_tensor("x", [H, L], F32, kind="ExternalInput").ap()
    winT_h = nc.dram_tensor("winT", [H, 2 * DIN], F32, kind="ExternalInput").ap()
    wxT_h = nc.dram_tensor("wxT", [DIN, 96], F32, kind="ExternalInput").ap()
    wdtT_h = nc.dram_tensor("wdtT", [RNK, DIN], F32, kind="ExternalInput").ap()
    woutT_h = nc.dram_tensor("woutT", [DIN, H], F32, kind="ExternalInput").ap()
    w1T_h = nc.dram_tensor("w1T", [H, DH], F32, kind="ExternalInput").ap()
    w2T_h = nc.dram_tensor("w2T", [DH, H], F32, kind="ExternalInput").ap()
    bdt_h = nc.dram_tensor("bdt", [128, GT], F32, kind="ExternalInput").ap()
    alog_h = nc.dram_tensor("alog", [128, GT * NST], F32, kind="ExternalInput").ap()
    dvec_h = nc.dram_tensor("dvec", [128, GT], F32, kind="ExternalInput").ap()
    cw1_h = nc.dram_tensor("cw1", [128, GT * KM], F32, kind="ExternalInput").ap()
    cw2_h = nc.dram_tensor("cw2", [128, MT * KF], F32, kind="ExternalInput").ap()
    cb2_h = nc.dram_tensor("cb2", [128, MT], F32, kind="ExternalInput").ap()

    out_h = nc.dram_tensor("out", [H, L], F32, kind="ExternalOutput").ap()

    wbf_h = nc.dram_tensor("wbf", [WMT, H, 128], BF16, kind="Internal").ap()
    wob_h = nc.dram_tensor("wob", [HK, DIN, 128], BF16, kind="Internal").ap()
    wq1_h = nc.dram_tensor("wq1", [MT, H, 128], BF16, kind="Internal").ap()
    wq2_h = nc.dram_tensor("wq2", [MT, 128, H], BF16, kind="Internal").ap()
    bcr_h = nc.dram_tensor("bcr", [32, T], BF16, kind="Internal").ap()
    h2sp_h = nc.dram_tensor("h2sp", [MT, 128, T], BF16, kind="Internal").ap()
    xsp_h = nc.dram_tensor("xsp", [H, L], F32, kind="Internal").ap()

    dbg_handles = {}

    def dbg_out(name, shape, dtype):
        h = nc.dram_tensor(name, shape, dtype, kind="ExternalOutput").ap()
        dbg_handles[name] = h
        return h

    with ExitStack() as st:
        def sb(name, shape, dt=F32):
            return st.enter_context(nc.sbuf_tensor(name, shape, dt))[:]

        def psm(name, shape, dt=F32):
            return st.enter_context(nc.psum_tensor(name, shape, dt))[:]

        s_pe = Sem(st, nc, "s_pe")
        s_act = Sem(st, nc, "s_act")
        s_dve = Sem(st, nc, "s_dve")
        s_gps = Sem(st, nc, "s_gps")
        s_spd = Sem(st, nc, "s_spd", 16)
        s_gpd = Sem(st, nc, "s_gpd", 16)
        s_sp = [Sem(st, nc, "s_sp0", 16), Sem(st, nc, "s_sp1", 16)]
        s_gp = [Sem(st, nc, "s_gp0", 16), Sem(st, nc, "s_gp1", 16)]
        s_spx = s_sp + [Sem(st, nc, "s_sp2", 16), Sem(st, nc, "s_sp3", 16)]
        s_gpx = s_gp + [Sem(st, nc, "s_gp2", 16), Sem(st, nc, "s_gp3", 16)]

        xf = sb("xf", [128, HK * T], BF16)
        scf = sb("scf", [128, HK * T])
        nb1 = sb("nb1", [128, HK * T], BF16)
        hp = [sb(f"hp{i}", [128, T + KM - 1], BF16) for i in range(2)]
        mt3 = sb("mt3", [128, GT * (KM - 1)], BF16)
        hsb = sb("hsb", [128, GT * T], BF16)
        resb = sb("resb", [128, GT * T], BF16)
        dlg = [sb(f"dlg{i}", [128, T], FP16) for i in range(2)]
        dxg = sb("dxg", [128, T], BF16)
        xdbq = sb("xdbq", [96, T], BF16)
        Brep = sb("Brep", [128, NST * T], BF16)
        Crep = sb("Crep", [128, NST * T], BF16)
        dAq = [sb(f"dAq{i}", [128, 2 * T], BF16) for i in range(3)]
        dBq = [sb(f"dBq{i}", [128, 2 * T], BF16) for i in range(3)]
        NSL = 3
        zq = sb("zq", [128, 2 * T], BF16)
        carry = sb("carry", [128, GT * NST])
        yb = sb("yb", [128, GT * T], BF16)
        sp_a = sb("sp_a", [128, T])
        sp_r = [sb(f"sp_r{i}", [128, T]) for i in range(2)]
        sp_u = [sb(f"sp_u{i}", [128, T]) for i in range(2)]
        xpr = sb("xpr", [128, HK * T])
        xq1 = [sb(f"xq1_{i}", [128, HK * (T + 2)], BF16) for i in range(2)]
        mxsq = sb("mxsq", [128, T], BF16)
        zrot = sb("zrot", [128, 128], BF16)
        ftail = sb("ftail", [128, MT], BF16)
        h2t = [sb(f"h2t_{i}", [128, T], BF16) for i in range(2)]
        mx2 = sb("mx2", [128, T], BF16)
        ofb = nb1
        wpA = [sb(f"wpA{i}", [128, HK * 128], BF16) for i in range(2)]
        wpB = [sb("wpB0", [128, GT * 128], BF16)]
        wpC = [wpA[0][:, 0:H], wpA[1][:, 0:H]]
        wpC4 = [wpA[0][:, 0:512], wpA[0][:, 512:1024],
                wpA[1][:, 0:512], wpA[1][:, 512:1024]]
        W1S = [wpA[0], wpA[1],
               wpB[0][:, 0:HK * 128], wpB[0][:, HK * 128:2 * HK * 128]]
        H2S = [h2t[0], h2t[1],
               scf.bitcast(BF16)[:, 3584:4096], scf.bitcast(BF16)[:, 4096:4608]]
        dxgs = [dxg, scf.bitcast(BF16)[:, 3072:3584]]
        hp2 = [hp[0][:, 0:T + KF], hp[1][:, 0:T + KF],
               scf.bitcast(BF16)[:, 4608:4608 + T + KF],
               scf.bitcast(BF16)[:, 5124:5124 + T + KF]]
        h2A = [xpr.bitcast(BF16)[:, i * 512:(i + 1) * 512] for i in range(16)]
        ones_a = sb("ones_a", [128, 1])
        ones_b = sb("ones_b", [128, 1])
        onesh_a = sb("onesh_a", [128, 1], BF16)
        onesh_b = sb("onesh_b", [128, 1], BF16)
        ones1 = sb("ones1", [1, 128])
        idnt = [sb(f"idnt{i}", [128, 128], BF16) for i in range(2)]
        iof = sb("iof", [128, 128])
        iop = sb("iop", [128, 1])
        maskd = sb("maskd", [128, 128])
        A_sb = sb("A_sb", [128, GT * NST])
        D_sb = sb("D_sb", [128, GT])
        bdt_sb = sb("bdt_sb", [128, GT])
        cw1_sb = sb("cw1_sb", [128, GT * KM])
        cw2_sb = sb("cw2_sb", [128, MT * KF])
        cb2_sb = sb("cb2_sb", [128, MT])
        diag = [sb(f"diag{i}", [128, 128], BF16) for i in range(KM * 2)]
        dgw = diag + [scf.bitcast(BF16)[:, 5640 + i * 128:5768 + i * 128]
                      for i in range(10)]
        wxq = sb("wxq", [128, GT * 96], BF16)
        wdtq = sb("wdtq", [RNK, GT * 128], BF16)
        swrow = sb("swrow", [1, 256])
        swb = sb("swb", [128, 2])
        r_m = Crep.bitcast(F32)[0:1, 2048:2560]
        r_r = Crep.bitcast(F32)[0:1, 2560:3072]
        r_t2 = Crep.bitcast(F32)[0:1, 3072:3584]
        r_is = Crep.bitcast(F32)[0:1, 3584:4096]
        r_t1 = sb("r_t1", [1, T])
        r_mx = sb("r_mx", [1, T])
        r_s = sb("r_s", [1, T])
        mxq = sb("mxq", [128, 8])

        pb = [psm(f"pb{i}", [128, 512]) for i in range(8)]

        block = st.enter_context(nc.Block())

        def PE(f): block.tensor(f)
        def VE(f): block.vector(f)
        def SC(f): block.scalar(f)
        def GP(f): block.gpsimd(f)
        def SY(f): block.sync(f)

        def barrier():
            all_s = [(s_pe,), (s_act,), (s_dve,), (s_gps,), (s_spd,), (s_gpd,),
                     (s_sp[0],), (s_sp[1],), (s_gp[0],), (s_gp[1],),
                     (s_spx[2],), (s_spx[3],), (s_gpx[2],), (s_gpx[3],)]
            vals = [(s, s.v) for (s,) in all_s]

            def mk(skip):
                def f(eng):
                    for s, v in vals:
                        if s is not skip and v > 0:
                            eng.wait_ge(s.h, v)
                return f
            PE(mk(s_pe))
            VE(mk(s_dve))
            SC(mk(s_act))
            GP(mk(s_gps))
            SY(mk(None))

        def dbg_tap(name, ap_in, rows, cols, dtype):
            if not debug:
                return
            h = dbg_out(name, [rows, cols], dtype)
            barrier()

            def f2(sy):
                s_spd.inc(sy.dma_start(out=h, in_=ap_in))
            SY(f2)
            barrier()

        def setup_sync(sy):
            s_spd.inc(sy.dma_start(out=bdt_sb, in_=bdt_h))
            s_spd.inc(sy.dma_start(out=A_sb, in_=alog_h))
            s_spd.inc(sy.dma_start(out=D_sb, in_=dvec_h))
            s_spd.inc(sy.dma_start(out=cw1_sb, in_=cw1_h))
            s_spd.inc(sy.dma_start(out=cw2_sb, in_=cw2_h))
            s_spd.inc(sy.dma_start(out=cb2_sb, in_=cb2_h))
        SY(setup_sync)

        def setup_gps(g):
            g.iota(iof, [[1, 128]], channel_multiplier=0,
                   allow_small_or_imprecise_dtypes=True)
            g.iota(iop, [[0, 1]], channel_multiplier=1,
                   allow_small_or_imprecise_dtypes=True)
            g.memset(swrow, 1.0)
            g.memset(zrot, 0.0)
            g.memset(xq1[0], 0.0)
            g.memset(xq1[1], 0.0)
            g.memset(ones_a, 1.0)
            g.memset(ones_b, 1.0)
            g.memset(onesh_a, 1.0)
            g.memset(onesh_b, 1.0)
            s_gps.inc(g.memset(ones1, 1.0))
        GP(setup_gps)

        def setup_ve(v):
            v.wait_ge(s_gps.h, s_gps.v)
            v.tensor_scalar(out=maskd, in0=iof, scalar1=iop, scalar2=None,
                            op0=ALU.is_equal)
            v.tensor_copy(out=idnt[0], in_=maskd)
            s_dve.inc(v.tensor_copy(out=idnt[1], in_=maskd))
        VE(setup_ve)

        def setup_sc(s):
            s.wait_ge(s_spd.h, s_spd.v)
            s_act.inc(s.activation(out=A_sb, in_=A_sb, func=AF.Exp))
        SC(setup_sc)

        def setup_ve2(v):
            v.wait_ge(s_act.h, s_act.v)
            s_dve.inc(v.tensor_scalar(out=A_sb, in0=A_sb, scalar1=-1.0,
                                      scalar2=None, op0=ALU.mult))
        VE(setup_ve2)
        barrier()

        def res_w_sync(sy):
            for k in range(GT):
                s_spd.inc(sy.dma_start(out=scf[:, k * 96:(k + 1) * 96],
                                       in_=wxT_h[k * 128:(k + 1) * 128, :]))
            for k in range(GT):
                s_spd.inc(sy.dma_start(
                    out=scf[0:RNK, GT * 96 + k * 128:GT * 96 + (k + 1) * 128],
                    in_=wdtT_h[:, k * 128:(k + 1) * 128]))
        SY(res_w_sync)

        def res_w_ve(v):
            v.wait_ge(s_spd.h, s_spd.v)
            v.tensor_copy(out=wxq, in_=scf[:, 0:GT * 96])
            s_dve.inc(v.tensor_copy(
                out=wdtq, in_=scf[0:RNK, GT * 96:GT * 96 + GT * 128]))
        VE(res_w_ve)
        barrier()

        def conv_weight(src_h, rows, n_m, dst_h, quant=False):
            kb_n = rows // 128
            width = n_m * 128
            lbuf = [scf[:, 0:width], xpr[:, 0:width]]
            obuf = [nb1[:, 0:width], yb[:, 0:width]]
            spd0, dve0, gpd0 = s_spd.v, s_dve.v, s_gpd.v
            for kb in range(kb_n):
                def f_sy(sy, kb=kb):
                    if kb >= 2:
                        sy.wait_ge(s_dve.h, dve0 + kb - 1)
                    s_spd.inc(sy.dma_start(
                        out=lbuf[kb % 2],
                        in_=src_h[kb * 128:(kb + 1) * 128, 0:width]))
                SY(f_sy)

                def f_ve(v, kb=kb):
                    v.wait_ge(s_spd.h, spd0 + 16 * (kb + 1))
                    if kb >= 2:
                        v.wait_ge(s_gpd.h, gpd0 + 16 * (kb - 1))
                    src = lbuf[kb % 2]
                    if not quant:
                        s_dve.inc(v.tensor_copy(out=obuf[kb % 2], in_=src))
                    else:
                        v.tensor_scalar(out=src, in0=src,
                                        scalar1=swb[:, 0:1], scalar2=None,
                                        op0=ALU.mult)
                        v.tensor_scalar(out=src, in0=src,
                                        scalar1=MAGIC, scalar2=MAGIC,
                                        op0=ALU.add, op1=ALU.subtract)
                        v.tensor_scalar(out=src, in0=src,
                                        scalar1=1.0, scalar2=-1.0,
                                        op0=ALU.min, op1=ALU.max)
                        s_dve.inc(v.tensor_scalar(out=obuf[kb % 2], in0=src,
                                                  scalar1=swb[:, 1:2],
                                                  scalar2=None, op0=ALU.mult))
                VE(f_ve)

                def f_gp(g, kb=kb):
                    g.wait_ge(s_dve.h, dve0 + kb + 1)
                    dst = dst_h[:, kb * 128:(kb + 1) * 128, :]
                    dst = dst.rearrange("m p c -> p m c")
                    s_gpd.inc(g.dma_start(
                        out=dst,
                        in_=obuf[kb % 2].rearrange("p (m c) -> p m c", c=128)))
                GP(f_gp)
            barrier()

        def wsum_pass(src_h, rows, width, total):
            kb_n = rows // 128
            lbuf = [scf[:, 0:width], xpr[:, 0:width]]

            def z_ve(v):
                s_dve.inc(v.memset(carry[:, 0:1], 0.0))
            VE(z_ve)
            spd0, act0, dve0 = s_spd.v, s_act.v, s_dve.v
            for kb in range(kb_n):
                def f_sy(sy, kb=kb):
                    if kb >= 2:
                        sy.wait_ge(s_act.h, act0 + kb - 1)
                    s_spd.inc(sy.dma_start(
                        out=lbuf[kb % 2],
                        in_=src_h[kb * 128:(kb + 1) * 128, 0:width]))
                SY(f_sy)

                def f_sc(s, kb=kb):
                    s.wait_ge(s_spd.h, spd0 + 16 * (kb + 1))
                    if kb >= 2:
                        s.wait_ge(s_dve.h, dve0 + kb - 1)
                    s_act.inc(s.activation(
                        out=nb1[:, 0:width], in_=lbuf[kb % 2], func=AF.Abs,
                        accum_out=sp_a[:, kb % 2:kb % 2 + 1]))
                SC(f_sc)

                def f_ve(v, kb=kb):
                    v.wait_ge(s_act.h, act0 + kb + 1)
                    s_dve.inc(v.tensor_tensor(
                        out=carry[:, 0:1], in0=carry[:, 0:1],
                        in1=sp_a[:, kb % 2:kb % 2 + 1], op=ALU.add))
                VE(f_ve)

            def f_pe(t):
                t.wait_ge(s_dve.h, s_dve.v)
                s_pe.inc(t.matmul(pb[6][0:1, 0:1], carry[:, 0:1], ones_a,
                                  start=True, stop=True))
            PE(f_pe)

            def f_ve2(v):
                v.wait_ge(s_pe.h, s_pe.v)
                s_dve.inc(v.tensor_scalar(out=swrow[:, 0:1], in0=pb[6][0:1, 0:1],
                                          scalar1=1.0 / total, scalar2=1e-5,
                                          op0=ALU.mult, op1=ALU.max))
            VE(f_ve2)

            def f_sc_r(s):
                s.wait_ge(s_dve.h, s_dve.v)
                s.activation(out=swrow[:, 1:2], in_=swrow[:, 0:1], func=AF.Ln)
                s_act.inc(s.activation(out=swrow[:, 128:129], in_=swrow[:, 1:2],
                                       func=AF.Exp, scale=-1.0))
            SC(f_sc_r)

            def f_pe2(t):
                t.wait_ge(s_dve.h, s_dve.v)
                t.wait_ge(s_act.h, s_act.v)
                t.matmul(pb[6][:, 2:3], ones1, swrow[:, 128:129], start=True, stop=True)
                s_pe.inc(t.matmul(pb[6][:, 3:4], ones1, swrow[:, 0:1],
                                  start=True, stop=True))
            PE(f_pe2)

            def f_ve3(v):
                v.wait_ge(s_pe.h, s_pe.v)
                v.tensor_copy(out=swb[:, 0:1], in_=pb[6][:, 2:3])
                s_dve.inc(v.tensor_copy(out=swb[:, 1:2], in_=pb[6][:, 3:4]))
            VE(f_ve3)
            barrier()

        conv_weight(winT_h, H, WMT, wbf_h)
        conv_weight(woutT_h, DIN, HK, wob_h)
        if stop_after != "front":
            wsum_pass(w1T_h, H, DH, H * DH)
            dbg_tap("d_swb1", swb[:, 0:2], 128, 2, F32)
            conv_weight(w1T_h, H, MT, wq1_h, quant=True)
            wsum_pass(w2T_h, DH, H, DH * H)
            lbuf2 = [scf[:, 0:H], xpr[:, 0:H]]
            obuf2 = [nb1[:, 0:H], yb[:, 0:H]]
            spd0w, dve0w, gpd0w = s_spd.v, s_dve.v, s_gpd.v
            for kb in range(DH // 128):
                def f_sy(sy, kb=kb):
                    if kb >= 2:
                        sy.wait_ge(s_dve.h, dve0w + kb - 1)
                    s_spd.inc(sy.dma_start(out=lbuf2[kb % 2],
                                           in_=w2T_h[kb * 128:(kb + 1) * 128, :]))
                SY(f_sy)

                def f_ve(v, kb=kb):
                    v.wait_ge(s_spd.h, spd0w + 16 * (kb + 1))
                    if kb >= 2:
                        v.wait_ge(s_gpd.h, gpd0w + 16 * (kb - 1))
                    src = lbuf2[kb % 2]
                    v.tensor_scalar(out=src, in0=src,
                                    scalar1=swb[:, 0:1], scalar2=None,
                                    op0=ALU.mult)
                    v.tensor_scalar(out=src, in0=src,
                                    scalar1=MAGIC, scalar2=MAGIC,
                                    op0=ALU.add, op1=ALU.subtract)
                    v.tensor_scalar(out=src, in0=src,
                                    scalar1=1.0, scalar2=-1.0,
                                    op0=ALU.min, op1=ALU.max)
                    s_dve.inc(v.tensor_scalar(out=obuf2[kb % 2], in0=src,
                                              scalar1=swb[:, 1:2], scalar2=None,
                                              op0=ALU.mult))
                VE(f_ve)

                def f_gp(g, kb=kb):
                    g.wait_ge(s_dve.h, dve0w + kb + 1)
                    s_gpd.inc(g.dma_start(out=wq2_h[kb, :, :],
                                          in_=obuf2[kb % 2]))
                GP(f_gp)
            barrier()

        NPAIR = NST // 2

        def merged_stage(c, w, n_chunks):
            cheap = (w is not None and w == n_chunks)
            WC = 2 if cheap else T
            V = {}
            ops = {"pe": [], "act": [], "dve": [], "gps": []}

            win = {"pe": {}, "act": {}, "dve": {}, "gps": {}}
            if w is not None:
                def at(e, q, item):
                    win[e].setdefault(q, []).append(item)
                for m in range(MT):
                    qa = min(6, m // 5)
                    at("pe", qa, ("w1", m, 1))
                    at("act", qa, ("wev", m, 1))
                    at("dve", qa, ("wdiag", m, 1))
                    at("pe", qa, ("wconv", m, 1))
                    at("act", qa, ("wsilu", m, 1))
                    at("dve", min(7, qa + 1), ("wmx", m, 1))
                at("pe", 7, ("btr1", 0, 1))
                at("dve", 7, ("bred", 0, 1))
                at("pe", 7, ("btr2", 0, 1))
                at("act", 7, ("bcp", 0, 1))
                at("act", 7, ("bsqrt", 0, 1))
                at("dve", 7, ("bscale", 0, 1))
                at("pe", 7, ("bbc", 0, 1))
                at("dve", 7, ("bcopy", 0, 1))
                for quad in range(2):
                    for k in range(MT):
                        idx = quad * MT + k
                        qc = 8 + idx // 8
                        if quad == 0:
                            at("gps", qc, ("wmul", (0, k), 1))
                            at("dve", qc, ("wxq", (0, k), 1))
                        at("pe", qc, ("w2", (quad, k), 1))
                    at("act", min(15, 8 + (quad * MT + MT - 1) // 8),
                       ("wofev", quad, 1))


            if c is not None:
                ops["pe"] += [("wdt", 0, 1), ("wdt", 1, 1)]
                ops["act"] += ([("sp", 0, 1)]
                               + [("dA", (0, p), 1) for p in range(NSL)])
                ops["dve"] += [("dlt", 0, 1), ("dx", 0, 1)]
                for g in range(GT):
                    scan_pe = ([("ymm", (g, p), 1) for p in range(NPAIR)]
                               + [("hsd", g, 1)])
                    if g + 2 < GT:
                        scan_pe.append(("wdt", g + 2, 1))
                    ops["pe"] += scan_pe + win["pe"].get(g - 1, [])
                    av = [("dA", (g, p), 1) for p in range(NSL, NPAIR)]
                    if g + 1 < GT:
                        av += ([("sp", g + 1, 1)]
                               + [("dA", (g + 1, p), 1) for p in range(NSL)])
                    av.append(("hsdc", g, 1))
                    ops["act"] += av + win["act"].get(g - 1, [])
                    dv = []
                    for p in range(NPAIR):
                        if p == 6 and g + 1 < GT:
                            dv += [("dlt", g + 1, 1), ("dx", g + 1, 1)]
                        if c == 0:
                            dv.append(("cz", (g, p), 1))
                        dv += [("dbq", (g, p), 1), ("scan", (g, p), 1),
                               ("pq", (g, p), 1)]
                    dv.append(("yev", g, 1))
                    ops["dve"] += dv + win["dve"].get(g - 1, [])
                    ops["gps"] += win["gps"].get(g - 1, [])
                for e in ("pe", "act", "dve", "gps"):
                    for q in sorted(win[e]):
                        if q >= GT - 1:
                            ops[e] += win[e][q]
            else:
                for e in ("pe", "act", "dve", "gps"):
                    for q in sorted(win[e]):
                        ops[e] += win[e][q]
            full = ops

            base = {"pe": s_pe.v, "act": s_act.v, "dve": s_dve.v,
                    "gps": s_gps.v}
            for e in ("pe", "act", "dve", "gps"):
                vv = base[e]
                for kind, key, n in full[e]:
                    vv += n
                    V[(kind, key)] = vv
            if w is not None:
                gp_base = [s_gp[0].v, s_gp[1].v]
                cnt4 = [s.v for s in s_spx]
                wp_done = {}
                for m in range(MT):
                    sl = m % 4
                    cnt4[sl] += 16
                    wp_done[m] = (sl, cnt4[sl])
                wld_done = {}
                for quad in range(2):
                    for k in range(MT):
                        sl = (quad * MT + k) % 4
                        cnt4[sl] += 16
                        wld_done[(quad, k)] = (sl, cnt4[sl])
                cntg = [s.v for s in s_gpx]
                spill_done = {}
                for m in range(MT):
                    cntg[m % 2] += 16
                    spill_done[m] = (m % 2, cntg[m % 2])
                ldh2_done, splq_done, ldxq_done = {}, {}, {}
                for k in range(MT):
                    p = k % 4
                    cntg[p] += 16
                    ldh2_done[k] = (p, cntg[p])
                    cntg[p] += 16
                    splq_done[k] = (p, cntg[p])
                for k in range(MT):
                    p = k % 4
                    cntg[p] += 16
                    ldxq_done[k] = (p, cntg[p])
                xq_cur = xq1[w % 2] if not cheap else None
                xq_prv = xq1[(w - 1) % 2] if w > 0 else None
                if cheap:
                    xq_prv = xq1[(w - 1) % 2]
            TE = T + 2

            def pe_emit(t):
                for kind, key, n in full["pe"]:
                    if kind == "wdt":
                        g = key
                        if g >= 2:
                            t.wait_ge(s_act.h, V[("sp", g - 2)])
                        inst = t.matmul(pb[g % 2][:, 0:T],
                                        wdtq[0:RNK, g * 128:(g + 1) * 128],
                                        xdbq[0:RNK, 0:T], start=True, stop=True)
                        inst.then_inc(s_pe.h, 1)
                    elif kind == "ymm":
                        g, p = key
                        t.wait_ge(s_dve.h, V[("pq", (g, p))])
                        if p == 0 and g >= 2:
                            t.wait_ge(s_dve.h, V[("yev", g - 2)])
                        sl = (g * NPAIR + p) % NSL
                        bank = pb[4 + g % 2]
                        for j in range(2):
                            inst = t.matmul(bank[:, 0:T], idnt[j],
                                            dBq[sl][:, j * T:(j + 1) * T],
                                            start=(p == 0 and j == 0), stop=False)
                        inst.then_inc(s_pe.h, 1)
                    elif kind == "hsd":
                        g = key
                        t.wait_ge(s_act.h, V[("hsdc", g)])
                        inst = t.matmul(pb[4 + g % 2][:, 0:T], idnt[0],
                                        dlg[g % 2].bitcast(BF16)[:, 0:T],
                                        start=False, stop=True)
                        inst.then_inc(s_pe.h, 1)
                    elif kind == "w1":
                        m = key
                        sl, vv2 = wp_done[m]
                        t.wait_ge(s_spx[sl].h, vv2)
                        if c is None:
                            bmn, bmt = 2 + 2 * (m % 2), 3 + 2 * (m % 2)
                            if m >= 2:
                                t.wait_ge(s_act.h, V[("wev", m - 2)])
                        else:
                            bmn, bmt = 2, 3
                            if m >= 1:
                                t.wait_ge(s_act.h, V[("wev", m - 1)])
                        if not cheap:
                            for k in range(HK):
                                inst = t.matmul(
                                    pb[bmn][:, 0:512],
                                    W1S[sl][:, k * 128:(k + 1) * 128],
                                    xq_cur[:, k * TE:k * TE + 512],
                                    start=(k == 0), stop=(k == HK - 1))
                            for k in range(HK):
                                inst = t.matmul(
                                    pb[bmt][:, 0:2],
                                    W1S[sl][:, k * 128:(k + 1) * 128],
                                    xq_cur[:, k * TE + 512:k * TE + 514],
                                    start=(k == 0), stop=(k == HK - 1))
                        else:
                            for k in range(HK):
                                inst = t.matmul(
                                    pb[bmn][:, 0:2],
                                    W1S[sl][:, k * 128:(k + 1) * 128],
                                    xq_prv[:, k * TE + 512:k * TE + 514],
                                    start=(k == 0), stop=(k == HK - 1))
                        inst.then_inc(s_pe.h, 1)
                    elif kind == "wconv":
                        m = key
                        t.wait_ge(s_dve.h, V[("wdiag", m)])
                        t.wait_ge(s_act.h, V[("wev", m)])
                        bcv = 6 + (m % 2) if c is None else 6
                        if c is None:
                            if m >= 2:
                                t.wait_ge(s_act.h, V[("wsilu", m - 2)])
                        elif m >= 1:
                            t.wait_ge(s_act.h, V[("wsilu", m - 1)])
                        for tap in range(KF):
                            inst = t.matmul(pb[bcv][:, 0:WC],
                                            dgw[(m % 6) * 3 + tap],
                                            hp2[m % 4][:, tap:tap + WC],
                                            start=(tap == 0), stop=(tap == KF - 1))
                        inst.then_inc(s_pe.h, 1)
                    elif kind == "btr1":
                        t.wait_ge(s_dve.h, V[("wmx", MT - 1)])
                        for a in range(4):
                            inst = t.transpose(
                                pb[7].bitcast(BF16)[:, a * 128:(a + 1) * 128],
                                mx2[:, a * 128:(a + 1) * 128], idnt[0])
                        inst.then_inc(s_pe.h, 1)
                    elif kind == "btr2":
                        t.wait_ge(s_dve.h, V[("bred", 0)])
                        for a in range(4):
                            inst = t.transpose(pb[7][0:1, a * 128:(a + 1) * 128],
                                               mxq[:, 4 + a:5 + a], maskd)
                        inst.then_inc(s_pe.h, 1)
                    elif kind == "bbc":
                        t.wait_ge(s_dve.h, V[("bscale", 0)])
                        inst = t.matmul(pb[7][:, 0:T], ones1, r_s,
                                        start=True, stop=True)
                        inst.then_inc(s_pe.h, 1)
                    elif kind == "w2":
                        quad, k = key
                        sl, vv2 = wld_done[(quad, k)]
                        t.wait_ge(s_spx[sl].h, vv2)
                        if quad == 0:
                            t.wait_ge(s_dve.h, V[("wxq", (0, k))])
                        else:
                            pg, vg = ldxq_done[k]
                            t.wait_ge(s_gpx[pg].h, vg)
                            if k == 0:
                                t.wait_ge(s_act.h, V[("wofev", 0)])
                        for mi in range(4):
                            bkz = (2, 3, 6, 7)[mi]
                            inst = t.matmul(
                                pb[bkz][:, 0:WC],
                                wpC4[sl][:, mi * 128:(mi + 1) * 128],
                                H2S[k % 4][:, 0:WC],
                                start=(k == 0), stop=(k == MT - 1))
                        inst.then_inc(s_pe.h, 1)

            def act_emit(s):
                for kind, key, n in full["act"]:
                    if kind == "sp":
                        g = key
                        s.wait_ge(s_pe.h, V[("wdt", g)])
                        if g >= 2:
                            s.wait_ge(s_dve.h, V[("dlt", g - 2)])
                        bank = pb[g % 2][:, 0:T]
                        bcol = bdt_sb[:, g:g + 1]
                        gs = g % 2
                        s.activation(out=sp_a, in_=bank, func=AF.Abs,
                                     bias=bcol)
                        s.activation(out=sp_r[gs], in_=bank, func=AF.Relu,
                                     bias=bcol)
                        s.activation(out=sp_u[gs], in_=sp_a, func=AF.Exp,
                                     scale=-1.0)
                        inst = s.activation(out=sp_u[gs], in_=sp_u[gs],
                                            func=AF.Ln, bias=1.0)
                        inst.then_inc(s_act.h, 1)
                    elif kind == "dA":
                        g, p = key
                        s.wait_ge(s_dve.h, V[("dlt", g)])
                        i = g * NPAIR + p
                        sl = i % NSL
                        if i >= NSL:
                            prev = divmod(i - NSL, NPAIR)
                            s.wait_ge(s_dve.h, V[("scan", prev)])
                        for j in range(2):
                            nn2 = p * 2 + j
                            inst = s.activation(
                                out=dAq[sl][:, j * T:(j + 1) * T],
                                in_=dlg[g % 2], func=AF.Exp,
                                scale=A_sb[:, g * NST + nn2:g * NST + nn2 + 1])
                        inst.then_inc(s_act.h, 1)
                    elif kind == "hsdc":
                        g = key
                        s.wait_ge(s_dve.h, V[("dx", g)])
                        if g >= 1:
                            s.wait_ge(s_pe.h, V[("hsd", g - 1)])
                        inst = s.activation(out=dlg[g % 2].bitcast(BF16)[:, 0:T],
                                            in_=hsb[:, g * T:(g + 1) * T],
                                            func=AF.Copy, scale=D_sb[:, g:g + 1])
                        inst.then_inc(s_act.h, 1)
                    elif kind == "wev":
                        m = key
                        s.wait_ge(s_pe.h, V[("w1", m)])
                        if m >= 4:
                            s.wait_ge(s_pe.h, V[("wconv", m - 4)])
                        bmn = 2 + 2 * (m % 2) if c is None else 2
                        bmt = bmn + 1
                        if not cheap:
                            s.activation(out=hp2[m % 4][:, 0:512],
                                         in_=pb[bmn][:, 0:512], func=AF.Copy)
                            inst = s.activation(out=hp2[m % 4][:, 512:514],
                                                in_=pb[bmt][:, 0:2], func=AF.Copy)
                        else:
                            inst = s.activation(out=hp2[m % 4][:, 0:2],
                                                in_=pb[bmn][:, 0:2], func=AF.Copy)
                        inst.then_inc(s_act.h, 1)
                    elif kind == "wsilu":
                        m = key
                        s.wait_ge(s_pe.h, V[("wconv", m)])
                        if m >= 16:
                            sl2, vv3 = spill_done[m - 16]
                            s.wait_ge(s_gp[sl2].h, vv3)
                            s.wait_ge(s_dve.h, V[("wmx", m - 16)])
                        bcv = 6 + (m % 2) if c is None else 6
                        inst = s.activation(out=h2A[m % 16][:, 0:WC],
                                            in_=pb[bcv][:, 0:WC], func=AF.Silu,
                                            bias=cb2_sb[:, m:m + 1])
                        inst.then_inc(s_act.h, 1)
                    elif kind == "bcp":
                        s.wait_ge(s_pe.h, V[("btr2", 0)])
                        for a in range(4):
                            inst = s.activation(
                                out=r_mx[:, a * 128:(a + 1) * 128],
                                in_=pb[7][0:1, a * 128:(a + 1) * 128],
                                func=AF.Copy)
                        inst.then_inc(s_act.h, 1)
                    elif kind == "bsqrt":
                        s.activation(out=r_t1, in_=r_mx, func=AF.Ln)
                        inst = s.activation(out=r_mx, in_=r_t1, func=AF.Exp,
                                            scale=0.5)
                        inst.then_inc(s_act.h, 1)
                    elif kind == "wofev":
                        quad = key
                        s.wait_ge(s_pe.h, V[("w2", (quad, MT - 1))])
                        for mi in range(4):
                            m = quad * 4 + mi
                            bkz = (2, 3, 6, 7)[mi]
                            inst = s.activation(
                                out=ofb[:, m * T:m * T + WC],
                                in_=pb[bkz][:, 0:WC], func=AF.Copy)
                        inst.then_inc(s_act.h, 1)

            def dve_emit(v):
                for kind, key, n in full["dve"]:
                    if kind == "dlt":
                        g = key
                        v.wait_ge(s_act.h, V[("sp", g)])
                        if g >= 2:
                            v.wait_ge(s_pe.h, V[("hsd", g - 2)])
                        inst = v.tensor_tensor(out=dlg[g % 2], in0=sp_r[g % 2],
                                               in1=sp_u[g % 2], op=ALU.add)
                        inst.then_inc(s_dve.h, 1)
                    elif kind == "dx":
                        g = key
                        inst = v.tensor_tensor(out=dxgs[g % 2], in0=dlg[g % 2],
                                               in1=hsb[:, g * T:(g + 1) * T],
                                               op=ALU.mult)
                        inst.then_inc(s_dve.h, 1)
                    elif kind == "cz":
                        g, p = key
                        inst = v.memset(
                            carry[:, g * NST + p * 2:g * NST + p * 2 + 2], 0.0)
                        inst.then_inc(s_dve.h, 1)
                        v.wait_ge(s_dve.h, V[("cz", (g, p))])
                    elif kind == "dbq":
                        g, p = key
                        i = g * NPAIR + p
                        sl = i % NSL
                        if i >= NSL:
                            v.wait_ge(s_pe.h, V[("ymm", divmod(i - NSL, NPAIR))])
                        inst = v.tensor_tensor(
                            out=dBq[sl].rearrange("q (n t) -> q n t", n=2),
                            in0=dxgs[g % 2].unsqueeze(1).broadcast_to(
                                [128, 2, T]),
                            in1=Brep.rearrange("q (n t) -> q n t", n=NST)
                            [:, p * 2:p * 2 + 2, :], op=ALU.mult)
                        inst.then_inc(s_dve.h, 1)
                    elif kind == "scan":
                        g, p = key
                        sl = (g * NPAIR + p) % NSL
                        v.wait_ge(s_act.h, V[("dA", (g, p))])
                        for j in range(2):
                            nn2 = p * 2 + j
                            inst = v.tensor_tensor_scan(
                                out=zq[:, j * T:(j + 1) * T],
                                data0=dAq[sl][:, j * T:(j + 1) * T],
                                data1=dBq[sl][:, j * T:(j + 1) * T],
                                initial=carry[:, g * NST + nn2:g * NST + nn2 + 1],
                                op0=ALU.mult, op1=ALU.add)
                        inst.then_inc(s_dve.h, 1)
                        v.wait_ge(s_dve.h, V[("scan", (g, p))])
                    elif kind == "pq":
                        g, p = key
                        sl = (g * NPAIR + p) % NSL
                        for j in range(2):
                            nn2 = p * 2 + j
                            v.tensor_copy(
                                out=carry[:, g * NST + nn2:g * NST + nn2 + 1],
                                in_=zq[:, j * T + T - 1:j * T + T])
                        inst = v.tensor_tensor(
                            out=dBq[sl], in0=zq,
                            in1=Crep[:, p * 2 * T:(p * 2 + 2) * T], op=ALU.mult)
                        inst.then_inc(s_dve.h, 1)
                    elif kind == "yev":
                        g = key
                        v.wait_ge(s_pe.h, V[("hsd", g)])
                        inst = v.tensor_tensor(out=yb[:, g * T:(g + 1) * T],
                                               in0=pb[4 + g % 2][:, 0:T],
                                               in1=resb[:, g * T:(g + 1) * T],
                                               op=ALU.mult)
                        inst.then_inc(s_dve.h, 1)
                    elif kind == "wdiag":
                        m = key
                        if m >= 6:
                            v.wait_ge(s_pe.h, V[("wconv", m - 6)])
                        if m == 0:
                            v.memset(mx2, 1.0 if cheap else 0.0)
                        if cheap:
                            v.wait_ge(s_act.h, V[("wev", m)])
                            v.memset(hp2[m % 4][:, 2:4], 0.0)
                        for tap in range(KF):
                            inst = v.tensor_scalar(
                                out=dgw[(m % 6) * 3 + tap], in0=maskd,
                                scalar1=cw2_sb[:, m * KF + tap:m * KF + tap + 1],
                                scalar2=None, op0=ALU.mult)
                        inst.then_inc(s_dve.h, 1)
                    elif kind == "wmx":
                        m = key
                        v.wait_ge(s_act.h, V[("wsilu", m)])
                        v.tensor_tensor(out=mxsq[:, 0:WC],
                                        in0=h2A[m % 16][:, 0:WC],
                                        in1=h2A[m % 16][:, 0:WC], op=ALU.mult)
                        inst = v.tensor_tensor(out=mx2[:, 0:WC],
                                               in0=mx2[:, 0:WC],
                                               in1=mxsq[:, 0:WC], op=ALU.max)
                        inst.then_inc(s_dve.h, 1)
                    elif kind == "bred":
                        v.wait_ge(s_pe.h, V[("btr1", 0)])
                        inst = v.tensor_reduce(
                            out=mxq[:, 4:8],
                            in_=pb[7].bitcast(BF16)[:, 0:512].rearrange(
                                "p (a q) -> p a q", a=4),
                            axis=mybir.AxisListType.X, op=ALU.max)
                        inst.then_inc(s_dve.h, 1)
                    elif kind == "bscale":
                        v.wait_ge(s_act.h, V[("bsqrt", 0)])
                        v.tensor_scalar(out=r_t1, in0=r_mx, scalar1=1e-5,
                                        scalar2=None, op0=ALU.max)
                        v.reciprocal(out=r_s, in_=r_t1)
                        inst = v.tensor_scalar(out=r_s, in0=r_s, scalar1=127.0,
                                               scalar2=None, op0=ALU.mult)
                        inst.then_inc(s_dve.h, 1)
                    elif kind == "bcopy":
                        v.wait_ge(s_pe.h, V[("bbc", 0)])
                        inst = v.tensor_copy(out=scf[:, 0:T], in_=pb[7][:, 0:T])
                        inst.then_inc(s_dve.h, 1)
                    elif kind == "wxq":
                        _, k = key
                        v.wait_ge(s_gps.h, V[("wmul", (0, k))])
                        inst = v.tensor_scalar(
                            out=H2S[k % 4][:, 0:WC],
                            in0=scf[:, 512 + (k % 2) * 512:
                                    512 + (k % 2) * 512 + WC],
                            scalar1=MAGIC, scalar2=MAGIC,
                            op0=ALU.add, op1=ALU.subtract)
                        inst.then_inc(s_dve.h, 1)

            PE(pe_emit)
            SC(act_emit)
            VE(dve_emit)
            for e, sem in (("pe", s_pe), ("act", s_act), ("dve", s_dve),
                           ("gps", s_gps)):
                tot = base[e] + sum(n for _, _, n in full[e])
                sem.v = tot

            if w is not None:
                def f_sy(sy):
                    for m in range(MT):
                        if m >= 4:
                            sy.wait_ge(s_pe.h, V[("w1", m - 4)])
                        s_spx[m % 4].inc(sy.dma_start(
                            out=W1S[m % 4][:, 0:HK * 128].rearrange(
                                "p (k c2) -> p k c2", c2=128),
                            in_=wq1_h[m, :, :].rearrange("(k p) c2 -> p k c2",
                                                         p=128)))
                    for quad in range(2):
                        for k in range(MT):
                            idx = quad * MT + k
                            sl = idx % 4
                            if idx >= 4:
                                pidx = idx - 4
                                sy.wait_ge(s_pe.h,
                                           V[("w2", (pidx // MT, pidx % MT))])
                            else:
                                sy.wait_ge(s_pe.h, V[("w1", MT - 1)])
                            s_spx[sl].inc(sy.dma_start(
                                out=wpC4[sl],
                                in_=wq2_h[k, :, quad * 512:(quad + 1) * 512]))
                SY(f_sy)

                def f_gp(g2):
                    for m in range(MT):
                        g2.wait_ge(s_dve.h, V[("wmx", m)])
                        if m >= 2:
                            g2.wait_ge(s_gp[m % 2].h, spill_done[m - 2][1])
                        s_gp[m % 2].inc(g2.dma_start(
                            out=h2sp_h[m, :, 0:WC],
                            in_=h2A[m % 16][:, 0:WC],
                            allow_slow_non_contiguous=(WC < 8)))
                    for k in range(MT):
                        p = k % 4
                        g2.wait_ge(s_gp[k % 2].h, spill_done[k][1])
                        if k >= 4:
                            g2.wait_ge(s_pe.h, V[("w2", (0, k - 4))])
                            g2.wait_ge(s_gpx[p].h, splq_done[k - 4][1])
                        s_gpx[p].inc(g2.dma_start(
                            out=H2S[p][:, 0:WC], in_=h2sp_h[k, :, 0:WC],
                            allow_slow_non_contiguous=(WC < 8)))
                        if k >= 1:
                            pp = (k - 1) % 4
                            g2.wait_ge(s_dve.h, V[("wxq", (0, k - 1))])
                            s_gpx[pp].inc(g2.dma_start(
                                out=h2sp_h[k - 1, :, 0:WC],
                                in_=H2S[pp][:, 0:WC],
                                allow_slow_non_contiguous=(WC < 8)))
                        g2.wait_ge(s_gpx[p].h, ldh2_done[k][1])
                        g2.wait_ge(s_dve.h, V[("bcopy", 0)])
                        if k >= 2:
                            g2.wait_ge(s_dve.h, V[("wxq", (0, k - 2))])
                        inst = g2.tensor_tensor(
                            out=scf[:, 512 + (k % 2) * 512:
                                    512 + (k % 2) * 512 + WC],
                            in0=H2S[p][:, 0:WC], in1=scf[:, 0:WC],
                            op=ALU.mult)
                        inst.then_inc(s_gps.h, 1)
                    g2.wait_ge(s_dve.h, V[("wxq", (0, MT - 1))])
                    s_gpx[(MT - 1) % 4].inc(g2.dma_start(
                        out=h2sp_h[MT - 1, :, 0:WC],
                        in_=H2S[(MT - 1) % 4][:, 0:WC],
                        allow_slow_non_contiguous=(WC < 8)))
                    for k in range(MT):
                        p = k % 4
                        if k >= 4:
                            g2.wait_ge(s_pe.h, V[("w2", (1, k - 4))])
                        else:
                            g2.wait_ge(s_gpx[p].h, splq_done[28 + p][1])
                            g2.wait_ge(s_pe.h, V[("w2", (0, 28 + p))])
                        s_gpx[p].inc(g2.dma_start(
                            out=H2S[p][:, 0:WC], in_=h2sp_h[k, :, 0:WC],
                            allow_slow_non_contiguous=(WC < 8)))
                GP(f_gp)

        def window_tail(w, n_chunks, bar=True):
            cheap = (w == n_chunks)
            if debug and w == 0:
                dbg_tap("d_of0", ofb[:, 0:HK * T], 128, HK * T, BF16)
                hh = dbg_out("d_h2w0", [MT, 128, T], BF16)
                barrier()

                def f2t(sy):
                    s_spd.inc(sy.dma_start(out=hh, in_=h2sp_h))
                SY(f2t)
                barrier()
            ln_quant_stage(ofb[:, 0:HK * T], True, 'out')
            dve_ln = s_dve.v
            gpd0x = s_gpd.v
            dve0x = s_dve.v
            gp_base = [s_gp[0].v, s_gp[1].v]
            if w == 0:
                src0, dst0, width = 0, 1, T - 1
            elif cheap:
                src0, dst0, width = w * T - 1, 0, 1
            else:
                src0, dst0, width = w * T - 1, 0, T

            def f_gp3(g2):
                g2.wait_ge(s_dve.h, dve_ln)
                for k in range(HK):
                    if k >= 1:
                        g2.wait_ge(s_dve.h, dve0x + k)
                    s_gp[k % 2].inc(g2.dma_start(
                        out=(sp_a if k % 2 == 0 else sp_r[0])[:, dst0:dst0 + width],
                        in_=xsp_h[k * 128:(k + 1) * 128, src0:src0 + width],
                        allow_slow_non_contiguous=(width < 8)))
            GP(f_gp3)

            def f_ve10(v):
                for k in range(HK):
                    v.wait_ge(s_gp[k % 2].h, gp_base[k % 2] + 16 * (k // 2 + 1))
                    s_dve.inc(v.tensor_tensor(
                        out=xpr[:, k * T:(k + 1) * T],
                        in0=scf[:, k * T:(k + 1) * T],
                        in1=sp_a if k % 2 == 0 else sp_r[0], op=ALU.add))
            VE(f_ve10)

            def f_gp4(g2):
                for k in range(HK):
                    g2.wait_ge(s_dve.h, dve0x + k + 1)
                    s_gpd.inc(g2.dma_start(
                        out=out_h[k * 128:(k + 1) * 128, src0:src0 + width],
                        in_=xpr[:, k * T + dst0:k * T + dst0 + width],
                        allow_slow_non_contiguous=(width < 8)))
            GP(f_gp4)
            if bar:
                barrier()

        def pbc(bank, k):
            return pb[bank][:, 0:T].unsqueeze(1).broadcast_to([128, k, T])

        def ln_quant_stage(src_ap, src_is_bf, mode, c=None):
            pe0, act0, dve0 = s_pe.v, s_act.v, s_dve.v
            n_k = HK
            D = n_k * 128
            has_cast = not src_is_bf
            dve_cast = dve0 + 1 if has_cast else dve0
            dve_stat = dve_cast + 1
            act_sq = lambda k: act0 + k + 1
            act_rstd = act0 + n_k + 1
            pe_sum = lambda k: pe0 + k + 1
            pe_done_sums = pe0 + n_k

            def f_sc(s):
                for k in range(n_k):
                    if k >= 2:
                        s.wait_ge(s_pe.h, pe_sum(k - 2))
                    if has_cast and k == 0:
                        s.wait_ge(s_dve.h, dve_cast)
                    srck = (src_ap[:, k * T:(k + 1) * T] if src_is_bf
                            else nb1[:, k * T:(k + 1) * T])
                    s_act.inc(s.activation(out=h2t[k % 2][:, 0:T], in_=srck,
                                           func=AF.Square))
                s.wait_ge(s_dve.h, dve_stat)
                s.activation(out=r_t2, in_=r_t1, func=AF.Ln)
                s_act.inc(s.activation(out=r_r, in_=r_t2, func=AF.Exp,
                                       scale=-0.5))
            SC(f_sc)
            assert s_act.v == act_rstd

            if has_cast:
                def f_vc(v):
                    s_dve.inc(v.tensor_copy(out=nb1[:, 0:n_k * T], in_=src_ap))
                VE(f_vc)

            def f_pe(t):
                for k in range(n_k):
                    t.wait_ge(s_act.h, act_sq(k))
                    o = onesh_a if k % 2 == 0 else onesh_b
                    rhs = (src_ap[:, k * T:(k + 1) * T] if src_is_bf
                           else nb1[:, k * T:(k + 1) * T])
                    t.matmul(pb[6][0:1, 0:T], o, rhs,
                             start=(k == 0), stop=(k == n_k - 1))
                    o2 = onesh_b if k % 2 == 0 else onesh_a
                    inst = t.matmul(pb[7][0:1, 0:T], o2, h2t[k % 2][:, 0:T],
                                    start=(k == 0), stop=(k == n_k - 1))
                    s_pe.inc(inst)
                t.wait_ge(s_dve.h, dve_stat)
                s_pe.inc(t.matmul(pb[4][:, 0:T], ones1, r_m,
                                  start=True, stop=True))
                t.wait_ge(s_act.h, act_rstd)
                s_pe.inc(t.matmul(pb[5][:, 0:T], ones1, r_r,
                                  start=True, stop=True))
            PE(f_pe)
            pe_bc_m = pe_done_sums + 1
            pe_bc_r = pe_done_sums + 2

            def f_ve(v):
                if has_cast:
                    pass
                v.wait_ge(s_pe.h, pe_done_sums)
                v.tensor_scalar(out=r_m, in0=pb[6][0:1, 0:T], scalar1=1.0 / D,
                                scalar2=None, op0=ALU.mult)
                v.tensor_scalar(out=r_t1, in0=pb[7][0:1, 0:T], scalar1=1.0 / D,
                                scalar2=None, op0=ALU.mult)
                v.tensor_tensor(out=r_t2, in0=r_m, in1=r_m, op=ALU.mult)
                v.tensor_tensor(out=r_t1, in0=r_t1, in1=r_t2, op=ALU.subtract)
                s_dve.inc(v.tensor_scalar(out=r_t1, in0=r_t1, scalar1=0.0,
                                          scalar2=EPS, op0=ALU.max, op1=ALU.add))
                assert s_dve.v == dve_stat
                if mode == 'xn1':
                    v.wait_ge(s_pe.h, pe_bc_r)
                    v.tensor_tensor(out=t3(scf[:, 0:HK * T], HK),
                                    in0=t3(src_ap, HK), in1=pbc(4, HK),
                                    op=ALU.subtract)
                    s_dve.inc(v.tensor_tensor(out=t3(nb1[:, 0:HK * T], HK),
                                              in0=t3(scf[:, 0:HK * T], HK),
                                              in1=pbc(5, HK), op=ALU.mult))
                elif mode == 'resid':
                    v.wait_ge(s_pe.h, pe_bc_r)
                    v.tensor_tensor(out=t3(scf[:, 0:HK * T], HK),
                                    in0=t3(src_ap, HK), in1=pbc(4, HK),
                                    op=ALU.subtract)
                    v.tensor_tensor(out=t3(scf[:, 0:HK * T], HK),
                                    in0=t3(scf[:, 0:HK * T], HK),
                                    in1=pbc(5, HK), op=ALU.mult)
                    s_dve.inc(v.tensor_tensor(out=xpr[:, 0:HK * T],
                                              in0=scf[:, 0:HK * T],
                                              in1=xf[:, 0:HK * T], op=ALU.add))
                elif mode == 'out':
                    v.wait_ge(s_pe.h, pe_bc_r)
                    v.tensor_tensor(out=t3(scf[:, 0:HK * T], HK),
                                    in0=t3(src_ap, HK), in1=pbc(4, HK),
                                    op=ALU.subtract)
                    s_dve.inc(v.tensor_tensor(out=t3(scf[:, 0:HK * T], HK),
                                              in0=t3(scf[:, 0:HK * T], HK),
                                              in1=pbc(5, HK), op=ALU.mult))
            VE(f_ve)

            if mode == 'quant':
                quant1_tail(src_ap, c)

        def quant1_tail(src_ap, c):
            pe0, act0, dve0 = s_pe.v, s_act.v, s_dve.v
            pe_ready = s_pe.v

            def f_ve(v):
                v.wait_ge(s_pe.h, pe_ready)
                for k in range(HK):
                    v.tensor_tensor(out=h2t[0][:, 0:T],
                                    in0=src_ap[:, k * T:(k + 1) * T],
                                    in1=pb[4][:, 0:T], op=ALU.subtract)
                    v.tensor_tensor(out=h2t[1][:, 0:T], in0=h2t[0][:, 0:T],
                                    in1=h2t[0][:, 0:T], op=ALU.mult)
                    if k == 0:
                        v.tensor_copy(out=mx2, in_=h2t[1][:, 0:T])
                    else:
                        v.tensor_tensor(out=mx2, in0=mx2, in1=h2t[1][:, 0:T],
                                        op=ALU.max)
                s_dve.inc(v.engine_nop())
            VE(f_ve)
            dve_tree = dve0 + 1
            colmax_to_row(0, dve_tree)
            act_mx = s_act.v
            dve_stat2 = s_dve.v + 1

            def f_sc2(s):
                s.activation(out=r_t1, in_=r_mx, func=AF.Ln)
                s_act.inc(s.activation(out=r_mx, in_=r_t1, func=AF.Exp,
                                       scale=0.5))
            SC(f_sc2)
            act_sqrt = s_act.v

            def f_ve3(v):
                v.wait_ge(s_act.h, act_sqrt)
                v.tensor_tensor(out=r_t1, in0=r_r, in1=r_mx, op=ALU.mult)
                v.tensor_scalar(out=r_t1, in0=r_t1, scalar1=1e-5, scalar2=None,
                                op0=ALU.max)
                v.reciprocal(out=r_s, in_=r_t1)
                v.tensor_scalar(out=r_s, in0=r_s, scalar1=127.0, scalar2=None,
                                op0=ALU.mult)
                v.tensor_tensor(out=r_t2, in0=r_r, in1=r_s, op=ALU.mult)
                v.tensor_tensor(out=r_mx, in0=r_m, in1=r_t2, op=ALU.mult)
                v.tensor_scalar(out=r_mx, in0=r_mx, scalar1=-1.0, scalar2=None,
                                op0=ALU.mult)
                s_dve.inc(v.reciprocal(out=r_is, in_=r_s))
            VE(f_ve3)
            dve_sc = s_dve.v

            def f_pe4(t):
                t.wait_ge(s_dve.h, dve_sc)
                t.matmul(pb[4][:, 0:T], ones1, r_t2, start=True, stop=True)
                t.matmul(pb[5][:, 0:T], ones1, r_mx, start=True, stop=True)
                s_pe.inc(t.matmul(pb[6][:, 0:T], ones1, r_is,
                                  start=True, stop=True))
            PE(f_pe4)
            pe_bc = s_pe.v

            def f_ve4(v):
                v.wait_ge(s_pe.h, pe_bc)
                v.tensor_tensor(out=t3(scf[:, 0:HK * T], HK),
                                in0=t3(src_ap, HK), in1=pbc(4, HK), op=ALU.mult)
                v.tensor_tensor(out=t3(scf[:, 0:HK * T], HK),
                                in0=t3(scf[:, 0:HK * T], HK), in1=pbc(5, HK),
                                op=ALU.add)
                v.tensor_scalar(out=scf[:, 0:HK * T], in0=scf[:, 0:HK * T],
                                scalar1=MAGIC, scalar2=MAGIC,
                                op0=ALU.add, op1=ALU.subtract)
                xq_view = xq1[c % 2].rearrange("p (k t) -> p k t", k=HK)[:, :, 2:]
                v.tensor_tensor(out=xq_view, in0=t3(scf[:, 0:HK * T], HK),
                                in1=pbc(6, HK), op=ALU.mult)
                if c > 0:
                    prev = xq1[(c - 1) % 2]
                    for k in range(HK):
                        inst = v.tensor_copy(
                            out=xq1[c % 2][:, k * (T + 2):k * (T + 2) + 2],
                            in_=prev[:, k * (T + 2) + 512:k * (T + 2) + 514])
                s_dve.inc(v.engine_nop())
            VE(f_ve4)

        def colmax_to_row(mxq_off, dve_ready):
            pe0 = s_pe.v

            def f_pe(t):
                t.wait_ge(s_dve.h, dve_ready)
                for a in range(4):
                    inst = t.transpose(
                        pb[7].bitcast(BF16)[:, a * 128:(a + 1) * 128],
                        mx2[:, a * 128:(a + 1) * 128], idnt[0])
                s_pe.inc(inst)
            PE(f_pe)

            def f_ve(v):
                v.wait_ge(s_pe.h, pe0 + 1)
                s_dve.inc(v.tensor_reduce(
                    out=mxq[:, mxq_off:mxq_off + 4],
                    in_=pb[7].bitcast(BF16)[:, 0:512].rearrange(
                        "p (a q) -> p a q", a=4),
                    axis=mybir.AxisListType.X, op=ALU.max))
            VE(f_ve)
            dve_red = s_dve.v

            def f_pe2(t):
                t.wait_ge(s_dve.h, dve_red)
                for a in range(4):
                    inst = t.transpose(pb[6][0:1, a * 128:(a + 1) * 128],
                                       mxq[:, mxq_off + a:mxq_off + a + 1],
                                       maskd)
                s_pe.inc(inst)
            PE(f_pe2)
            pe_tr2 = s_pe.v

            def f_sc(s):
                s.wait_ge(s_pe.h, pe_tr2)
                for a in range(4):
                    inst = s.activation(out=r_mx[:, a * 128:(a + 1) * 128],
                                        in_=pb[6][0:1, a * 128:(a + 1) * 128],
                                        func=AF.Copy)
                s_act.inc(inst)
            SC(f_sc)

        for c in range(n_chunks):
            def f_gpx(g2, c=c):
                for k in range(HK):
                    s_gpd.inc(g2.dma_start(out=xf[:, k * T:(k + 1) * T],
                                           in_=x_h[k * 128:(k + 1) * 128,
                                                   c * T:(c + 1) * T]))
            GP(f_gpx)
            barrier()

            ln_quant_stage(xf[:, 0:HK * T], True, 'xn1')
            barrier()
            if c == 0:
                dbg_tap("d_xn1", nb1[:, 0:HK * T], 128, HK * T, BF16)

            pe0, act0, dve0, spd0 = s_pe.v, s_act.v, s_dve.v, s_spd.v
            mm_done = {}
            conv_done = {}
            pe_order = []
            for m in range(WMT):
                pe_order.append(("mm", m))
                if m >= 2 and m - 2 < GT:
                    pe_order.append(("conv", m - 2))
            vv = pe0
            for kind, m in pe_order:
                vv += 1
                (mm_done if kind == "mm" else conv_done)[m] = vv
            evac_done = {}
            silu_done = {}
            act_order = []
            for m in range(WMT):
                act_order.append(("evac", m))
                if m >= 1 and m - 1 < GT:
                    act_order.append(("silu", m - 1))
            vv = act0
            for kind, m in act_order:
                vv += 1
                (evac_done if kind == "evac" else silu_done)[m] = vv
            diag_done = {m: dve0 + m + 1 for m in range(GT)}

            sp_base = [s_sp[0].v, s_sp[1].v]

            def f_sy(sy, c=c):
                for m in range(WMT):
                    if m >= 2:
                        sy.wait_ge(s_pe.h, mm_done[m - 2])
                    s_sp[m % 2].inc(sy.dma_start(
                        out=wpA[m % 2][:, 0:HK * 128].rearrange(
                            "p (k c2) -> p k c2", c2=128),
                        in_=wbf_h[m, :, :].rearrange("(k p) c2 -> p k c2",
                                                     p=128)))
            SY(f_sy)

            def f_pe(t, c=c):
                for kind, m in pe_order:
                    if kind == "mm":
                        t.wait_ge(s_sp[m % 2].h, sp_base[m % 2] + 16 * (m // 2 + 1))
                        if m >= 2:
                            t.wait_ge(s_act.h, evac_done[m - 2])
                        bank = pb[m % 2]
                        for k in range(HK):
                            inst = t.matmul(bank[:, 0:T],
                                            wpA[m % 2][:, k * 128:(k + 1) * 128],
                                            nb1[:, k * T:(k + 1) * T],
                                            start=(k == 0), stop=(k == HK - 1))
                        s_pe.inc(inst)
                    else:
                        t.wait_ge(s_dve.h, diag_done[m])
                        if m >= 2:
                            t.wait_ge(s_act.h, silu_done[m - 2])
                        bank = pb[2 + m % 2]
                        for tap in range(KM):
                            inst = t.matmul(bank[:, 0:T],
                                            diag[(m % 2) * KM + tap],
                                            hp[m % 2][:, tap:tap + T],
                                            start=(tap == 0),
                                            stop=(tap == KM - 1))
                        s_pe.inc(inst)
            PE(f_pe)

            def f_sc(s, c=c):
                for kind, m in act_order:
                    if kind == "evac":
                        s.wait_ge(s_pe.h, mm_done[m])
                        if m < GT:
                            if m >= 2:
                                s.wait_ge(s_pe.h, conv_done[m - 2])
                                s.wait_ge(s_dve.h, diag_done[m - 2])
                            s_act.inc(s.activation(
                                out=hp[m % 2][:, KM - 1:KM - 1 + T],
                                in_=pb[m % 2][:, 0:T], func=AF.Copy))
                        else:
                            g = m - GT
                            s_act.inc(s.activation(
                                out=resb[:, g * T:(g + 1) * T],
                                in_=pb[m % 2][:, 0:T], func=AF.Silu))
                    else:
                        s.wait_ge(s_pe.h, conv_done[m])
                        s_act.inc(s.activation(out=hsb[:, m * T:(m + 1) * T],
                                               in_=pb[2 + m % 2][:, 0:T],
                                               func=AF.Silu))
            SC(f_sc)

            def f_ve(v, c=c):
                for m in range(GT):
                    v.wait_ge(s_act.h, evac_done[m])
                    if m >= 2:
                        v.wait_ge(s_pe.h, conv_done[m - 2])
                    if c == 0:
                        v.memset(hp[m % 2][:, 0:KM - 1], 0.0)
                    else:
                        v.tensor_copy(out=hp[m % 2][:, 0:KM - 1],
                                      in_=mt3[:, m * (KM - 1):(m + 1) * (KM - 1)])
                    v.tensor_copy(out=mt3[:, m * (KM - 1):(m + 1) * (KM - 1)],
                                  in_=hp[m % 2][:, T:T + KM - 1])
                    for tap in range(KM):
                        inst = v.tensor_scalar(
                            out=diag[(m % 2) * KM + tap], in0=maskd,
                            scalar1=cw1_sb[:, m * KM + tap:m * KM + tap + 1],
                            scalar2=None, op0=ALU.mult)
                    s_dve.inc(inst)
            VE(f_ve)
            barrier()
            if c == 0:
                dbg_tap("d_hs", hsb[:, 0:GT * T], 128, GT * T, BF16)
                dbg_tap("d_res", resb[:, 0:GT * T], 128, GT * T, BF16)

            pe0 = s_pe.v

            def f_pe(t):
                for k in range(GT):
                    inst = t.matmul(pb[4][0:96, 0:T], wxq[:, k * 96:(k + 1) * 96],
                                    hsb[:, k * T:(k + 1) * T],
                                    start=(k == 0), stop=(k == GT - 1))
                s_pe.inc(inst)
            PE(f_pe)

            def f_ve(v):
                v.wait_ge(s_pe.h, pe0 + 1)
                s_dve.inc(v.tensor_copy(out=xdbq[0:96, 0:T],
                                        in_=pb[4][0:96, 0:T]))
            VE(f_ve)
            dve_xdb = s_dve.v

            def f_gp(g2):
                g2.wait_ge(s_dve.h, dve_xdb)
                s_gpd.inc(g2.dma_start(out=bcr_h, in_=xdbq[64:96, 0:T]))
                g2.wait_ge(s_gpd.h, s_gpd.v)
                for n in range(NST):
                    s_gpd.inc(g2.dma_start(
                        out=Brep[:, n * T:(n + 1) * T],
                        in_=bcr_h[n:n + 1, :].broadcast_to([128, T])))
                    s_gpd.inc(g2.dma_start(
                        out=Crep[:, n * T:(n + 1) * T],
                        in_=bcr_h[NST + n:NST + n + 1, :].broadcast_to([128, T])))
            GP(f_gp)
            barrier()

            ddbg = None
            if debug and c == 0:
                ddbg = dbg_out("d_delta", [128, GT * T], FP16)
                dbg_tap("d_Brep", Brep[:, 0:NST * T], 128, NST * T, BF16)


            merged_stage(c, (c - 1) if c > 0 else None, n_chunks)
            barrier()
            if c > 0:
                window_tail(c - 1, n_chunks, bar=False)
            if c == 0 and debug:
                dbg_tap("d_y", yb[:, 0:GT * T], 128, GT * T, BF16)

            pe0, act0, spd0 = s_pe.v, s_act.v, s_spd.v
            mm_done = {m: pe0 + m + 1 for m in range(HK)}
            evac_done = {m: act0 + m + 1 for m in range(HK)}
            sp_base = [s_sp[0].v, s_sp[1].v]
            spd_done = {}
            for m in range(HK):
                if m % 2 == 0:
                    spd_done[m] = (0, sp_base[0] + 16 * (m // 2 + 1))
                else:
                    spd_done[m] = (1, sp_base[1] + 32 * (m // 2 + 1))

            def lhs_wout(m, k):
                if m % 2 == 0:
                    return wpB[0][:, k * 128:(k + 1) * 128]
                return wpA[k // HK][:, (k % HK) * 128:(k % HK + 1) * 128]

            def f_sy(sy):
                for m in range(HK):
                    if m >= 2:
                        sy.wait_ge(s_pe.h, mm_done[m - 2])
                    if m % 2 == 0:
                        s_sp[0].inc(sy.dma_start(
                            out=wpB[0][:, 0:GT * 128].rearrange(
                                "p (k c2) -> p k c2", c2=128),
                            in_=wob_h[m, :, :].rearrange("(k p) c2 -> p k c2",
                                                         p=128)))
                    else:
                        for half in range(2):
                            s_sp[1].inc(sy.dma_start(
                                out=wpA[half][:, 0:HK * 128].rearrange(
                                    "p (k c2) -> p k c2", c2=128),
                                in_=wob_h[m, half * HK * 128:(half + 1) * HK * 128,
                                          :].rearrange("(k p) c2 -> p k c2",
                                                       p=128)))
            SY(f_sy)

            def f_pe(t):
                for m in range(HK):
                    sl, vv2 = spd_done[m]
                    t.wait_ge(s_sp[sl].h, vv2)
                    if m >= 2:
                        t.wait_ge(s_act.h, evac_done[m - 2])
                    bank = pb[m % 2]
                    for k in range(GT):
                        inst = t.matmul(bank[:, 0:T], lhs_wout(m, k),
                                        yb[:, k * T:(k + 1) * T],
                                        start=(k == 0), stop=(k == GT - 1))
                    s_pe.inc(inst)
            PE(f_pe)

            def f_sc(s):
                for m in range(HK):
                    s.wait_ge(s_pe.h, mm_done[m])
                    s_act.inc(s.activation(out=Brep[:, m * T:(m + 1) * T],
                                           in_=pb[m % 2][:, 0:T], func=AF.Copy))
            SC(f_sc)
            barrier()

            ln_quant_stage(Brep[:, 0:HK * T], True, 'resid')
            barrier()
            if c == 0:
                dbg_tap("d_xp", xpr[:, 0:HK * T], 128, HK * T, F32)

            def f_gp(g2, c=c):
                for k in range(HK):
                    s_gpd.inc(g2.dma_start(
                        out=xsp_h[k * 128:(k + 1) * 128, c * T:(c + 1) * T],
                        in_=xpr[:, k * T:(k + 1) * T]))
            GP(f_gp)

            ln_quant_stage(xpr[:, 0:HK * T], False, 'quant', c=c)
            barrier()
            if c == 0:
                dbg_tap("d_xq1",
                        xq1[0].rearrange("p (k t) -> p k t", k=HK)[:, :, 2:],
                        128, HK * T, BF16)


        merged_stage(None, n_chunks - 1, n_chunks)
        barrier()
        window_tail(n_chunks - 1, n_chunks)
        merged_stage(None, n_chunks, n_chunks)
        barrier()
        window_tail(n_chunks, n_chunks)


        def fin(sy):
            sy.wait_ge(s_gpd.h, s_gpd.v)
            sy.wait_ge(s_spd.h, s_spd.v)
        SY(fin)

    return nc, dbg_handles



def prep_inputs(inputs, core):
    x = inputs["L0_attn_hidden_states"]
    d = {}
    d["x"] = np.ascontiguousarray(np.asarray(x)[core].T)
    d["winT"] = np.ascontiguousarray(np.asarray(inputs["L0_attn_w_in"]).T)
    d["wxT"] = np.ascontiguousarray(np.asarray(inputs["L0_attn_w_xproj"]).T)
    d["wdtT"] = np.ascontiguousarray(np.asarray(inputs["L0_attn_w_dt"]).T)
    d["woutT"] = np.ascontiguousarray(np.asarray(inputs["L0_attn_w_out"]).T)
    d["w1T"] = np.ascontiguousarray(np.asarray(inputs["L0_ffn_w1"]).T)
    d["w2T"] = np.ascontiguousarray(np.asarray(inputs["L0_ffn_w2"]).T)
    d["bdt"] = np.ascontiguousarray(
        np.asarray(inputs["L0_attn_b_dt"]).reshape(GT, 128).T)
    d["alog"] = np.ascontiguousarray(
        np.asarray(inputs["L0_attn_A_log"]).reshape(GT, 128, NST)
        .transpose(1, 0, 2).reshape(128, GT * NST))
    d["dvec"] = np.ascontiguousarray(
        np.asarray(inputs["L0_attn_D"]).reshape(GT, 128).T)
    d["cw1"] = np.ascontiguousarray(
        np.asarray(inputs["L0_attn_w_conv"]).reshape(GT, 128, KM)
        .transpose(1, 0, 2).reshape(128, GT * KM))
    d["cw2"] = np.ascontiguousarray(
        np.asarray(inputs["L0_ffn_conv_w"]).reshape(MT, 128, KF)
        .transpose(1, 0, 2).reshape(128, MT * KF))
    d["cb2"] = np.ascontiguousarray(
        np.asarray(inputs["L0_ffn_conv_b"]).reshape(MT, 128).T)
    return {k: np.ascontiguousarray(np.asarray(v, dtype=np.float32))
            for k, v in d.items()}


def kernel(**inputs):
    nc, _ = build(debug=False)
    in_maps = [prep_inputs(inputs, core) for core in range(8)]
    res = run_bass_kernel_spmd(nc, in_maps, core_ids=list(range(8)))
    outs = [np.asarray(r["out"], np.float32).T for r in res.results]
    return np.stack(outs, axis=0)


# version 15
# speedup vs baseline: 1.0554x; 1.0084x over previous
import sys
sys.path.insert(0, "/opt/trn_rl_repo")
from contextlib import ExitStack

import numpy as np
import ml_dtypes
import concourse.bass as bass
import concourse.mybir as mybir
from concourse.bass_utils import run_bass_kernel_spmd

F32 = mybir.dt.float32
BF16 = mybir.dt.bfloat16
FP16 = mybir.dt.float16
AF = mybir.ActivationFunctionType
ALU = mybir.AluOpType

H = 1024
DIN = 2048
DH = 4096
NST = 16
RNK = 64
L = 4096
T = 512
NCH = L // T
HK = H // 128
GT = DIN // 128
MT = DH // 128
WMT = (2 * DIN) // 128
KM = 4
KF = 3
MAGIC = float(np.float32(1.5 * 2 ** 23))
EPS = 1e-5


class Sem:
    def __init__(self, stack, nc, name, step=1):
        self.h = stack.enter_context(nc.semaphore(name))
        self.v = 0
        self.step = step

    def inc(self, inst, k=None):
        k = self.step if k is None else k
        inst.then_inc(self.h, k)
        self.v += k
        return self.v


def t3(ap, k):
    return ap.rearrange("p (k t) -> p k t", k=k)


def build(debug=False, n_chunks=NCH, stop_after=None):
    nc = bass.Bass("TRN2", target_bir_lowering=False, debug=False,
                   detect_race_conditions=False)

    x_h = nc.dram_tensor("x", [H, L], F32, kind="ExternalInput").ap()
    winT_h = nc.dram_tensor("winT", [H, 2 * DIN], F32, kind="ExternalInput").ap()
    wxT_h = nc.dram_tensor("wxT", [DIN, 96], F32, kind="ExternalInput").ap()
    wdtT_h = nc.dram_tensor("wdtT", [RNK, DIN], F32, kind="ExternalInput").ap()
    woutT_h = nc.dram_tensor("woutT", [DIN, H], F32, kind="ExternalInput").ap()
    w1T_h = nc.dram_tensor("w1T", [H, DH], F32, kind="ExternalInput").ap()
    w2T_h = nc.dram_tensor("w2T", [DH, H], F32, kind="ExternalInput").ap()
    bdt_h = nc.dram_tensor("bdt", [128, GT], F32, kind="ExternalInput").ap()
    alog_h = nc.dram_tensor("alog", [128, GT * NST], F32, kind="ExternalInput").ap()
    dvec_h = nc.dram_tensor("dvec", [128, GT], F32, kind="ExternalInput").ap()
    cw1_h = nc.dram_tensor("cw1", [128, GT * KM], F32, kind="ExternalInput").ap()
    cw2_h = nc.dram_tensor("cw2", [128, MT * KF], F32, kind="ExternalInput").ap()
    cb2_h = nc.dram_tensor("cb2", [128, MT], F32, kind="ExternalInput").ap()

    out_h = nc.dram_tensor("out", [H, L], F32, kind="ExternalOutput").ap()

    wbf_h = nc.dram_tensor("wbf", [WMT, H, 128], BF16, kind="Internal").ap()
    wob_h = nc.dram_tensor("wob", [HK, DIN, 128], BF16, kind="Internal").ap()
    wq1_h = nc.dram_tensor("wq1", [MT, H, 128], BF16, kind="Internal").ap()
    wq2_h = nc.dram_tensor("wq2", [MT, 128, H], BF16, kind="Internal").ap()
    bcr_h = nc.dram_tensor("bcr", [32, T], BF16, kind="Internal").ap()
    h2sp_h = nc.dram_tensor("h2sp", [MT, 128, T], BF16, kind="Internal").ap()
    xsp_h = nc.dram_tensor("xsp", [H, L], F32, kind="Internal").ap()

    dbg_handles = {}

    def dbg_out(name, shape, dtype):
        h = nc.dram_tensor(name, shape, dtype, kind="ExternalOutput").ap()
        dbg_handles[name] = h
        return h

    with ExitStack() as st:
        def sb(name, shape, dt=F32):
            return st.enter_context(nc.sbuf_tensor(name, shape, dt))[:]

        def psm(name, shape, dt=F32):
            return st.enter_context(nc.psum_tensor(name, shape, dt))[:]

        s_pe = Sem(st, nc, "s_pe")
        s_act = Sem(st, nc, "s_act")
        s_dve = Sem(st, nc, "s_dve")
        s_gps = Sem(st, nc, "s_gps")
        s_spd = Sem(st, nc, "s_spd", 16)
        s_gpd = Sem(st, nc, "s_gpd", 16)
        s_sp = [Sem(st, nc, "s_sp0", 16), Sem(st, nc, "s_sp1", 16)]
        s_gp = [Sem(st, nc, "s_gp0", 16), Sem(st, nc, "s_gp1", 16)]
        s_spx = s_sp + [Sem(st, nc, "s_sp2", 16), Sem(st, nc, "s_sp3", 16)]
        s_gpx = s_gp + [Sem(st, nc, "s_gp2", 16), Sem(st, nc, "s_gp3", 16)]

        xf = sb("xf", [128, HK * T], BF16)
        scf = sb("scf", [128, HK * T])
        nb1 = sb("nb1", [128, HK * T], BF16)
        hp = [sb(f"hp{i}", [128, T + KM - 1], BF16) for i in range(2)]
        mt3 = sb("mt3", [128, GT * (KM - 1)], BF16)
        hsb = sb("hsb", [128, GT * T], BF16)
        resb = sb("resb", [128, GT * T], BF16)
        dlg = [sb(f"dlg{i}", [128, T], FP16) for i in range(2)]
        dxg = sb("dxg", [128, T], BF16)
        xdbq = sb("xdbq", [96, T], BF16)
        Brep = sb("Brep", [128, NST * T], BF16)
        Crep = sb("Crep", [128, NST * T], BF16)
        dAq = [sb(f"dAq{i}", [128, 2 * T], BF16) for i in range(3)]
        dBq = [sb(f"dBq{i}", [128, 2 * T], BF16) for i in range(3)]
        NSL = 3
        zq = sb("zq", [128, 2 * T], BF16)
        carry = sb("carry", [128, GT * NST])
        yb = sb("yb", [128, GT * T], BF16)
        sp_a = sb("sp_a", [128, T])
        sp_r = [sb(f"sp_r{i}", [128, T]) for i in range(2)]
        sp_u = [sb(f"sp_u{i}", [128, T]) for i in range(2)]
        xpr = sb("xpr", [128, HK * T])
        xq1 = [sb(f"xq1_{i}", [128, HK * (T + 2)], BF16) for i in range(2)]
        mxsq = sb("mxsq", [128, T], BF16)
        zrot = sb("zrot", [128, 128], BF16)
        ftail = sb("ftail", [128, MT], BF16)
        h2t = [sb(f"h2t_{i}", [128, T], BF16) for i in range(2)]
        mx2 = sb("mx2", [128, T], BF16)
        ofb = nb1
        wpA = [sb(f"wpA{i}", [128, HK * 128], BF16) for i in range(2)]
        wpB = [sb("wpB0", [128, GT * 128], BF16)]
        wpC = [wpA[0][:, 0:H], wpA[1][:, 0:H]]
        wpC4 = [wpA[0][:, 0:512], wpA[0][:, 512:1024],
                wpA[1][:, 0:512], wpA[1][:, 512:1024]]
        W1S = [wpA[0], wpA[1],
               wpB[0][:, 0:HK * 128], wpB[0][:, HK * 128:2 * HK * 128]]
        H2S = [h2t[0], h2t[1],
               scf.bitcast(BF16)[:, 3584:4096], scf.bitcast(BF16)[:, 4096:4608]]
        dxgs = [dxg, scf.bitcast(BF16)[:, 3072:3584]]
        hp2 = [hp[0][:, 0:T + KF], hp[1][:, 0:T + KF],
               scf.bitcast(BF16)[:, 4608:4608 + T + KF],
               scf.bitcast(BF16)[:, 5124:5124 + T + KF]]
        h2A = [xpr.bitcast(BF16)[:, i * 512:(i + 1) * 512] for i in range(16)]
        ones_a = sb("ones_a", [128, 1])
        ones_b = sb("ones_b", [128, 1])
        onesh_a = sb("onesh_a", [128, 1], BF16)
        onesh_b = sb("onesh_b", [128, 1], BF16)
        ones1 = sb("ones1", [1, 128])
        idnt = [sb(f"idnt{i}", [128, 128], BF16) for i in range(2)]
        iof = sb("iof", [128, 128])
        iop = sb("iop", [128, 1])
        maskd = sb("maskd", [128, 128])
        A_sb = sb("A_sb", [128, GT * NST])
        D_sb = sb("D_sb", [128, GT])
        bdt_sb = sb("bdt_sb", [128, GT])
        cw1_sb = sb("cw1_sb", [128, GT * KM])
        cw2_sb = sb("cw2_sb", [128, MT * KF])
        cb2_sb = sb("cb2_sb", [128, MT])
        diag = [sb(f"diag{i}", [128, 128], BF16) for i in range(KM * 2)]
        dgw = diag + [scf.bitcast(BF16)[:, 5640 + i * 128:5768 + i * 128]
                      for i in range(10)]
        wxq = sb("wxq", [128, GT * 96], BF16)
        wdtq = sb("wdtq", [RNK, GT * 128], BF16)
        swrow = sb("swrow", [1, 256])
        swb = sb("swb", [128, 2])
        r_m = Crep.bitcast(F32)[0:1, 2048:2560]
        r_r = Crep.bitcast(F32)[0:1, 2560:3072]
        r_t2 = Crep.bitcast(F32)[0:1, 3072:3584]
        r_is = Crep.bitcast(F32)[0:1, 3584:4096]
        r_t1 = sb("r_t1", [1, T])
        r_mx = sb("r_mx", [1, T])
        r_s = sb("r_s", [1, T])
        mxq = sb("mxq", [128, 8])

        pb = [psm(f"pb{i}", [128, 512]) for i in range(8)]

        block = st.enter_context(nc.Block())

        def PE(f): block.tensor(f)
        def VE(f): block.vector(f)
        def SC(f): block.scalar(f)
        def GP(f): block.gpsimd(f)
        def SY(f): block.sync(f)

        def barrier():
            all_s = [(s_pe,), (s_act,), (s_dve,), (s_gps,), (s_spd,), (s_gpd,),
                     (s_sp[0],), (s_sp[1],), (s_gp[0],), (s_gp[1],),
                     (s_spx[2],), (s_spx[3],), (s_gpx[2],), (s_gpx[3],)]
            vals = [(s, s.v) for (s,) in all_s]

            def mk(skip):
                def f(eng):
                    for s, v in vals:
                        if s is not skip and v > 0:
                            eng.wait_ge(s.h, v)
                return f
            PE(mk(s_pe))
            VE(mk(s_dve))
            SC(mk(s_act))
            GP(mk(s_gps))
            SY(mk(None))

        def dbg_tap(name, ap_in, rows, cols, dtype):
            if not debug:
                return
            h = dbg_out(name, [rows, cols], dtype)
            barrier()

            def f2(sy):
                s_spd.inc(sy.dma_start(out=h, in_=ap_in))
            SY(f2)
            barrier()

        def setup_sync(sy):
            s_spd.inc(sy.dma_start(out=bdt_sb, in_=bdt_h))
            s_spd.inc(sy.dma_start(out=A_sb, in_=alog_h))
            s_spd.inc(sy.dma_start(out=D_sb, in_=dvec_h))
            s_spd.inc(sy.dma_start(out=cw1_sb, in_=cw1_h))
            s_spd.inc(sy.dma_start(out=cw2_sb, in_=cw2_h))
            s_spd.inc(sy.dma_start(out=cb2_sb, in_=cb2_h))
        SY(setup_sync)

        def setup_gps(g):
            g.iota(iof, [[1, 128]], channel_multiplier=0,
                   allow_small_or_imprecise_dtypes=True)
            g.iota(iop, [[0, 1]], channel_multiplier=1,
                   allow_small_or_imprecise_dtypes=True)
            g.memset(swrow, 1.0)
            g.memset(zrot, 0.0)
            g.memset(xq1[0], 0.0)
            g.memset(xq1[1], 0.0)
            g.memset(ones_a, 1.0)
            g.memset(ones_b, 1.0)
            g.memset(onesh_a, 1.0)
            g.memset(onesh_b, 1.0)
            s_gps.inc(g.memset(ones1, 1.0))
        GP(setup_gps)

        def setup_ve(v):
            v.wait_ge(s_gps.h, s_gps.v)
            v.tensor_scalar(out=maskd, in0=iof, scalar1=iop, scalar2=None,
                            op0=ALU.is_equal)
            v.tensor_copy(out=idnt[0], in_=maskd)
            s_dve.inc(v.tensor_copy(out=idnt[1], in_=maskd))
        VE(setup_ve)

        def setup_sc(s):
            s.wait_ge(s_spd.h, s_spd.v)
            s_act.inc(s.activation(out=A_sb, in_=A_sb, func=AF.Exp))
        SC(setup_sc)

        def setup_ve2(v):
            v.wait_ge(s_act.h, s_act.v)
            s_dve.inc(v.tensor_scalar(out=A_sb, in0=A_sb, scalar1=-1.0,
                                      scalar2=None, op0=ALU.mult))
        VE(setup_ve2)
        barrier()

        def res_w_sync(sy):
            for k in range(GT):
                s_spd.inc(sy.dma_start(out=scf[:, k * 96:(k + 1) * 96],
                                       in_=wxT_h[k * 128:(k + 1) * 128, :]))
            for k in range(GT):
                s_spd.inc(sy.dma_start(
                    out=scf[0:RNK, GT * 96 + k * 128:GT * 96 + (k + 1) * 128],
                    in_=wdtT_h[:, k * 128:(k + 1) * 128]))
        SY(res_w_sync)

        def res_w_ve(v):
            v.wait_ge(s_spd.h, s_spd.v)
            v.tensor_copy(out=wxq, in_=scf[:, 0:GT * 96])
            s_dve.inc(v.tensor_copy(
                out=wdtq, in_=scf[0:RNK, GT * 96:GT * 96 + GT * 128]))
        VE(res_w_ve)
        barrier()

        def conv_weight(src_h, rows, n_m, dst_h, quant=False):
            kb_n = rows // 128
            width = n_m * 128
            lbuf = [scf[:, 0:width], xpr[:, 0:width]]
            obuf = [nb1[:, 0:width], yb[:, 0:width]]
            spd0, dve0, gpd0 = s_spd.v, s_dve.v, s_gpd.v
            for kb in range(kb_n):
                def f_sy(sy, kb=kb):
                    if kb >= 2:
                        sy.wait_ge(s_dve.h, dve0 + kb - 1)
                    s_spd.inc(sy.dma_start(
                        out=lbuf[kb % 2],
                        in_=src_h[kb * 128:(kb + 1) * 128, 0:width]))
                SY(f_sy)

                def f_ve(v, kb=kb):
                    v.wait_ge(s_spd.h, spd0 + 16 * (kb + 1))
                    if kb >= 2:
                        v.wait_ge(s_gpd.h, gpd0 + 16 * (kb - 1))
                    src = lbuf[kb % 2]
                    if not quant:
                        s_dve.inc(v.tensor_copy(out=obuf[kb % 2], in_=src))
                    else:
                        v.tensor_scalar(out=src, in0=src,
                                        scalar1=swb[:, 0:1], scalar2=None,
                                        op0=ALU.mult)
                        v.tensor_scalar(out=src, in0=src,
                                        scalar1=MAGIC, scalar2=MAGIC,
                                        op0=ALU.add, op1=ALU.subtract)
                        v.tensor_scalar(out=src, in0=src,
                                        scalar1=1.0, scalar2=-1.0,
                                        op0=ALU.min, op1=ALU.max)
                        s_dve.inc(v.tensor_scalar(out=obuf[kb % 2], in0=src,
                                                  scalar1=swb[:, 1:2],
                                                  scalar2=None, op0=ALU.mult))
                VE(f_ve)

                def f_gp(g, kb=kb):
                    g.wait_ge(s_dve.h, dve0 + kb + 1)
                    dst = dst_h[:, kb * 128:(kb + 1) * 128, :]
                    dst = dst.rearrange("m p c -> p m c")
                    s_gpd.inc(g.dma_start(
                        out=dst,
                        in_=obuf[kb % 2].rearrange("p (m c) -> p m c", c=128)))
                GP(f_gp)
            barrier()

        def wsum_pass(src_h, rows, width, total):
            kb_n = rows // 128
            lbuf = [scf[:, 0:width], xpr[:, 0:width]]

            def z_ve(v):
                s_dve.inc(v.memset(carry[:, 0:1], 0.0))
            VE(z_ve)
            spd0, act0, dve0 = s_spd.v, s_act.v, s_dve.v
            for kb in range(kb_n):
                def f_sy(sy, kb=kb):
                    if kb >= 2:
                        sy.wait_ge(s_act.h, act0 + kb - 1)
                    s_spd.inc(sy.dma_start(
                        out=lbuf[kb % 2],
                        in_=src_h[kb * 128:(kb + 1) * 128, 0:width]))
                SY(f_sy)

                def f_sc(s, kb=kb):
                    s.wait_ge(s_spd.h, spd0 + 16 * (kb + 1))
                    if kb >= 2:
                        s.wait_ge(s_dve.h, dve0 + kb - 1)
                    s_act.inc(s.activation(
                        out=nb1[:, 0:width], in_=lbuf[kb % 2], func=AF.Abs,
                        accum_out=sp_a[:, kb % 2:kb % 2 + 1]))
                SC(f_sc)

                def f_ve(v, kb=kb):
                    v.wait_ge(s_act.h, act0 + kb + 1)
                    s_dve.inc(v.tensor_tensor(
                        out=carry[:, 0:1], in0=carry[:, 0:1],
                        in1=sp_a[:, kb % 2:kb % 2 + 1], op=ALU.add))
                VE(f_ve)

            def f_pe(t):
                t.wait_ge(s_dve.h, s_dve.v)
                s_pe.inc(t.matmul(pb[6][0:1, 0:1], carry[:, 0:1], ones_a,
                                  start=True, stop=True))
            PE(f_pe)

            def f_ve2(v):
                v.wait_ge(s_pe.h, s_pe.v)
                s_dve.inc(v.tensor_scalar(out=swrow[:, 0:1], in0=pb[6][0:1, 0:1],
                                          scalar1=1.0 / total, scalar2=1e-5,
                                          op0=ALU.mult, op1=ALU.max))
            VE(f_ve2)

            def f_sc_r(s):
                s.wait_ge(s_dve.h, s_dve.v)
                s.activation(out=swrow[:, 1:2], in_=swrow[:, 0:1], func=AF.Ln)
                s_act.inc(s.activation(out=swrow[:, 128:129], in_=swrow[:, 1:2],
                                       func=AF.Exp, scale=-1.0))
            SC(f_sc_r)

            def f_pe2(t):
                t.wait_ge(s_dve.h, s_dve.v)
                t.wait_ge(s_act.h, s_act.v)
                t.matmul(pb[6][:, 2:3], ones1, swrow[:, 128:129], start=True, stop=True)
                s_pe.inc(t.matmul(pb[6][:, 3:4], ones1, swrow[:, 0:1],
                                  start=True, stop=True))
            PE(f_pe2)

            def f_ve3(v):
                v.wait_ge(s_pe.h, s_pe.v)
                v.tensor_copy(out=swb[:, 0:1], in_=pb[6][:, 2:3])
                s_dve.inc(v.tensor_copy(out=swb[:, 1:2], in_=pb[6][:, 3:4]))
            VE(f_ve3)
            barrier()

        conv_weight(winT_h, H, WMT, wbf_h)
        conv_weight(woutT_h, DIN, HK, wob_h)
        if stop_after != "front":
            wsum_pass(w1T_h, H, DH, H * DH)
            dbg_tap("d_swb1", swb[:, 0:2], 128, 2, F32)
            conv_weight(w1T_h, H, MT, wq1_h, quant=True)
            wsum_pass(w2T_h, DH, H, DH * H)
            lbuf2 = [scf[:, 0:H], xpr[:, 0:H]]
            obuf2 = [nb1[:, 0:H], yb[:, 0:H]]
            spd0w, dve0w, gpd0w = s_spd.v, s_dve.v, s_gpd.v
            for kb in range(DH // 128):
                def f_sy(sy, kb=kb):
                    if kb >= 2:
                        sy.wait_ge(s_dve.h, dve0w + kb - 1)
                    s_spd.inc(sy.dma_start(out=lbuf2[kb % 2],
                                           in_=w2T_h[kb * 128:(kb + 1) * 128, :]))
                SY(f_sy)

                def f_ve(v, kb=kb):
                    v.wait_ge(s_spd.h, spd0w + 16 * (kb + 1))
                    if kb >= 2:
                        v.wait_ge(s_gpd.h, gpd0w + 16 * (kb - 1))
                    src = lbuf2[kb % 2]
                    v.tensor_scalar(out=src, in0=src,
                                    scalar1=swb[:, 0:1], scalar2=None,
                                    op0=ALU.mult)
                    v.tensor_scalar(out=src, in0=src,
                                    scalar1=MAGIC, scalar2=MAGIC,
                                    op0=ALU.add, op1=ALU.subtract)
                    v.tensor_scalar(out=src, in0=src,
                                    scalar1=1.0, scalar2=-1.0,
                                    op0=ALU.min, op1=ALU.max)
                    s_dve.inc(v.tensor_scalar(out=obuf2[kb % 2], in0=src,
                                              scalar1=swb[:, 1:2], scalar2=None,
                                              op0=ALU.mult))
                VE(f_ve)

                def f_gp(g, kb=kb):
                    g.wait_ge(s_dve.h, dve0w + kb + 1)
                    s_gpd.inc(g.dma_start(out=wq2_h[kb, :, :],
                                          in_=obuf2[kb % 2]))
                GP(f_gp)
            barrier()

        NPAIR = NST // 2

        def merged_stage(c, w, n_chunks):
            cheap = (w is not None and w == n_chunks)
            WC = 2 if cheap else T
            V = {}
            ops = {"pe": [], "act": [], "dve": [], "gps": []}

            win = {"pe": {}, "act": {}, "dve": {}, "gps": {}}
            if w is not None:
                def at(e, q, item):
                    win[e].setdefault(q, []).append(item)
                for m in range(MT):
                    qa = min(6, m // 5)
                    at("pe", qa, ("w1", m, 1))
                    at("act", qa, ("wev", m, 1))
                    at("dve", qa, ("wdiag", m, 1))
                    at("pe", qa, ("wconv", m, 1))
                    at("act", qa, ("wsilu", m, 1))
                    at("dve", min(7, qa + 1), ("wmx", m, 1))
                at("pe", 7, ("btr1", 0, 1))
                at("dve", 7, ("bred", 0, 1))
                at("pe", 7, ("btr2", 0, 1))
                at("act", 7, ("bcp", 0, 1))
                at("act", 7, ("bsqrt", 0, 1))
                at("dve", 7, ("bscale", 0, 1))
                at("pe", 7, ("bbc", 0, 1))
                at("dve", 7, ("bcopy", 0, 1))
                if c is None:
                    for k in range(MT):
                        qc = 8 + k // 8
                        at("gps", qc, ("wmul", (0, k), 1))
                        at("dve", qc, ("wxq", (0, k), 1))
                        at("pe", qc, ("w2", (0, k), 1))
                        at("pe", qc, ("w2", (1, k), 1))
                    at("act", 11, ("wofev", 0, 1))
                    at("act", 11, ("wofev", 1, 1))
                else:
                    for quad in range(2):
                        for k in range(MT):
                            idx = quad * MT + k
                            qc = 8 + idx // 8
                            if quad == 0:
                                at("gps", qc, ("wmul", (0, k), 1))
                                at("dve", qc, ("wxq", (0, k), 1))
                            at("pe", qc, ("w2", (quad, k), 1))
                        at("act", min(15, 8 + (quad * MT + MT - 1) // 8),
                           ("wofev", quad, 1))


            if c is not None:
                ops["pe"] += [("wdt", 0, 1), ("wdt", 1, 1)]
                ops["act"] += ([("sp", 0, 1)]
                               + [("dA", (0, p), 1) for p in range(NSL)])
                ops["dve"] += [("dlt", 0, 1), ("dx", 0, 1)]
                for g in range(GT):
                    scan_pe = ([("ymm", (g, p), 1) for p in range(NPAIR)]
                               + [("hsd", g, 1)])
                    if g + 2 < GT:
                        scan_pe.append(("wdt", g + 2, 1))
                    ops["pe"] += scan_pe + win["pe"].get(g - 1, [])
                    av = [("dA", (g, p), 1) for p in range(NSL, NPAIR)]
                    if g + 1 < GT:
                        av += ([("sp", g + 1, 1)]
                               + [("dA", (g + 1, p), 1) for p in range(NSL)])
                    av.append(("hsdc", g, 1))
                    ops["act"] += av + win["act"].get(g - 1, [])
                    dv = []
                    for p in range(NPAIR):
                        if p == 6 and g + 1 < GT:
                            dv += [("dlt", g + 1, 1), ("dx", g + 1, 1)]
                        if c == 0:
                            dv.append(("cz", (g, p), 1))
                        dv += [("dbq", (g, p), 1), ("scan", (g, p), 1),
                               ("pq", (g, p), 1)]
                    dv.append(("yev", g, 1))
                    ops["dve"] += dv + win["dve"].get(g - 1, [])
                    ops["gps"] += win["gps"].get(g - 1, [])
                for e in ("pe", "act", "dve", "gps"):
                    for q in sorted(win[e]):
                        if q >= GT - 1:
                            ops[e] += win[e][q]
            else:
                for e in ("pe", "act", "dve", "gps"):
                    for q in sorted(win[e]):
                        ops[e] += win[e][q]
            full = ops

            base = {"pe": s_pe.v, "act": s_act.v, "dve": s_dve.v,
                    "gps": s_gps.v}
            for e in ("pe", "act", "dve", "gps"):
                vv = base[e]
                for kind, key, n in full[e]:
                    vv += n
                    V[(kind, key)] = vv
            if w is not None:
                gp_base = [s_gp[0].v, s_gp[1].v]
                cnt4 = [s.v for s in s_spx]
                wp_done = {}
                for m in range(MT):
                    sl = m % 4
                    cnt4[sl] += 16
                    wp_done[m] = (sl, cnt4[sl])
                wld_done = {}
                if c is None:
                    for k in range(MT):
                        sl = k % 4
                        cnt4[sl] += 16
                        wld_done[(0, k)] = (sl, cnt4[sl])
                        wld_done[(1, k)] = (sl, cnt4[sl])
                else:
                    for quad in range(2):
                        for k in range(MT):
                            sl = (quad * MT + k) % 4
                            cnt4[sl] += 16
                            wld_done[(quad, k)] = (sl, cnt4[sl])
                cntg = [s.v for s in s_gpx]
                spill_done = {}
                for m in range(MT):
                    cntg[m % 2] += 16
                    spill_done[m] = (m % 2, cntg[m % 2])
                ldh2_done, splq_done, ldxq_done = {}, {}, {}
                for k in range(MT):
                    p = k % 4
                    cntg[p] += 16
                    ldh2_done[k] = (p, cntg[p])
                    if c is not None:
                        cntg[p] += 16
                        splq_done[k] = (p, cntg[p])
                if c is not None:
                    for k in range(MT):
                        p = k % 4
                        cntg[p] += 16
                        ldxq_done[k] = (p, cntg[p])
                xq_cur = xq1[w % 2] if not cheap else None
                xq_prv = xq1[(w - 1) % 2] if w > 0 else None
                if cheap:
                    xq_prv = xq1[(w - 1) % 2]
            TE = T + 2

            def pe_emit(t):
                for kind, key, n in full["pe"]:
                    if kind == "wdt":
                        g = key
                        if g >= 2:
                            t.wait_ge(s_act.h, V[("sp", g - 2)])
                        inst = t.matmul(pb[g % 2][:, 0:T],
                                        wdtq[0:RNK, g * 128:(g + 1) * 128],
                                        xdbq[0:RNK, 0:T], start=True, stop=True)
                        inst.then_inc(s_pe.h, 1)
                    elif kind == "ymm":
                        g, p = key
                        t.wait_ge(s_dve.h, V[("pq", (g, p))])
                        if p == 0 and g >= 2:
                            t.wait_ge(s_dve.h, V[("yev", g - 2)])
                        sl = (g * NPAIR + p) % NSL
                        bank = pb[4 + g % 2]
                        for j in range(2):
                            inst = t.matmul(bank[:, 0:T], idnt[j],
                                            dBq[sl][:, j * T:(j + 1) * T],
                                            start=(p == 0 and j == 0), stop=False)
                        inst.then_inc(s_pe.h, 1)
                    elif kind == "hsd":
                        g = key
                        t.wait_ge(s_act.h, V[("hsdc", g)])
                        inst = t.matmul(pb[4 + g % 2][:, 0:T], idnt[0],
                                        dlg[g % 2].bitcast(BF16)[:, 0:T],
                                        start=False, stop=True)
                        inst.then_inc(s_pe.h, 1)
                    elif kind == "w1":
                        m = key
                        sl, vv2 = wp_done[m]
                        t.wait_ge(s_spx[sl].h, vv2)
                        if c is None:
                            bmn, bmt = 2 + 2 * (m % 2), 3 + 2 * (m % 2)
                            if m >= 2:
                                t.wait_ge(s_act.h, V[("wev", m - 2)])
                        else:
                            bmn, bmt = 2, 3
                            if m >= 1:
                                t.wait_ge(s_act.h, V[("wev", m - 1)])
                        if not cheap:
                            for k in range(HK):
                                inst = t.matmul(
                                    pb[bmn][:, 0:512],
                                    W1S[sl][:, k * 128:(k + 1) * 128],
                                    xq_cur[:, k * TE:k * TE + 512],
                                    start=(k == 0), stop=(k == HK - 1))
                            for k in range(HK):
                                inst = t.matmul(
                                    pb[bmt][:, 0:2],
                                    W1S[sl][:, k * 128:(k + 1) * 128],
                                    xq_cur[:, k * TE + 512:k * TE + 514],
                                    start=(k == 0), stop=(k == HK - 1))
                        else:
                            for k in range(HK):
                                inst = t.matmul(
                                    pb[bmn][:, 0:2],
                                    W1S[sl][:, k * 128:(k + 1) * 128],
                                    xq_prv[:, k * TE + 512:k * TE + 514],
                                    start=(k == 0), stop=(k == HK - 1))
                        inst.then_inc(s_pe.h, 1)
                    elif kind == "wconv":
                        m = key
                        t.wait_ge(s_dve.h, V[("wdiag", m)])
                        t.wait_ge(s_act.h, V[("wev", m)])
                        bcv = 6 + (m % 2) if c is None else 6
                        if c is None:
                            if m >= 2:
                                t.wait_ge(s_act.h, V[("wsilu", m - 2)])
                        elif m >= 1:
                            t.wait_ge(s_act.h, V[("wsilu", m - 1)])
                        for tap in range(KF):
                            inst = t.matmul(pb[bcv][:, 0:WC],
                                            dgw[(m % 6) * 3 + tap],
                                            hp2[m % 4][:, tap:tap + WC],
                                            start=(tap == 0), stop=(tap == KF - 1))
                        inst.then_inc(s_pe.h, 1)
                    elif kind == "btr1":
                        t.wait_ge(s_dve.h, V[("wmx", MT - 1)])
                        for a in range(4):
                            inst = t.transpose(
                                pb[7].bitcast(BF16)[:, a * 128:(a + 1) * 128],
                                mx2[:, a * 128:(a + 1) * 128], idnt[0])
                        inst.then_inc(s_pe.h, 1)
                    elif kind == "btr2":
                        t.wait_ge(s_dve.h, V[("bred", 0)])
                        for a in range(4):
                            inst = t.transpose(pb[7][0:1, a * 128:(a + 1) * 128],
                                               mxq[:, 4 + a:5 + a], maskd)
                        inst.then_inc(s_pe.h, 1)
                    elif kind == "bbc":
                        t.wait_ge(s_dve.h, V[("bscale", 0)])
                        inst = t.matmul(pb[7][:, 0:T], ones1, r_s,
                                        start=True, stop=True)
                        inst.then_inc(s_pe.h, 1)
                    elif kind == "w2":
                        quad, k = key
                        sl, vv2 = wld_done[(quad, k)]
                        if c is None:
                            if quad == 0:
                                t.wait_ge(s_spx[sl].h, vv2)
                                t.wait_ge(s_dve.h, V[("wxq", (0, k))])
                            for mi in range(4):
                                bkz = ((2, 3, 6, 7) if quad == 0
                                       else (0, 1, 4, 5))[mi]
                                inst = t.matmul(
                                    pb[bkz][:, 0:WC],
                                    W1S[sl][:, quad * 512 + mi * 128:
                                            quad * 512 + (mi + 1) * 128],
                                    H2S[k % 4][:, 0:WC],
                                    start=(k == 0), stop=(k == MT - 1))
                            inst.then_inc(s_pe.h, 1)
                        else:
                            t.wait_ge(s_spx[sl].h, vv2)
                            if quad == 0:
                                t.wait_ge(s_dve.h, V[("wxq", (0, k))])
                            else:
                                pg, vg = ldxq_done[k]
                                t.wait_ge(s_gpx[pg].h, vg)
                                if k == 0:
                                    t.wait_ge(s_act.h, V[("wofev", 0)])
                            for mi in range(4):
                                bkz = (2, 3, 6, 7)[mi]
                                inst = t.matmul(
                                    pb[bkz][:, 0:WC],
                                    wpC4[sl][:, mi * 128:(mi + 1) * 128],
                                    H2S[k % 4][:, 0:WC],
                                    start=(k == 0), stop=(k == MT - 1))
                            inst.then_inc(s_pe.h, 1)

            def act_emit(s):
                for kind, key, n in full["act"]:
                    if kind == "sp":
                        g = key
                        s.wait_ge(s_pe.h, V[("wdt", g)])
                        if g >= 2:
                            s.wait_ge(s_dve.h, V[("dlt", g - 2)])
                        bank = pb[g % 2][:, 0:T]
                        bcol = bdt_sb[:, g:g + 1]
                        gs = g % 2
                        s.activation(out=sp_a, in_=bank, func=AF.Abs,
                                     bias=bcol)
                        s.activation(out=sp_r[gs], in_=bank, func=AF.Relu,
                                     bias=bcol)
                        s.activation(out=sp_u[gs], in_=sp_a, func=AF.Exp,
                                     scale=-1.0)
                        inst = s.activation(out=sp_u[gs], in_=sp_u[gs],
                                            func=AF.Ln, bias=1.0)
                        inst.then_inc(s_act.h, 1)
                    elif kind == "dA":
                        g, p = key
                        s.wait_ge(s_dve.h, V[("dlt", g)])
                        i = g * NPAIR + p
                        sl = i % NSL
                        if i >= NSL:
                            prev = divmod(i - NSL, NPAIR)
                            s.wait_ge(s_dve.h, V[("scan", prev)])
                        for j in range(2):
                            nn2 = p * 2 + j
                            inst = s.activation(
                                out=dAq[sl][:, j * T:(j + 1) * T],
                                in_=dlg[g % 2], func=AF.Exp,
                                scale=A_sb[:, g * NST + nn2:g * NST + nn2 + 1])
                        inst.then_inc(s_act.h, 1)
                    elif kind == "hsdc":
                        g = key
                        s.wait_ge(s_dve.h, V[("dx", g)])
                        if g >= 1:
                            s.wait_ge(s_pe.h, V[("hsd", g - 1)])
                        inst = s.activation(out=dlg[g % 2].bitcast(BF16)[:, 0:T],
                                            in_=hsb[:, g * T:(g + 1) * T],
                                            func=AF.Copy, scale=D_sb[:, g:g + 1])
                        inst.then_inc(s_act.h, 1)
                    elif kind == "wev":
                        m = key
                        s.wait_ge(s_pe.h, V[("w1", m)])
                        if m >= 4:
                            s.wait_ge(s_pe.h, V[("wconv", m - 4)])
                        bmn = 2 + 2 * (m % 2) if c is None else 2
                        bmt = bmn + 1
                        if not cheap:
                            s.activation(out=hp2[m % 4][:, 0:512],
                                         in_=pb[bmn][:, 0:512], func=AF.Copy)
                            inst = s.activation(out=hp2[m % 4][:, 512:514],
                                                in_=pb[bmt][:, 0:2], func=AF.Copy)
                        else:
                            inst = s.activation(out=hp2[m % 4][:, 0:2],
                                                in_=pb[bmn][:, 0:2], func=AF.Copy)
                        inst.then_inc(s_act.h, 1)
                    elif kind == "wsilu":
                        m = key
                        s.wait_ge(s_pe.h, V[("wconv", m)])
                        if m >= 16:
                            sl2, vv3 = spill_done[m - 16]
                            s.wait_ge(s_gp[sl2].h, vv3)
                            s.wait_ge(s_dve.h, V[("wmx", m - 16)])
                        bcv = 6 + (m % 2) if c is None else 6
                        inst = s.activation(out=h2A[m % 16][:, 0:WC],
                                            in_=pb[bcv][:, 0:WC], func=AF.Silu,
                                            bias=cb2_sb[:, m:m + 1])
                        inst.then_inc(s_act.h, 1)
                    elif kind == "bcp":
                        s.wait_ge(s_pe.h, V[("btr2", 0)])
                        for a in range(4):
                            inst = s.activation(
                                out=r_mx[:, a * 128:(a + 1) * 128],
                                in_=pb[7][0:1, a * 128:(a + 1) * 128],
                                func=AF.Copy)
                        inst.then_inc(s_act.h, 1)
                    elif kind == "bsqrt":
                        s.activation(out=r_t1, in_=r_mx, func=AF.Ln)
                        inst = s.activation(out=r_mx, in_=r_t1, func=AF.Exp,
                                            scale=0.5)
                        inst.then_inc(s_act.h, 1)
                    elif kind == "wofev":
                        quad = key
                        s.wait_ge(s_pe.h, V[("w2", (quad, MT - 1))])
                        for mi in range(4):
                            m = quad * 4 + mi
                            if c is None and quad == 1:
                                bkz = (0, 1, 4, 5)[mi]
                            else:
                                bkz = (2, 3, 6, 7)[mi]
                            inst = s.activation(
                                out=ofb[:, m * T:m * T + WC],
                                in_=pb[bkz][:, 0:WC], func=AF.Copy)
                        inst.then_inc(s_act.h, 1)

            def dve_emit(v):
                for kind, key, n in full["dve"]:
                    if kind == "dlt":
                        g = key
                        v.wait_ge(s_act.h, V[("sp", g)])
                        if g >= 2:
                            v.wait_ge(s_pe.h, V[("hsd", g - 2)])
                        inst = v.tensor_tensor(out=dlg[g % 2], in0=sp_r[g % 2],
                                               in1=sp_u[g % 2], op=ALU.add)
                        inst.then_inc(s_dve.h, 1)
                    elif kind == "dx":
                        g = key
                        inst = v.tensor_tensor(out=dxgs[g % 2], in0=dlg[g % 2],
                                               in1=hsb[:, g * T:(g + 1) * T],
                                               op=ALU.mult)
                        inst.then_inc(s_dve.h, 1)
                    elif kind == "cz":
                        g, p = key
                        inst = v.memset(
                            carry[:, g * NST + p * 2:g * NST + p * 2 + 2], 0.0)
                        inst.then_inc(s_dve.h, 1)
                        v.wait_ge(s_dve.h, V[("cz", (g, p))])
                    elif kind == "dbq":
                        g, p = key
                        i = g * NPAIR + p
                        sl = i % NSL
                        if i >= NSL:
                            v.wait_ge(s_pe.h, V[("ymm", divmod(i - NSL, NPAIR))])
                        inst = v.tensor_tensor(
                            out=dBq[sl].rearrange("q (n t) -> q n t", n=2),
                            in0=dxgs[g % 2].unsqueeze(1).broadcast_to(
                                [128, 2, T]),
                            in1=Brep.rearrange("q (n t) -> q n t", n=NST)
                            [:, p * 2:p * 2 + 2, :], op=ALU.mult)
                        inst.then_inc(s_dve.h, 1)
                    elif kind == "scan":
                        g, p = key
                        sl = (g * NPAIR + p) % NSL
                        v.wait_ge(s_act.h, V[("dA", (g, p))])
                        for j in range(2):
                            nn2 = p * 2 + j
                            inst = v.tensor_tensor_scan(
                                out=zq[:, j * T:(j + 1) * T],
                                data0=dAq[sl][:, j * T:(j + 1) * T],
                                data1=dBq[sl][:, j * T:(j + 1) * T],
                                initial=carry[:, g * NST + nn2:g * NST + nn2 + 1],
                                op0=ALU.mult, op1=ALU.add)
                        inst.then_inc(s_dve.h, 1)
                        v.wait_ge(s_dve.h, V[("scan", (g, p))])
                    elif kind == "pq":
                        g, p = key
                        sl = (g * NPAIR + p) % NSL
                        for j in range(2):
                            nn2 = p * 2 + j
                            v.tensor_copy(
                                out=carry[:, g * NST + nn2:g * NST + nn2 + 1],
                                in_=zq[:, j * T + T - 1:j * T + T])
                        inst = v.tensor_tensor(
                            out=dBq[sl], in0=zq,
                            in1=Crep[:, p * 2 * T:(p * 2 + 2) * T], op=ALU.mult)
                        inst.then_inc(s_dve.h, 1)
                    elif kind == "yev":
                        g = key
                        v.wait_ge(s_pe.h, V[("hsd", g)])
                        inst = v.tensor_tensor(out=yb[:, g * T:(g + 1) * T],
                                               in0=pb[4 + g % 2][:, 0:T],
                                               in1=resb[:, g * T:(g + 1) * T],
                                               op=ALU.mult)
                        inst.then_inc(s_dve.h, 1)
                    elif kind == "wdiag":
                        m = key
                        if m >= 6:
                            v.wait_ge(s_pe.h, V[("wconv", m - 6)])
                        if m == 0:
                            v.memset(mx2, 1.0 if cheap else 0.0)
                        if cheap:
                            v.wait_ge(s_act.h, V[("wev", m)])
                            v.memset(hp2[m % 4][:, 2:4], 0.0)
                        for tap in range(KF):
                            inst = v.tensor_scalar(
                                out=dgw[(m % 6) * 3 + tap], in0=maskd,
                                scalar1=cw2_sb[:, m * KF + tap:m * KF + tap + 1],
                                scalar2=None, op0=ALU.mult)
                        inst.then_inc(s_dve.h, 1)
                    elif kind == "wmx":
                        m = key
                        v.wait_ge(s_act.h, V[("wsilu", m)])
                        v.tensor_tensor(out=mxsq[:, 0:WC],
                                        in0=h2A[m % 16][:, 0:WC],
                                        in1=h2A[m % 16][:, 0:WC], op=ALU.mult)
                        inst = v.tensor_tensor(out=mx2[:, 0:WC],
                                               in0=mx2[:, 0:WC],
                                               in1=mxsq[:, 0:WC], op=ALU.max)
                        inst.then_inc(s_dve.h, 1)
                    elif kind == "bred":
                        v.wait_ge(s_pe.h, V[("btr1", 0)])
                        inst = v.tensor_reduce(
                            out=mxq[:, 4:8],
                            in_=pb[7].bitcast(BF16)[:, 0:512].rearrange(
                                "p (a q) -> p a q", a=4),
                            axis=mybir.AxisListType.X, op=ALU.max)
                        inst.then_inc(s_dve.h, 1)
                    elif kind == "bscale":
                        v.wait_ge(s_act.h, V[("bsqrt", 0)])
                        v.tensor_scalar(out=r_t1, in0=r_mx, scalar1=1e-5,
                                        scalar2=None, op0=ALU.max)
                        v.reciprocal(out=r_s, in_=r_t1)
                        inst = v.tensor_scalar(out=r_s, in0=r_s, scalar1=127.0,
                                               scalar2=None, op0=ALU.mult)
                        inst.then_inc(s_dve.h, 1)
                    elif kind == "bcopy":
                        v.wait_ge(s_pe.h, V[("bbc", 0)])
                        inst = v.tensor_copy(out=scf[:, 0:T], in_=pb[7][:, 0:T])
                        inst.then_inc(s_dve.h, 1)
                    elif kind == "wxq":
                        _, k = key
                        v.wait_ge(s_gps.h, V[("wmul", (0, k))])
                        inst = v.tensor_scalar(
                            out=H2S[k % 4][:, 0:WC],
                            in0=scf[:, 512 + (k % 2) * 512:
                                    512 + (k % 2) * 512 + WC],
                            scalar1=MAGIC, scalar2=MAGIC,
                            op0=ALU.add, op1=ALU.subtract)
                        inst.then_inc(s_dve.h, 1)

            PE(pe_emit)
            SC(act_emit)
            VE(dve_emit)
            for e, sem in (("pe", s_pe), ("act", s_act), ("dve", s_dve),
                           ("gps", s_gps)):
                tot = base[e] + sum(n for _, _, n in full[e])
                sem.v = tot

            if w is not None:
                def f_sy(sy):
                    for m in range(MT):
                        if m >= 4:
                            sy.wait_ge(s_pe.h, V[("w1", m - 4)])
                        s_spx[m % 4].inc(sy.dma_start(
                            out=W1S[m % 4][:, 0:HK * 128].rearrange(
                                "p (k c2) -> p k c2", c2=128),
                            in_=wq1_h[m, :, :].rearrange("(k p) c2 -> p k c2",
                                                         p=128)))
                    if c is None:
                        for k in range(MT):
                            sl = k % 4
                            if k >= 4:
                                sy.wait_ge(s_pe.h, V[("w2", (1, k - 4))])
                            else:
                                sy.wait_ge(s_act.h, V[("wev", MT - 1)])
                                sy.wait_ge(s_pe.h, V[("wconv", MT - 1)])
                            s_spx[sl].inc(sy.dma_start(
                                out=W1S[sl][:, 0:H], in_=wq2_h[k, :, :]))
                    else:
                        for quad in range(2):
                            for k in range(MT):
                                idx = quad * MT + k
                                sl = idx % 4
                                if idx >= 4:
                                    pidx = idx - 4
                                    sy.wait_ge(
                                        s_pe.h,
                                        V[("w2", (pidx // MT, pidx % MT))])
                                else:
                                    sy.wait_ge(s_pe.h, V[("w1", MT - 1)])
                                s_spx[sl].inc(sy.dma_start(
                                    out=wpC4[sl],
                                    in_=wq2_h[k, :,
                                              quad * 512:(quad + 1) * 512]))
                SY(f_sy)

                def f_gp(g2):
                    for m in range(MT):
                        g2.wait_ge(s_dve.h, V[("wmx", m)])
                        if m >= 2:
                            g2.wait_ge(s_gp[m % 2].h, spill_done[m - 2][1])
                        s_gp[m % 2].inc(g2.dma_start(
                            out=h2sp_h[m, :, 0:WC],
                            in_=h2A[m % 16][:, 0:WC],
                            allow_slow_non_contiguous=(WC < 8)))
                    for k in range(MT):
                        p = k % 4
                        g2.wait_ge(s_gp[k % 2].h, spill_done[k][1])
                        if k >= 4:
                            if c is None:
                                g2.wait_ge(s_pe.h, V[("w2", (1, k - 4))])
                            else:
                                g2.wait_ge(s_pe.h, V[("w2", (0, k - 4))])
                                g2.wait_ge(s_gpx[p].h, splq_done[k - 4][1])
                        s_gpx[p].inc(g2.dma_start(
                            out=H2S[p][:, 0:WC], in_=h2sp_h[k, :, 0:WC],
                            allow_slow_non_contiguous=(WC < 8)))
                        if c is not None and k >= 1:
                            pp = (k - 1) % 4
                            g2.wait_ge(s_dve.h, V[("wxq", (0, k - 1))])
                            s_gpx[pp].inc(g2.dma_start(
                                out=h2sp_h[k - 1, :, 0:WC],
                                in_=H2S[pp][:, 0:WC],
                                allow_slow_non_contiguous=(WC < 8)))
                        g2.wait_ge(s_gpx[p].h, ldh2_done[k][1])
                        g2.wait_ge(s_dve.h, V[("bcopy", 0)])
                        if k >= 2:
                            g2.wait_ge(s_dve.h, V[("wxq", (0, k - 2))])
                        inst = g2.tensor_tensor(
                            out=scf[:, 512 + (k % 2) * 512:
                                    512 + (k % 2) * 512 + WC],
                            in0=H2S[p][:, 0:WC], in1=scf[:, 0:WC],
                            op=ALU.mult)
                        inst.then_inc(s_gps.h, 1)
                    if c is not None:
                        g2.wait_ge(s_dve.h, V[("wxq", (0, MT - 1))])
                        s_gpx[(MT - 1) % 4].inc(g2.dma_start(
                            out=h2sp_h[MT - 1, :, 0:WC],
                            in_=H2S[(MT - 1) % 4][:, 0:WC],
                            allow_slow_non_contiguous=(WC < 8)))
                        for k in range(MT):
                            p = k % 4
                            if k >= 4:
                                g2.wait_ge(s_pe.h, V[("w2", (1, k - 4))])
                            else:
                                g2.wait_ge(s_gpx[p].h, splq_done[28 + p][1])
                                g2.wait_ge(s_pe.h, V[("w2", (0, 28 + p))])
                            s_gpx[p].inc(g2.dma_start(
                                out=H2S[p][:, 0:WC], in_=h2sp_h[k, :, 0:WC],
                                allow_slow_non_contiguous=(WC < 8)))
                GP(f_gp)

        def window_tail(w, n_chunks, bar=True):
            cheap = (w == n_chunks)
            if debug and w == 0:
                dbg_tap("d_of0", ofb[:, 0:HK * T], 128, HK * T, BF16)
                hh = dbg_out("d_h2w0", [MT, 128, T], BF16)
                barrier()

                def f2t(sy):
                    s_spd.inc(sy.dma_start(out=hh, in_=h2sp_h))
                SY(f2t)
                barrier()
            ln_quant_stage(ofb[:, 0:HK * T], True, 'out')
            dve_ln = s_dve.v
            gpd0x = s_gpd.v
            dve0x = s_dve.v
            gp_base = [s_gp[0].v, s_gp[1].v]
            if w == 0:
                src0, dst0, width = 0, 1, T - 1
            elif cheap:
                src0, dst0, width = w * T - 1, 0, 1
            else:
                src0, dst0, width = w * T - 1, 0, T

            def f_gp3(g2):
                g2.wait_ge(s_dve.h, dve_ln)
                for k in range(HK):
                    if k >= 1:
                        g2.wait_ge(s_dve.h, dve0x + k)
                    s_gp[k % 2].inc(g2.dma_start(
                        out=(sp_a if k % 2 == 0 else sp_r[0])[:, dst0:dst0 + width],
                        in_=xsp_h[k * 128:(k + 1) * 128, src0:src0 + width],
                        allow_slow_non_contiguous=(width < 8)))
            GP(f_gp3)

            def f_ve10(v):
                for k in range(HK):
                    v.wait_ge(s_gp[k % 2].h, gp_base[k % 2] + 16 * (k // 2 + 1))
                    s_dve.inc(v.tensor_tensor(
                        out=xpr[:, k * T:(k + 1) * T],
                        in0=scf[:, k * T:(k + 1) * T],
                        in1=sp_a if k % 2 == 0 else sp_r[0], op=ALU.add))
            VE(f_ve10)

            def f_gp4(g2):
                for k in range(HK):
                    g2.wait_ge(s_dve.h, dve0x + k + 1)
                    s_gpd.inc(g2.dma_start(
                        out=out_h[k * 128:(k + 1) * 128, src0:src0 + width],
                        in_=xpr[:, k * T + dst0:k * T + dst0 + width],
                        allow_slow_non_contiguous=(width < 8)))
            GP(f_gp4)
            if bar:
                barrier()

        def pbc(bank, k):
            return pb[bank][:, 0:T].unsqueeze(1).broadcast_to([128, k, T])

        def ln_quant_stage(src_ap, src_is_bf, mode, c=None):
            pe0, act0, dve0 = s_pe.v, s_act.v, s_dve.v
            n_k = HK
            D = n_k * 128
            has_cast = not src_is_bf
            dve_cast = dve0 + 1 if has_cast else dve0
            dve_stat = dve_cast + 1
            act_sq = lambda k: act0 + k + 1
            act_rstd = act0 + n_k + 1
            pe_sum = lambda k: pe0 + k + 1
            pe_done_sums = pe0 + n_k

            def f_sc(s):
                for k in range(n_k):
                    if k >= 2:
                        s.wait_ge(s_pe.h, pe_sum(k - 2))
                    if has_cast and k == 0:
                        s.wait_ge(s_dve.h, dve_cast)
                    srck = (src_ap[:, k * T:(k + 1) * T] if src_is_bf
                            else nb1[:, k * T:(k + 1) * T])
                    s_act.inc(s.activation(out=h2t[k % 2][:, 0:T], in_=srck,
                                           func=AF.Square))
                s.wait_ge(s_dve.h, dve_stat)
                s.activation(out=r_t2, in_=r_t1, func=AF.Ln)
                s_act.inc(s.activation(out=r_r, in_=r_t2, func=AF.Exp,
                                       scale=-0.5))
            SC(f_sc)
            assert s_act.v == act_rstd

            if has_cast:
                def f_vc(v):
                    s_dve.inc(v.tensor_copy(out=nb1[:, 0:n_k * T], in_=src_ap))
                VE(f_vc)

            def f_pe(t):
                for k in range(n_k):
                    t.wait_ge(s_act.h, act_sq(k))
                    o = onesh_a if k % 2 == 0 else onesh_b
                    rhs = (src_ap[:, k * T:(k + 1) * T] if src_is_bf
                           else nb1[:, k * T:(k + 1) * T])
                    t.matmul(pb[6][0:1, 0:T], o, rhs,
                             start=(k == 0), stop=(k == n_k - 1))
                    o2 = onesh_b if k % 2 == 0 else onesh_a
                    inst = t.matmul(pb[7][0:1, 0:T], o2, h2t[k % 2][:, 0:T],
                                    start=(k == 0), stop=(k == n_k - 1))
                    s_pe.inc(inst)
                t.wait_ge(s_dve.h, dve_stat)
                s_pe.inc(t.matmul(pb[4][:, 0:T], ones1, r_m,
                                  start=True, stop=True))
                t.wait_ge(s_act.h, act_rstd)
                s_pe.inc(t.matmul(pb[5][:, 0:T], ones1, r_r,
                                  start=True, stop=True))
            PE(f_pe)
            pe_bc_m = pe_done_sums + 1
            pe_bc_r = pe_done_sums + 2

            def f_ve(v):
                if has_cast:
                    pass
                v.wait_ge(s_pe.h, pe_done_sums)
                v.tensor_scalar(out=r_m, in0=pb[6][0:1, 0:T], scalar1=1.0 / D,
                                scalar2=None, op0=ALU.mult)
                v.tensor_scalar(out=r_t1, in0=pb[7][0:1, 0:T], scalar1=1.0 / D,
                                scalar2=None, op0=ALU.mult)
                v.tensor_tensor(out=r_t2, in0=r_m, in1=r_m, op=ALU.mult)
                v.tensor_tensor(out=r_t1, in0=r_t1, in1=r_t2, op=ALU.subtract)
                s_dve.inc(v.tensor_scalar(out=r_t1, in0=r_t1, scalar1=0.0,
                                          scalar2=EPS, op0=ALU.max, op1=ALU.add))
                assert s_dve.v == dve_stat
                if mode == 'xn1':
                    v.wait_ge(s_pe.h, pe_bc_r)
                    v.tensor_tensor(out=t3(scf[:, 0:HK * T], HK),
                                    in0=t3(src_ap, HK), in1=pbc(4, HK),
                                    op=ALU.subtract)
                    s_dve.inc(v.tensor_tensor(out=t3(nb1[:, 0:HK * T], HK),
                                              in0=t3(scf[:, 0:HK * T], HK),
                                              in1=pbc(5, HK), op=ALU.mult))
                elif mode == 'resid':
                    v.wait_ge(s_pe.h, pe_bc_r)
                    v.tensor_tensor(out=t3(scf[:, 0:HK * T], HK),
                                    in0=t3(src_ap, HK), in1=pbc(4, HK),
                                    op=ALU.subtract)
                    v.tensor_tensor(out=t3(scf[:, 0:HK * T], HK),
                                    in0=t3(scf[:, 0:HK * T], HK),
                                    in1=pbc(5, HK), op=ALU.mult)
                    s_dve.inc(v.tensor_tensor(out=xpr[:, 0:HK * T],
                                              in0=scf[:, 0:HK * T],
                                              in1=xf[:, 0:HK * T], op=ALU.add))
                elif mode == 'out':
                    v.wait_ge(s_pe.h, pe_bc_r)
                    v.tensor_tensor(out=t3(scf[:, 0:HK * T], HK),
                                    in0=t3(src_ap, HK), in1=pbc(4, HK),
                                    op=ALU.subtract)
                    s_dve.inc(v.tensor_tensor(out=t3(scf[:, 0:HK * T], HK),
                                              in0=t3(scf[:, 0:HK * T], HK),
                                              in1=pbc(5, HK), op=ALU.mult))
            VE(f_ve)

            if mode == 'quant':
                quant1_tail(src_ap, c)

        def quant1_tail(src_ap, c):
            pe0, act0, dve0 = s_pe.v, s_act.v, s_dve.v
            pe_ready = s_pe.v

            def f_ve(v):
                v.wait_ge(s_pe.h, pe_ready)
                for k in range(HK):
                    v.tensor_tensor(out=h2t[0][:, 0:T],
                                    in0=src_ap[:, k * T:(k + 1) * T],
                                    in1=pb[4][:, 0:T], op=ALU.subtract)
                    v.tensor_tensor(out=h2t[1][:, 0:T], in0=h2t[0][:, 0:T],
                                    in1=h2t[0][:, 0:T], op=ALU.mult)
                    if k == 0:
                        v.tensor_copy(out=mx2, in_=h2t[1][:, 0:T])
                    else:
                        v.tensor_tensor(out=mx2, in0=mx2, in1=h2t[1][:, 0:T],
                                        op=ALU.max)
                s_dve.inc(v.engine_nop())
            VE(f_ve)
            dve_tree = dve0 + 1
            colmax_to_row(0, dve_tree)
            act_mx = s_act.v
            dve_stat2 = s_dve.v + 1

            def f_sc2(s):
                s.activation(out=r_t1, in_=r_mx, func=AF.Ln)
                s_act.inc(s.activation(out=r_mx, in_=r_t1, func=AF.Exp,
                                       scale=0.5))
            SC(f_sc2)
            act_sqrt = s_act.v

            def f_ve3(v):
                v.wait_ge(s_act.h, act_sqrt)
                v.tensor_tensor(out=r_t1, in0=r_r, in1=r_mx, op=ALU.mult)
                v.tensor_scalar(out=r_t1, in0=r_t1, scalar1=1e-5, scalar2=None,
                                op0=ALU.max)
                v.reciprocal(out=r_s, in_=r_t1)
                v.tensor_scalar(out=r_s, in0=r_s, scalar1=127.0, scalar2=None,
                                op0=ALU.mult)
                v.tensor_tensor(out=r_t2, in0=r_r, in1=r_s, op=ALU.mult)
                v.tensor_tensor(out=r_mx, in0=r_m, in1=r_t2, op=ALU.mult)
                v.tensor_scalar(out=r_mx, in0=r_mx, scalar1=-1.0, scalar2=None,
                                op0=ALU.mult)
                s_dve.inc(v.reciprocal(out=r_is, in_=r_s))
            VE(f_ve3)
            dve_sc = s_dve.v

            def f_pe4(t):
                t.wait_ge(s_dve.h, dve_sc)
                t.matmul(pb[4][:, 0:T], ones1, r_t2, start=True, stop=True)
                t.matmul(pb[5][:, 0:T], ones1, r_mx, start=True, stop=True)
                s_pe.inc(t.matmul(pb[6][:, 0:T], ones1, r_is,
                                  start=True, stop=True))
            PE(f_pe4)
            pe_bc = s_pe.v

            def f_ve4(v):
                v.wait_ge(s_pe.h, pe_bc)
                v.tensor_tensor(out=t3(scf[:, 0:HK * T], HK),
                                in0=t3(src_ap, HK), in1=pbc(4, HK), op=ALU.mult)
                v.tensor_tensor(out=t3(scf[:, 0:HK * T], HK),
                                in0=t3(scf[:, 0:HK * T], HK), in1=pbc(5, HK),
                                op=ALU.add)
                v.tensor_scalar(out=scf[:, 0:HK * T], in0=scf[:, 0:HK * T],
                                scalar1=MAGIC, scalar2=MAGIC,
                                op0=ALU.add, op1=ALU.subtract)
                xq_view = xq1[c % 2].rearrange("p (k t) -> p k t", k=HK)[:, :, 2:]
                v.tensor_tensor(out=xq_view, in0=t3(scf[:, 0:HK * T], HK),
                                in1=pbc(6, HK), op=ALU.mult)
                if c > 0:
                    prev = xq1[(c - 1) % 2]
                    for k in range(HK):
                        inst = v.tensor_copy(
                            out=xq1[c % 2][:, k * (T + 2):k * (T + 2) + 2],
                            in_=prev[:, k * (T + 2) + 512:k * (T + 2) + 514])
                s_dve.inc(v.engine_nop())
            VE(f_ve4)

        def colmax_to_row(mxq_off, dve_ready):
            pe0 = s_pe.v

            def f_pe(t):
                t.wait_ge(s_dve.h, dve_ready)
                for a in range(4):
                    inst = t.transpose(
                        pb[7].bitcast(BF16)[:, a * 128:(a + 1) * 128],
                        mx2[:, a * 128:(a + 1) * 128], idnt[0])
                s_pe.inc(inst)
            PE(f_pe)

            def f_ve(v):
                v.wait_ge(s_pe.h, pe0 + 1)
                s_dve.inc(v.tensor_reduce(
                    out=mxq[:, mxq_off:mxq_off + 4],
                    in_=pb[7].bitcast(BF16)[:, 0:512].rearrange(
                        "p (a q) -> p a q", a=4),
                    axis=mybir.AxisListType.X, op=ALU.max))
            VE(f_ve)
            dve_red = s_dve.v

            def f_pe2(t):
                t.wait_ge(s_dve.h, dve_red)
                for a in range(4):
                    inst = t.transpose(pb[6][0:1, a * 128:(a + 1) * 128],
                                       mxq[:, mxq_off + a:mxq_off + a + 1],
                                       maskd)
                s_pe.inc(inst)
            PE(f_pe2)
            pe_tr2 = s_pe.v

            def f_sc(s):
                s.wait_ge(s_pe.h, pe_tr2)
                for a in range(4):
                    inst = s.activation(out=r_mx[:, a * 128:(a + 1) * 128],
                                        in_=pb[6][0:1, a * 128:(a + 1) * 128],
                                        func=AF.Copy)
                s_act.inc(inst)
            SC(f_sc)

        for c in range(n_chunks):
            def f_gpx(g2, c=c):
                for k in range(HK):
                    s_gpd.inc(g2.dma_start(out=xf[:, k * T:(k + 1) * T],
                                           in_=x_h[k * 128:(k + 1) * 128,
                                                   c * T:(c + 1) * T]))
            GP(f_gpx)
            barrier()

            ln_quant_stage(xf[:, 0:HK * T], True, 'xn1')
            barrier()
            if c == 0:
                dbg_tap("d_xn1", nb1[:, 0:HK * T], 128, HK * T, BF16)

            pe0, act0, dve0, spd0 = s_pe.v, s_act.v, s_dve.v, s_spd.v
            mm_done = {}
            conv_done = {}
            pe_order = []
            for m in range(WMT):
                pe_order.append(("mm", m))
                if m >= 2 and m - 2 < GT:
                    pe_order.append(("conv", m - 2))
            vv = pe0
            for kind, m in pe_order:
                vv += 1
                (mm_done if kind == "mm" else conv_done)[m] = vv
            evac_done = {}
            silu_done = {}
            act_order = []
            for m in range(WMT):
                act_order.append(("evac", m))
                if m >= 1 and m - 1 < GT:
                    act_order.append(("silu", m - 1))
            vv = act0
            for kind, m in act_order:
                vv += 1
                (evac_done if kind == "evac" else silu_done)[m] = vv
            diag_done = {m: dve0 + m + 1 for m in range(GT)}

            sp_base = [s_sp[0].v, s_sp[1].v]

            def f_sy(sy, c=c):
                for m in range(WMT):
                    if m >= 2:
                        sy.wait_ge(s_pe.h, mm_done[m - 2])
                    s_sp[m % 2].inc(sy.dma_start(
                        out=wpA[m % 2][:, 0:HK * 128].rearrange(
                            "p (k c2) -> p k c2", c2=128),
                        in_=wbf_h[m, :, :].rearrange("(k p) c2 -> p k c2",
                                                     p=128)))
            SY(f_sy)

            def f_pe(t, c=c):
                for kind, m in pe_order:
                    if kind == "mm":
                        t.wait_ge(s_sp[m % 2].h, sp_base[m % 2] + 16 * (m // 2 + 1))
                        if m >= 2:
                            t.wait_ge(s_act.h, evac_done[m - 2])
                        bank = pb[m % 2]
                        for k in range(HK):
                            inst = t.matmul(bank[:, 0:T],
                                            wpA[m % 2][:, k * 128:(k + 1) * 128],
                                            nb1[:, k * T:(k + 1) * T],
                                            start=(k == 0), stop=(k == HK - 1))
                        s_pe.inc(inst)
                    else:
                        t.wait_ge(s_dve.h, diag_done[m])
                        if m >= 2:
                            t.wait_ge(s_act.h, silu_done[m - 2])
                        bank = pb[2 + m % 2]
                        for tap in range(KM):
                            inst = t.matmul(bank[:, 0:T],
                                            diag[(m % 2) * KM + tap],
                                            hp[m % 2][:, tap:tap + T],
                                            start=(tap == 0),
                                            stop=(tap == KM - 1))
                        s_pe.inc(inst)
            PE(f_pe)

            def f_sc(s, c=c):
                for kind, m in act_order:
                    if kind == "evac":
                        s.wait_ge(s_pe.h, mm_done[m])
                        if m < GT:
                            if m >= 2:
                                s.wait_ge(s_pe.h, conv_done[m - 2])
                                s.wait_ge(s_dve.h, diag_done[m - 2])
                            s_act.inc(s.activation(
                                out=hp[m % 2][:, KM - 1:KM - 1 + T],
                                in_=pb[m % 2][:, 0:T], func=AF.Copy))
                        else:
                            g = m - GT
                            s_act.inc(s.activation(
                                out=resb[:, g * T:(g + 1) * T],
                                in_=pb[m % 2][:, 0:T], func=AF.Silu))
                    else:
                        s.wait_ge(s_pe.h, conv_done[m])
                        s_act.inc(s.activation(out=hsb[:, m * T:(m + 1) * T],
                                               in_=pb[2 + m % 2][:, 0:T],
                                               func=AF.Silu))
            SC(f_sc)

            def f_ve(v, c=c):
                for m in range(GT):
                    v.wait_ge(s_act.h, evac_done[m])
                    if m >= 2:
                        v.wait_ge(s_pe.h, conv_done[m - 2])
                    if c == 0:
                        v.memset(hp[m % 2][:, 0:KM - 1], 0.0)
                    else:
                        v.tensor_copy(out=hp[m % 2][:, 0:KM - 1],
                                      in_=mt3[:, m * (KM - 1):(m + 1) * (KM - 1)])
                    v.tensor_copy(out=mt3[:, m * (KM - 1):(m + 1) * (KM - 1)],
                                  in_=hp[m % 2][:, T:T + KM - 1])
                    for tap in range(KM):
                        inst = v.tensor_scalar(
                            out=diag[(m % 2) * KM + tap], in0=maskd,
                            scalar1=cw1_sb[:, m * KM + tap:m * KM + tap + 1],
                            scalar2=None, op0=ALU.mult)
                    s_dve.inc(inst)
            VE(f_ve)
            barrier()
            if c == 0:
                dbg_tap("d_hs", hsb[:, 0:GT * T], 128, GT * T, BF16)
                dbg_tap("d_res", resb[:, 0:GT * T], 128, GT * T, BF16)

            pe0 = s_pe.v

            def f_pe(t):
                for k in range(GT):
                    inst = t.matmul(pb[4][0:96, 0:T], wxq[:, k * 96:(k + 1) * 96],
                                    hsb[:, k * T:(k + 1) * T],
                                    start=(k == 0), stop=(k == GT - 1))
                s_pe.inc(inst)
            PE(f_pe)

            def f_ve(v):
                v.wait_ge(s_pe.h, pe0 + 1)
                s_dve.inc(v.tensor_copy(out=xdbq[0:96, 0:T],
                                        in_=pb[4][0:96, 0:T]))
            VE(f_ve)
            dve_xdb = s_dve.v

            def f_gp(g2):
                g2.wait_ge(s_dve.h, dve_xdb)
                s_gpd.inc(g2.dma_start(out=bcr_h, in_=xdbq[64:96, 0:T]))
                g2.wait_ge(s_gpd.h, s_gpd.v)
                for n in range(NST):
                    s_gpd.inc(g2.dma_start(
                        out=Brep[:, n * T:(n + 1) * T],
                        in_=bcr_h[n:n + 1, :].broadcast_to([128, T])))
                    s_gpd.inc(g2.dma_start(
                        out=Crep[:, n * T:(n + 1) * T],
                        in_=bcr_h[NST + n:NST + n + 1, :].broadcast_to([128, T])))
            GP(f_gp)
            barrier()

            ddbg = None
            if debug and c == 0:
                ddbg = dbg_out("d_delta", [128, GT * T], FP16)
                dbg_tap("d_Brep", Brep[:, 0:NST * T], 128, NST * T, BF16)


            merged_stage(c, (c - 1) if c > 0 else None, n_chunks)
            barrier()
            if c > 0:
                window_tail(c - 1, n_chunks, bar=False)
            if c == 0 and debug:
                dbg_tap("d_y", yb[:, 0:GT * T], 128, GT * T, BF16)

            pe0, act0, spd0 = s_pe.v, s_act.v, s_spd.v
            mm_done = {m: pe0 + m + 1 for m in range(HK)}
            evac_done = {m: act0 + m + 1 for m in range(HK)}
            sp_base = [s_sp[0].v, s_sp[1].v]
            spd_done = {}
            for m in range(HK):
                if m % 2 == 0:
                    spd_done[m] = (0, sp_base[0] + 16 * (m // 2 + 1))
                else:
                    spd_done[m] = (1, sp_base[1] + 32 * (m // 2 + 1))

            def lhs_wout(m, k):
                if m % 2 == 0:
                    return wpB[0][:, k * 128:(k + 1) * 128]
                return wpA[k // HK][:, (k % HK) * 128:(k % HK + 1) * 128]

            def f_sy(sy):
                for m in range(HK):
                    if m >= 2:
                        sy.wait_ge(s_pe.h, mm_done[m - 2])
                    if m % 2 == 0:
                        s_sp[0].inc(sy.dma_start(
                            out=wpB[0][:, 0:GT * 128].rearrange(
                                "p (k c2) -> p k c2", c2=128),
                            in_=wob_h[m, :, :].rearrange("(k p) c2 -> p k c2",
                                                         p=128)))
                    else:
                        for half in range(2):
                            s_sp[1].inc(sy.dma_start(
                                out=wpA[half][:, 0:HK * 128].rearrange(
                                    "p (k c2) -> p k c2", c2=128),
                                in_=wob_h[m, half * HK * 128:(half + 1) * HK * 128,
                                          :].rearrange("(k p) c2 -> p k c2",
                                                       p=128)))
            SY(f_sy)

            def f_pe(t):
                for m in range(HK):
                    sl, vv2 = spd_done[m]
                    t.wait_ge(s_sp[sl].h, vv2)
                    if m >= 2:
                        t.wait_ge(s_act.h, evac_done[m - 2])
                    bank = pb[m % 2]
                    for k in range(GT):
                        inst = t.matmul(bank[:, 0:T], lhs_wout(m, k),
                                        yb[:, k * T:(k + 1) * T],
                                        start=(k == 0), stop=(k == GT - 1))
                    s_pe.inc(inst)
            PE(f_pe)

            def f_sc(s):
                for m in range(HK):
                    s.wait_ge(s_pe.h, mm_done[m])
                    s_act.inc(s.activation(out=Brep[:, m * T:(m + 1) * T],
                                           in_=pb[m % 2][:, 0:T], func=AF.Copy))
            SC(f_sc)
            barrier()

            ln_quant_stage(Brep[:, 0:HK * T], True, 'resid')
            barrier()
            if c == 0:
                dbg_tap("d_xp", xpr[:, 0:HK * T], 128, HK * T, F32)

            def f_gp(g2, c=c):
                for k in range(HK):
                    s_gpd.inc(g2.dma_start(
                        out=xsp_h[k * 128:(k + 1) * 128, c * T:(c + 1) * T],
                        in_=xpr[:, k * T:(k + 1) * T]))
            GP(f_gp)

            ln_quant_stage(xpr[:, 0:HK * T], False, 'quant', c=c)
            barrier()
            if c == 0:
                dbg_tap("d_xq1",
                        xq1[0].rearrange("p (k t) -> p k t", k=HK)[:, :, 2:],
                        128, HK * T, BF16)


        merged_stage(None, n_chunks - 1, n_chunks)
        barrier()
        window_tail(n_chunks - 1, n_chunks)
        merged_stage(None, n_chunks, n_chunks)
        barrier()
        window_tail(n_chunks, n_chunks)


        def fin(sy):
            sy.wait_ge(s_gpd.h, s_gpd.v)
            sy.wait_ge(s_spd.h, s_spd.v)
        SY(fin)

    return nc, dbg_handles



def prep_inputs(inputs, core):
    x = inputs["L0_attn_hidden_states"]
    d = {}
    d["x"] = np.ascontiguousarray(np.asarray(x)[core].T)
    d["winT"] = np.ascontiguousarray(np.asarray(inputs["L0_attn_w_in"]).T)
    d["wxT"] = np.ascontiguousarray(np.asarray(inputs["L0_attn_w_xproj"]).T)
    d["wdtT"] = np.ascontiguousarray(np.asarray(inputs["L0_attn_w_dt"]).T)
    d["woutT"] = np.ascontiguousarray(np.asarray(inputs["L0_attn_w_out"]).T)
    d["w1T"] = np.ascontiguousarray(np.asarray(inputs["L0_ffn_w1"]).T)
    d["w2T"] = np.ascontiguousarray(np.asarray(inputs["L0_ffn_w2"]).T)
    d["bdt"] = np.ascontiguousarray(
        np.asarray(inputs["L0_attn_b_dt"]).reshape(GT, 128).T)
    d["alog"] = np.ascontiguousarray(
        np.asarray(inputs["L0_attn_A_log"]).reshape(GT, 128, NST)
        .transpose(1, 0, 2).reshape(128, GT * NST))
    d["dvec"] = np.ascontiguousarray(
        np.asarray(inputs["L0_attn_D"]).reshape(GT, 128).T)
    d["cw1"] = np.ascontiguousarray(
        np.asarray(inputs["L0_attn_w_conv"]).reshape(GT, 128, KM)
        .transpose(1, 0, 2).reshape(128, GT * KM))
    d["cw2"] = np.ascontiguousarray(
        np.asarray(inputs["L0_ffn_conv_w"]).reshape(MT, 128, KF)
        .transpose(1, 0, 2).reshape(128, MT * KF))
    d["cb2"] = np.ascontiguousarray(
        np.asarray(inputs["L0_ffn_conv_b"]).reshape(MT, 128).T)
    return {k: np.ascontiguousarray(np.asarray(v, dtype=np.float32))
            for k, v in d.items()}


def kernel(**inputs):
    nc, _ = build(debug=False)
    in_maps = [prep_inputs(inputs, core) for core in range(8)]
    res = run_bass_kernel_spmd(nc, in_maps, core_ids=list(range(8)))
    outs = [np.asarray(r["out"], np.float32).T for r in res.results]
    return np.stack(outs, axis=0)


# version 19
# speedup vs baseline: 1.0841x; 1.0272x over previous
import sys
sys.path.insert(0, "/opt/trn_rl_repo")
from contextlib import ExitStack

import numpy as np
import ml_dtypes
import concourse.bass as bass
import concourse.mybir as mybir
from concourse.bass_utils import run_bass_kernel_spmd

F32 = mybir.dt.float32
BF16 = mybir.dt.bfloat16
FP16 = mybir.dt.float16
AF = mybir.ActivationFunctionType
ALU = mybir.AluOpType

H = 1024
DIN = 2048
DH = 4096
NST = 16
RNK = 64
L = 4096
T = 512
NCH = L // T
HK = H // 128
GT = DIN // 128
MT = DH // 128
WMT = (2 * DIN) // 128
KM = 4
KF = 3
MAGIC = float(np.float32(1.5 * 2 ** 23))
EPS = 1e-5


class Sem:
    def __init__(self, stack, nc, name, step=1):
        self.h = stack.enter_context(nc.semaphore(name))
        self.v = 0
        self.step = step

    def inc(self, inst, k=None):
        k = self.step if k is None else k
        inst.then_inc(self.h, k)
        self.v += k
        return self.v


def t3(ap, k):
    return ap.rearrange("p (k t) -> p k t", k=k)


def build(debug=False, n_chunks=NCH, stop_after=None):
    nc = bass.Bass("TRN2", target_bir_lowering=False, debug=False,
                   detect_race_conditions=False)

    x_h = nc.dram_tensor("x", [H, L], F32, kind="ExternalInput").ap()
    winT_h = nc.dram_tensor("winT", [H, 2 * DIN], F32, kind="ExternalInput").ap()
    wxT_h = nc.dram_tensor("wxT", [DIN, 96], F32, kind="ExternalInput").ap()
    wdtT_h = nc.dram_tensor("wdtT", [RNK, DIN], F32, kind="ExternalInput").ap()
    woutT_h = nc.dram_tensor("woutT", [DIN, H], F32, kind="ExternalInput").ap()
    w1T_h = nc.dram_tensor("w1T", [H, DH], F32, kind="ExternalInput").ap()
    w2T_h = nc.dram_tensor("w2T", [DH, H], F32, kind="ExternalInput").ap()
    bdt_h = nc.dram_tensor("bdt", [128, GT], F32, kind="ExternalInput").ap()
    alog_h = nc.dram_tensor("alog", [128, GT * NST], F32, kind="ExternalInput").ap()
    dvec_h = nc.dram_tensor("dvec", [128, GT], F32, kind="ExternalInput").ap()
    cw1_h = nc.dram_tensor("cw1", [128, GT * KM], F32, kind="ExternalInput").ap()
    cw2_h = nc.dram_tensor("cw2", [128, MT * KF], F32, kind="ExternalInput").ap()
    cb2_h = nc.dram_tensor("cb2", [128, MT], F32, kind="ExternalInput").ap()

    out_h = nc.dram_tensor("out", [H, L], F32, kind="ExternalOutput").ap()

    wbf_h = nc.dram_tensor("wbf", [WMT, H, 128], BF16, kind="Internal").ap()
    wob_h = nc.dram_tensor("wob", [HK, DIN, 128], BF16, kind="Internal").ap()
    wq1_h = nc.dram_tensor("wq1", [MT, H, 128], BF16, kind="Internal").ap()
    wq2_h = nc.dram_tensor("wq2", [MT, 128, H], BF16, kind="Internal").ap()
    bcr_h = nc.dram_tensor("bcr", [32, T], BF16, kind="Internal").ap()
    h2sp_h = nc.dram_tensor("h2sp", [MT, 128, T], BF16, kind="Internal").ap()
    xsp_h = nc.dram_tensor("xsp", [H, L], F32, kind="Internal").ap()

    dbg_handles = {}

    def dbg_out(name, shape, dtype):
        h = nc.dram_tensor(name, shape, dtype, kind="ExternalOutput").ap()
        dbg_handles[name] = h
        return h

    with ExitStack() as st:
        def sb(name, shape, dt=F32):
            return st.enter_context(nc.sbuf_tensor(name, shape, dt))[:]

        def psm(name, shape, dt=F32):
            return st.enter_context(nc.psum_tensor(name, shape, dt))[:]

        s_pe = Sem(st, nc, "s_pe")
        s_act = Sem(st, nc, "s_act")
        s_dve = Sem(st, nc, "s_dve")
        s_gps = Sem(st, nc, "s_gps")
        s_spd = Sem(st, nc, "s_spd", 16)
        s_gpd = Sem(st, nc, "s_gpd", 16)
        s_sp = [Sem(st, nc, "s_sp0", 16), Sem(st, nc, "s_sp1", 16)]
        s_gp = [Sem(st, nc, "s_gp0", 16), Sem(st, nc, "s_gp1", 16)]
        s_spx = s_sp + [Sem(st, nc, "s_sp2", 16), Sem(st, nc, "s_sp3", 16)]
        s_gpx = s_gp + [Sem(st, nc, "s_gp2", 16), Sem(st, nc, "s_gp3", 16)]

        xf = sb("xf", [128, HK * T], BF16)
        scf = sb("scf", [128, HK * T])
        nb1 = sb("nb1", [128, HK * T], BF16)
        hp = [sb(f"hp{i}", [128, T + KM - 1], BF16) for i in range(2)]
        mt3 = sb("mt3", [128, GT * (KM - 1)], BF16)
        hsb = sb("hsb", [128, GT * T], BF16)
        resb = sb("resb", [128, GT * T], BF16)
        dlg = [sb(f"dlg{i}", [128, T], FP16) for i in range(2)]
        dxg = sb("dxg", [128, T], BF16)
        xdbq = sb("xdbq", [96, T], BF16)
        Brep = sb("Brep", [128, NST * T], BF16)
        Crep = sb("Crep", [128, NST * T], BF16)
        dAq = [sb(f"dAq{i}", [128, 2 * T], BF16) for i in range(3)]
        dBq = [sb(f"dBq{i}", [128, 2 * T], BF16) for i in range(3)]
        NSL = 3
        zq = sb("zq", [128, 2 * T], BF16)
        carry = sb("carry", [128, GT * NST])
        yb = sb("yb", [128, GT * T], BF16)
        sp_a = sb("sp_a", [128, T])
        sp_r = [sb(f"sp_r{i}", [128, T]) for i in range(2)]
        sp_u = [sb(f"sp_u{i}", [128, T]) for i in range(2)]
        xpr = sb("xpr", [128, HK * T])
        xq1 = [sb(f"xq1_{i}", [128, HK * (T + 2)], BF16) for i in range(2)]
        mxsq = sb("mxsq", [128, T], BF16)
        zrot = sb("zrot", [128, 128], BF16)
        ftail = sb("ftail", [128, MT], BF16)
        h2t = [sb(f"h2t_{i}", [128, T], BF16) for i in range(2)]
        mx2 = sb("mx2", [128, T], BF16)
        ofb = nb1
        wpA = [sb(f"wpA{i}", [128, HK * 128], BF16) for i in range(2)]
        wpB = [sb("wpB0", [128, GT * 128], BF16)]
        wpC = [wpA[0][:, 0:H], wpA[1][:, 0:H]]
        wpC4 = [wpA[0][:, 0:512], wpA[0][:, 512:1024],
                wpA[1][:, 0:512], wpA[1][:, 512:1024]]
        W1S = [wpA[0], wpA[1],
               wpB[0][:, 0:HK * 128], wpB[0][:, HK * 128:2 * HK * 128]]
        rtmp = scf[:, 3460:3972]
        H2S = [h2t[0], h2t[1],
               scf.bitcast(BF16)[:, 3584:4096], scf.bitcast(BF16)[:, 4096:4608]]
        dxgs = [dxg, scf.bitcast(BF16)[:, 3072:3584]]
        hp2 = [hp[0][:, 0:T + KF], hp[1][:, 0:T + KF],
               scf.bitcast(BF16)[:, 4608:4608 + T + KF],
               scf.bitcast(BF16)[:, 5124:5124 + T + KF]]
        h2A = [xpr.bitcast(BF16)[:, i * 512:(i + 1) * 512] for i in range(16)]
        ones_a = sb("ones_a", [128, 1])
        ones_b = sb("ones_b", [128, 1])
        onesh_a = sb("onesh_a", [128, 1], BF16)
        onesh_b = sb("onesh_b", [128, 1], BF16)
        ones1 = sb("ones1", [1, 128])
        idnt = [sb(f"idnt{i}", [128, 128], BF16) for i in range(2)]
        iof = sb("iof", [128, 128])
        iop = sb("iop", [128, 1])
        maskd = sb("maskd", [128, 128])
        A_sb = sb("A_sb", [128, GT * NST])
        D_sb = sb("D_sb", [128, GT])
        bdt_sb = sb("bdt_sb", [128, GT])
        cw1_sb = sb("cw1_sb", [128, GT * KM])
        cw2_sb = sb("cw2_sb", [128, MT * KF])
        cb2_sb = sb("cb2_sb", [128, MT])
        diag = [sb(f"diag{i}", [128, 128], BF16) for i in range(KM * 2)]
        dgw = diag + [scf.bitcast(BF16)[:, 5640 + i * 128:5768 + i * 128]
                      for i in range(10)]
        wxq = sb("wxq", [128, GT * 96], BF16)
        wdtq = sb("wdtq", [RNK, GT * 128], BF16)
        swrow = sb("swrow", [1, 256])
        swb = sb("swb", [128, 2])
        r_m = Crep.bitcast(F32)[0:1, 2048:2560]
        r_r = Crep.bitcast(F32)[0:1, 2560:3072]
        r_t2 = Crep.bitcast(F32)[0:1, 3072:3584]
        r_is = Crep.bitcast(F32)[0:1, 3584:4096]
        r_t1 = sb("r_t1", [1, T])
        r_mx = sb("r_mx", [1, T])
        r_s = sb("r_s", [1, T])
        mxq = sb("mxq", [128, 8])

        pb = [psm(f"pb{i}", [128, 512]) for i in range(8)]

        block = st.enter_context(nc.Block())

        def PE(f): block.tensor(f)
        def VE(f): block.vector(f)
        def SC(f): block.scalar(f)
        def GP(f): block.gpsimd(f)
        def SY(f): block.sync(f)

        def barrier():
            all_s = [(s_pe,), (s_act,), (s_dve,), (s_gps,), (s_spd,), (s_gpd,),
                     (s_sp[0],), (s_sp[1],), (s_gp[0],), (s_gp[1],),
                     (s_spx[2],), (s_spx[3],), (s_gpx[2],), (s_gpx[3],)]
            vals = [(s, s.v) for (s,) in all_s]

            def mk(skip):
                def f(eng):
                    for s, v in vals:
                        if s is not skip and v > 0:
                            eng.wait_ge(s.h, v)
                return f
            PE(mk(s_pe))
            VE(mk(s_dve))
            SC(mk(s_act))
            GP(mk(s_gps))
            SY(mk(None))

        def dbg_tap(name, ap_in, rows, cols, dtype):
            if not debug:
                return
            h = dbg_out(name, [rows, cols], dtype)
            barrier()

            def f2(sy):
                s_spd.inc(sy.dma_start(out=h, in_=ap_in))
            SY(f2)
            barrier()

        def setup_sync(sy):
            s_spd.inc(sy.dma_start(out=bdt_sb, in_=bdt_h))
            s_spd.inc(sy.dma_start(out=A_sb, in_=alog_h))
            s_spd.inc(sy.dma_start(out=D_sb, in_=dvec_h))
            s_spd.inc(sy.dma_start(out=cw1_sb, in_=cw1_h))
            s_spd.inc(sy.dma_start(out=cw2_sb, in_=cw2_h))
            s_spd.inc(sy.dma_start(out=cb2_sb, in_=cb2_h))
        SY(setup_sync)

        def setup_gps(g):
            g.iota(iof, [[1, 128]], channel_multiplier=0,
                   allow_small_or_imprecise_dtypes=True)
            g.iota(iop, [[0, 1]], channel_multiplier=1,
                   allow_small_or_imprecise_dtypes=True)
            g.memset(swrow, 1.0)
            g.memset(zrot, 0.0)
            g.memset(xq1[0], 0.0)
            g.memset(xq1[1], 0.0)
            g.memset(ones_a, 1.0)
            g.memset(ones_b, 1.0)
            g.memset(onesh_a, 1.0)
            g.memset(onesh_b, 1.0)
            s_gps.inc(g.memset(ones1, 1.0))
        GP(setup_gps)

        def setup_ve(v):
            v.wait_ge(s_gps.h, s_gps.v)
            v.tensor_scalar(out=maskd, in0=iof, scalar1=iop, scalar2=None,
                            op0=ALU.is_equal)
            v.tensor_copy(out=idnt[0], in_=maskd)
            s_dve.inc(v.tensor_copy(out=idnt[1], in_=maskd))
        VE(setup_ve)

        def setup_sc(s):
            s.wait_ge(s_spd.h, s_spd.v)
            s_act.inc(s.activation(out=A_sb, in_=A_sb, func=AF.Exp))
        SC(setup_sc)

        def setup_ve2(v):
            v.wait_ge(s_act.h, s_act.v)
            s_dve.inc(v.tensor_scalar(out=A_sb, in0=A_sb, scalar1=-1.0,
                                      scalar2=None, op0=ALU.mult))
        VE(setup_ve2)
        barrier()

        def res_w_sync(sy):
            for k in range(GT):
                s_spd.inc(sy.dma_start(out=scf[:, k * 96:(k + 1) * 96],
                                       in_=wxT_h[k * 128:(k + 1) * 128, :]))
            for k in range(GT):
                s_spd.inc(sy.dma_start(
                    out=scf[0:RNK, GT * 96 + k * 128:GT * 96 + (k + 1) * 128],
                    in_=wdtT_h[:, k * 128:(k + 1) * 128]))
        SY(res_w_sync)

        def res_w_ve(v):
            v.wait_ge(s_spd.h, s_spd.v)
            v.tensor_copy(out=wxq, in_=scf[:, 0:GT * 96])
            s_dve.inc(v.tensor_copy(
                out=wdtq, in_=scf[0:RNK, GT * 96:GT * 96 + GT * 128]))
        VE(res_w_ve)
        barrier()

        def conv_weight(src_h, rows, n_m, dst_h, quant=False):
            kb_n = rows // 128
            width = n_m * 128
            lbuf = [scf[:, 0:width], xpr[:, 0:width]]
            obuf = [nb1[:, 0:width], yb[:, 0:width]]
            spd0, dve0, gpd0 = s_spd.v, s_dve.v, s_gpd.v
            for kb in range(kb_n):
                def f_sy(sy, kb=kb):
                    if kb >= 2:
                        sy.wait_ge(s_dve.h, dve0 + kb - 1)
                    s_spd.inc(sy.dma_start(
                        out=lbuf[kb % 2],
                        in_=src_h[kb * 128:(kb + 1) * 128, 0:width]))
                SY(f_sy)

                def f_ve(v, kb=kb):
                    v.wait_ge(s_spd.h, spd0 + 16 * (kb + 1))
                    if kb >= 2:
                        v.wait_ge(s_gpd.h, gpd0 + 16 * (kb - 1))
                    src = lbuf[kb % 2]
                    if not quant:
                        s_dve.inc(v.tensor_copy(out=obuf[kb % 2], in_=src))
                    else:
                        v.tensor_scalar(out=src, in0=src,
                                        scalar1=swb[:, 0:1], scalar2=None,
                                        op0=ALU.mult)
                        v.tensor_scalar(out=src, in0=src,
                                        scalar1=MAGIC, scalar2=MAGIC,
                                        op0=ALU.add, op1=ALU.subtract)
                        v.tensor_scalar(out=src, in0=src,
                                        scalar1=1.0, scalar2=-1.0,
                                        op0=ALU.min, op1=ALU.max)
                        s_dve.inc(v.tensor_scalar(out=obuf[kb % 2], in0=src,
                                                  scalar1=swb[:, 1:2],
                                                  scalar2=None, op0=ALU.mult))
                VE(f_ve)

                def f_gp(g, kb=kb):
                    g.wait_ge(s_dve.h, dve0 + kb + 1)
                    dst = dst_h[:, kb * 128:(kb + 1) * 128, :]
                    dst = dst.rearrange("m p c -> p m c")
                    s_gpd.inc(g.dma_start(
                        out=dst,
                        in_=obuf[kb % 2].rearrange("p (m c) -> p m c", c=128)))
                GP(f_gp)
            barrier()

        def wsum_pass(src_h, rows, width, total):
            kb_n = rows // 128
            lbuf = [scf[:, 0:width], xpr[:, 0:width]]

            def z_ve(v):
                s_dve.inc(v.memset(carry[:, 0:1], 0.0))
            VE(z_ve)
            spd0, act0, dve0 = s_spd.v, s_act.v, s_dve.v
            for kb in range(kb_n):
                def f_sy(sy, kb=kb):
                    if kb >= 2:
                        sy.wait_ge(s_act.h, act0 + kb - 1)
                    s_spd.inc(sy.dma_start(
                        out=lbuf[kb % 2],
                        in_=src_h[kb * 128:(kb + 1) * 128, 0:width]))
                SY(f_sy)

                def f_sc(s, kb=kb):
                    s.wait_ge(s_spd.h, spd0 + 16 * (kb + 1))
                    if kb >= 2:
                        s.wait_ge(s_dve.h, dve0 + kb - 1)
                    s_act.inc(s.activation(
                        out=nb1[:, 0:width], in_=lbuf[kb % 2], func=AF.Abs,
                        accum_out=sp_a[:, kb % 2:kb % 2 + 1]))
                SC(f_sc)

                def f_ve(v, kb=kb):
                    v.wait_ge(s_act.h, act0 + kb + 1)
                    s_dve.inc(v.tensor_tensor(
                        out=carry[:, 0:1], in0=carry[:, 0:1],
                        in1=sp_a[:, kb % 2:kb % 2 + 1], op=ALU.add))
                VE(f_ve)

            def f_pe(t):
                t.wait_ge(s_dve.h, s_dve.v)
                s_pe.inc(t.matmul(pb[6][0:1, 0:1], carry[:, 0:1], ones_a,
                                  start=True, stop=True))
            PE(f_pe)

            def f_ve2(v):
                v.wait_ge(s_pe.h, s_pe.v)
                s_dve.inc(v.tensor_scalar(out=swrow[:, 0:1], in0=pb[6][0:1, 0:1],
                                          scalar1=1.0 / total, scalar2=1e-5,
                                          op0=ALU.mult, op1=ALU.max))
            VE(f_ve2)

            def f_sc_r(s):
                s.wait_ge(s_dve.h, s_dve.v)
                s.activation(out=swrow[:, 1:2], in_=swrow[:, 0:1], func=AF.Ln)
                s_act.inc(s.activation(out=swrow[:, 128:129], in_=swrow[:, 1:2],
                                       func=AF.Exp, scale=-1.0))
            SC(f_sc_r)

            def f_pe2(t):
                t.wait_ge(s_dve.h, s_dve.v)
                t.wait_ge(s_act.h, s_act.v)
                t.matmul(pb[6][:, 2:3], ones1, swrow[:, 128:129], start=True, stop=True)
                s_pe.inc(t.matmul(pb[6][:, 3:4], ones1, swrow[:, 0:1],
                                  start=True, stop=True))
            PE(f_pe2)

            def f_ve3(v):
                v.wait_ge(s_pe.h, s_pe.v)
                v.tensor_copy(out=swb[:, 0:1], in_=pb[6][:, 2:3])
                s_dve.inc(v.tensor_copy(out=swb[:, 1:2], in_=pb[6][:, 3:4]))
            VE(f_ve3)
            barrier()

        conv_weight(winT_h, H, WMT, wbf_h)
        conv_weight(woutT_h, DIN, HK, wob_h)
        if stop_after != "front":
            wsum_pass(w1T_h, H, DH, H * DH)
            dbg_tap("d_swb1", swb[:, 0:2], 128, 2, F32)
            conv_weight(w1T_h, H, MT, wq1_h, quant=True)
            wsum_pass(w2T_h, DH, H, DH * H)
            lbuf2 = [scf[:, 0:H], xpr[:, 0:H]]
            obuf2 = [nb1[:, 0:H], yb[:, 0:H]]
            spd0w, dve0w, gpd0w = s_spd.v, s_dve.v, s_gpd.v
            for kb in range(DH // 128):
                def f_sy(sy, kb=kb):
                    if kb >= 2:
                        sy.wait_ge(s_dve.h, dve0w + kb - 1)
                    s_spd.inc(sy.dma_start(out=lbuf2[kb % 2],
                                           in_=w2T_h[kb * 128:(kb + 1) * 128, :]))
                SY(f_sy)

                def f_ve(v, kb=kb):
                    v.wait_ge(s_spd.h, spd0w + 16 * (kb + 1))
                    if kb >= 2:
                        v.wait_ge(s_gpd.h, gpd0w + 16 * (kb - 1))
                    src = lbuf2[kb % 2]
                    v.tensor_scalar(out=src, in0=src,
                                    scalar1=swb[:, 0:1], scalar2=None,
                                    op0=ALU.mult)
                    v.tensor_scalar(out=src, in0=src,
                                    scalar1=MAGIC, scalar2=MAGIC,
                                    op0=ALU.add, op1=ALU.subtract)
                    v.tensor_scalar(out=src, in0=src,
                                    scalar1=1.0, scalar2=-1.0,
                                    op0=ALU.min, op1=ALU.max)
                    s_dve.inc(v.tensor_scalar(out=obuf2[kb % 2], in0=src,
                                              scalar1=swb[:, 1:2], scalar2=None,
                                              op0=ALU.mult))
                VE(f_ve)

                def f_gp(g, kb=kb):
                    g.wait_ge(s_dve.h, dve0w + kb + 1)
                    s_gpd.inc(g.dma_start(out=wq2_h[kb, :, :],
                                          in_=obuf2[kb % 2]))
                GP(f_gp)
            barrier()

        NPAIR = NST // 2

        def merged_stage(c, w, n_chunks):
            cheap = (w is not None and w == n_chunks)
            WC = 2 if cheap else T
            V = {}
            ops = {"pe": [], "act": [], "dve": [], "gps": []}

            win = {"pe": {}, "act": {}, "dve": {}, "gps": {}}
            if w is not None:
                def at(e, q, item):
                    win[e].setdefault(q, []).append(item)
                for m in range(MT):
                    qa = min(6, m // 5)
                    at("pe", qa, ("w1", m, 1))
                    at("act", qa, ("wev", m, 1))
                    at("dve", qa, ("wdiag", m, 1))
                    at("pe", qa, ("wconv", m, 1))
                    at("act", qa, ("wsilu", m, 1))
                    at("dve", min(7, qa + 1), ("wmx", m, 1))
                at("pe", 7, ("btr1", 0, 1))
                at("dve", 7, ("bred", 0, 1))
                at("pe", 7, ("btr2", 0, 1))
                at("act", 7, ("bcp", 0, 1))
                at("act", 7, ("bsqrt", 0, 1))
                at("dve", 7, ("bscale", 0, 1))
                at("pe", 7, ("bbc", 0, 1))
                at("dve", 7, ("bcopy", 0, 1))
                if c is None:
                    for k in range(MT):
                        qc = 8 + k // 8
                        at("gps", qc, ("wmul", (0, k), 1))
                        at("act", qc, ("wxq", (0, k), 1))
                        at("pe", qc, ("w2", (0, k), 1))
                        at("pe", qc, ("w2", (1, k), 1))
                    at("act", 11, ("wofev", 0, 1))
                    at("act", 11, ("wofev", 1, 1))
                else:
                    for quad in range(2):
                        for k in range(MT):
                            idx = quad * MT + k
                            qc = 8 + idx // 8
                            if quad == 0:
                                at("gps", qc, ("wmul", (0, k), 1))
                                at("act", qc, ("wxq", (0, k), 1))
                            at("pe", qc, ("w2", (quad, k), 1))
                        at("act", min(15, 8 + (quad * MT + MT - 1) // 8),
                           ("wofev", quad, 1))


            if c is not None:
                ops["pe"] += [("wdt", 0, 1), ("wdt", 1, 1)]
                ops["act"] += ([("sp", 0, 1)]
                               + [("dA", (0, p), 1) for p in range(NSL)])
                ops["dve"] += [("dlt", 0, 1), ("dx", 0, 1)]
                for g in range(GT):
                    scan_pe = ([("ymm", (g, p), 1) for p in range(NPAIR)]
                               + [("hsd", g, 1)])
                    if g + 2 < GT:
                        scan_pe.append(("wdt", g + 2, 1))
                    ops["pe"] += scan_pe + win["pe"].get(g - 1, [])
                    av = [("dA", (g, p), 1) for p in range(NSL, NPAIR)]
                    if g + 1 < GT:
                        av += ([("sp", g + 1, 1)]
                               + [("dA", (g + 1, p), 1) for p in range(NSL)])
                    av.append(("hsdc", g, 1))
                    ops["act"] += av + win["act"].get(g - 1, [])
                    dv = []
                    for p in range(NPAIR):
                        if p == 6 and g + 1 < GT:
                            dv += [("dlt", g + 1, 1), ("dx", g + 1, 1)]
                        if c == 0:
                            dv.append(("cz", (g, p), 1))
                        dv += [("dbq", (g, p), 1), ("scan", (g, p), 1),
                               ("pq", (g, p), 1)]
                    dv.append(("yev", g, 1))
                    ops["dve"] += dv + win["dve"].get(g - 1, [])
                    ops["gps"] += win["gps"].get(g - 1, [])
                for e in ("pe", "act", "dve", "gps"):
                    for q in sorted(win[e]):
                        if q >= GT - 1:
                            ops[e] += win[e][q]
            else:
                for e in ("pe", "act", "dve", "gps"):
                    for q in sorted(win[e]):
                        ops[e] += win[e][q]
            full = ops

            base = {"pe": s_pe.v, "act": s_act.v, "dve": s_dve.v,
                    "gps": s_gps.v}
            for e in ("pe", "act", "dve", "gps"):
                vv = base[e]
                for kind, key, n in full[e]:
                    vv += n
                    V[(kind, key)] = vv
            if w is not None:
                gp_base = [s_gp[0].v, s_gp[1].v]
                cnt4 = [s.v for s in s_spx]
                wp_done = {}
                for m in range(MT):
                    sl = m % 4
                    cnt4[sl] += 16
                    wp_done[m] = (sl, cnt4[sl])
                wld_done = {}
                if c is None:
                    for k in range(MT):
                        sl = k % 4
                        cnt4[sl] += 16
                        wld_done[(0, k)] = (sl, cnt4[sl])
                        wld_done[(1, k)] = (sl, cnt4[sl])
                else:
                    for quad in range(2):
                        for k in range(MT):
                            sl = (quad * MT + k) % 4
                            cnt4[sl] += 16
                            wld_done[(quad, k)] = (sl, cnt4[sl])
                cntg = [s.v for s in s_gpx]
                spill_done = {}
                for m in range(MT):
                    cntg[m % 2] += 16
                    spill_done[m] = (m % 2, cntg[m % 2])
                ldh2_done, splq_done, ldxq_done = {}, {}, {}
                for k in range(MT):
                    p = k % 4
                    cntg[p] += 16
                    ldh2_done[k] = (p, cntg[p])
                    if c is not None:
                        cntg[p] += 16
                        splq_done[k] = (p, cntg[p])
                if c is not None:
                    for k in range(MT):
                        p = k % 4
                        cntg[p] += 16
                        ldxq_done[k] = (p, cntg[p])
                xq_cur = xq1[w % 2] if not cheap else None
                xq_prv = xq1[(w - 1) % 2] if w > 0 else None
                if cheap:
                    xq_prv = xq1[(w - 1) % 2]
            TE = T + 2

            def pe_emit(t):
                for kind, key, n in full["pe"]:
                    if kind == "wdt":
                        g = key
                        if g >= 2:
                            t.wait_ge(s_act.h, V[("sp", g - 2)])
                        inst = t.matmul(pb[g % 2][:, 0:T],
                                        wdtq[0:RNK, g * 128:(g + 1) * 128],
                                        xdbq[0:RNK, 0:T], start=True, stop=True)
                        inst.then_inc(s_pe.h, 1)
                    elif kind == "ymm":
                        g, p = key
                        t.wait_ge(s_dve.h, V[("pq", (g, p))])
                        if p == 0 and g >= 2:
                            t.wait_ge(s_dve.h, V[("yev", g - 2)])
                        sl = (g * NPAIR + p) % NSL
                        bank = pb[4 + g % 2]
                        for j in range(2):
                            inst = t.matmul(bank[:, 0:T], idnt[j],
                                            dBq[sl][:, j * T:(j + 1) * T],
                                            start=(p == 0 and j == 0), stop=False)
                        inst.then_inc(s_pe.h, 1)
                    elif kind == "hsd":
                        g = key
                        t.wait_ge(s_act.h, V[("hsdc", g)])
                        inst = t.matmul(pb[4 + g % 2][:, 0:T], idnt[0],
                                        dlg[g % 2].bitcast(BF16)[:, 0:T],
                                        start=False, stop=True)
                        inst.then_inc(s_pe.h, 1)
                    elif kind == "w1":
                        m = key
                        sl, vv2 = wp_done[m]
                        t.wait_ge(s_spx[sl].h, vv2)
                        if c is None:
                            bmn, bmt = 2 + 2 * (m % 2), 3 + 2 * (m % 2)
                            if m >= 2:
                                t.wait_ge(s_act.h, V[("wev", m - 2)])
                        else:
                            bmn, bmt = 2, 3
                            if m >= 1:
                                t.wait_ge(s_act.h, V[("wev", m - 1)])
                        if not cheap:
                            for k in range(HK):
                                inst = t.matmul(
                                    pb[bmn][:, 0:512],
                                    W1S[sl][:, k * 128:(k + 1) * 128],
                                    xq_cur[:, k * TE:k * TE + 512],
                                    start=(k == 0), stop=(k == HK - 1))
                            for k in range(HK):
                                inst = t.matmul(
                                    pb[bmt][:, 0:2],
                                    W1S[sl][:, k * 128:(k + 1) * 128],
                                    xq_cur[:, k * TE + 512:k * TE + 514],
                                    start=(k == 0), stop=(k == HK - 1))
                        else:
                            for k in range(HK):
                                inst = t.matmul(
                                    pb[bmn][:, 0:2],
                                    W1S[sl][:, k * 128:(k + 1) * 128],
                                    xq_prv[:, k * TE + 512:k * TE + 514],
                                    start=(k == 0), stop=(k == HK - 1))
                        inst.then_inc(s_pe.h, 1)
                    elif kind == "wconv":
                        m = key
                        t.wait_ge(s_dve.h, V[("wdiag", m)])
                        t.wait_ge(s_act.h, V[("wev", m)])
                        bcv = 6 + (m % 2) if c is None else 6
                        if c is None:
                            if m >= 2:
                                t.wait_ge(s_act.h, V[("wsilu", m - 2)])
                        elif m >= 1:
                            t.wait_ge(s_act.h, V[("wsilu", m - 1)])
                        for tap in range(KF):
                            inst = t.matmul(pb[bcv][:, 0:WC],
                                            dgw[(m % 6) * 3 + tap],
                                            hp2[m % 4][:, tap:tap + WC],
                                            start=(tap == 0), stop=(tap == KF - 1))
                        inst.then_inc(s_pe.h, 1)
                    elif kind == "btr1":
                        t.wait_ge(s_dve.h, V[("wmx", MT - 1)])
                        for a in range(4):
                            inst = t.transpose(
                                pb[7].bitcast(BF16)[:, a * 128:(a + 1) * 128],
                                mx2[:, a * 128:(a + 1) * 128], idnt[0])
                        inst.then_inc(s_pe.h, 1)
                    elif kind == "btr2":
                        t.wait_ge(s_dve.h, V[("bred", 0)])
                        for a in range(4):
                            inst = t.transpose(pb[7][0:1, a * 128:(a + 1) * 128],
                                               mxq[:, 4 + a:5 + a], maskd)
                        inst.then_inc(s_pe.h, 1)
                    elif kind == "bbc":
                        t.wait_ge(s_dve.h, V[("bscale", 0)])
                        inst = t.matmul(pb[7][:, 0:T], ones1, r_s,
                                        start=True, stop=True)
                        inst.then_inc(s_pe.h, 1)
                    elif kind == "w2":
                        quad, k = key
                        sl, vv2 = wld_done[(quad, k)]
                        if c is None:
                            if quad == 0:
                                t.wait_ge(s_spx[sl].h, vv2)
                                t.wait_ge(s_act.h, V[("wxq", (0, k))])
                            for mi in range(4):
                                bkz = ((2, 3, 6, 7) if quad == 0
                                       else (0, 1, 4, 5))[mi]
                                inst = t.matmul(
                                    pb[bkz][:, 0:WC],
                                    W1S[sl][:, quad * 512 + mi * 128:
                                            quad * 512 + (mi + 1) * 128],
                                    H2S[k % 4][:, 0:WC],
                                    start=(k == 0), stop=(k == MT - 1))
                            inst.then_inc(s_pe.h, 1)
                        else:
                            t.wait_ge(s_spx[sl].h, vv2)
                            if quad == 0:
                                t.wait_ge(s_act.h, V[("wxq", (0, k))])
                            else:
                                pg, vg = ldxq_done[k]
                                t.wait_ge(s_gpx[pg].h, vg)
                                if k == 0:
                                    t.wait_ge(s_act.h, V[("wofev", 0)])
                            for mi in range(4):
                                bkz = (2, 3, 6, 7)[mi]
                                inst = t.matmul(
                                    pb[bkz][:, 0:WC],
                                    wpC4[sl][:, mi * 128:(mi + 1) * 128],
                                    H2S[k % 4][:, 0:WC],
                                    start=(k == 0), stop=(k == MT - 1))
                            inst.then_inc(s_pe.h, 1)

            def act_emit(s):
                for kind, key, n in full["act"]:
                    if kind == "sp":
                        g = key
                        s.wait_ge(s_pe.h, V[("wdt", g)])
                        if g >= 2:
                            s.wait_ge(s_dve.h, V[("dlt", g - 2)])
                        bank = pb[g % 2][:, 0:T]
                        bcol = bdt_sb[:, g:g + 1]
                        gs = g % 2
                        s.activation(out=sp_a, in_=bank, func=AF.Abs,
                                     bias=bcol)
                        s.activation(out=sp_r[gs], in_=bank, func=AF.Relu,
                                     bias=bcol)
                        s.activation(out=sp_u[gs], in_=sp_a, func=AF.Exp,
                                     scale=-1.0)
                        inst = s.activation(out=sp_u[gs], in_=sp_u[gs],
                                            func=AF.Ln, bias=1.0)
                        inst.then_inc(s_act.h, 1)
                    elif kind == "dA":
                        g, p = key
                        s.wait_ge(s_dve.h, V[("dlt", g)])
                        i = g * NPAIR + p
                        sl = i % NSL
                        if i >= NSL:
                            prev = divmod(i - NSL, NPAIR)
                            s.wait_ge(s_dve.h, V[("scan", prev)])
                        for j in range(2):
                            nn2 = p * 2 + j
                            inst = s.activation(
                                out=dAq[sl][:, j * T:(j + 1) * T],
                                in_=dlg[g % 2], func=AF.Exp,
                                scale=A_sb[:, g * NST + nn2:g * NST + nn2 + 1])
                        inst.then_inc(s_act.h, 1)
                    elif kind == "hsdc":
                        g = key
                        s.wait_ge(s_dve.h, V[("dx", g)])
                        if g >= 1:
                            s.wait_ge(s_pe.h, V[("hsd", g - 1)])
                        inst = s.activation(out=dlg[g % 2].bitcast(BF16)[:, 0:T],
                                            in_=hsb[:, g * T:(g + 1) * T],
                                            func=AF.Copy, scale=D_sb[:, g:g + 1])
                        inst.then_inc(s_act.h, 1)
                    elif kind == "wev":
                        m = key
                        s.wait_ge(s_pe.h, V[("w1", m)])
                        if m >= 4:
                            s.wait_ge(s_pe.h, V[("wconv", m - 4)])
                        bmn = 2 + 2 * (m % 2) if c is None else 2
                        bmt = bmn + 1
                        if not cheap:
                            s.activation(out=hp2[m % 4][:, 0:512],
                                         in_=pb[bmn][:, 0:512], func=AF.Copy)
                            inst = s.activation(out=hp2[m % 4][:, 512:514],
                                                in_=pb[bmt][:, 0:2], func=AF.Copy)
                        else:
                            inst = s.activation(out=hp2[m % 4][:, 0:2],
                                                in_=pb[bmn][:, 0:2], func=AF.Copy)
                        inst.then_inc(s_act.h, 1)
                    elif kind == "wsilu":
                        m = key
                        s.wait_ge(s_pe.h, V[("wconv", m)])
                        if m >= 16:
                            sl2, vv3 = spill_done[m - 16]
                            s.wait_ge(s_gp[sl2].h, vv3)
                            s.wait_ge(s_dve.h, V[("wmx", m - 16)])
                        bcv = 6 + (m % 2) if c is None else 6
                        inst = s.activation(out=h2A[m % 16][:, 0:WC],
                                            in_=pb[bcv][:, 0:WC], func=AF.Silu,
                                            bias=cb2_sb[:, m:m + 1])
                        inst.then_inc(s_act.h, 1)
                    elif kind == "bcp":
                        s.wait_ge(s_pe.h, V[("btr2", 0)])
                        for a in range(4):
                            inst = s.activation(
                                out=r_mx[:, a * 128:(a + 1) * 128],
                                in_=pb[7][0:1, a * 128:(a + 1) * 128],
                                func=AF.Copy)
                        inst.then_inc(s_act.h, 1)
                    elif kind == "bsqrt":
                        s.activation(out=r_t1, in_=r_mx, func=AF.Ln)
                        inst = s.activation(out=r_mx, in_=r_t1, func=AF.Exp,
                                            scale=0.5)
                        inst.then_inc(s_act.h, 1)
                    elif kind == "wofev":
                        quad = key
                        s.wait_ge(s_pe.h, V[("w2", (quad, MT - 1))])
                        for mi in range(4):
                            m = quad * 4 + mi
                            if c is None and quad == 1:
                                bkz = (0, 1, 4, 5)[mi]
                            else:
                                bkz = (2, 3, 6, 7)[mi]
                            inst = s.activation(
                                out=ofb[:, m * T:m * T + WC],
                                in_=pb[bkz][:, 0:WC], func=AF.Copy)
                        inst.then_inc(s_act.h, 1)
                    elif kind == "wxq":
                        _, k = key
                        s.wait_ge(s_gps.h, V[("wmul", (0, k))])
                        s.activation(out=rtmp[:, 0:WC],
                                     in_=scf[:, 512 + (k % 2) * 512:
                                             512 + (k % 2) * 512 + WC],
                                     func=AF.Copy, bias=MAGIC)
                        inst = s.activation(out=H2S[k % 4][:, 0:WC],
                                            in_=rtmp[:, 0:WC],
                                            func=AF.Copy, bias=-MAGIC)
                        inst.then_inc(s_act.h, 1)

            def dve_emit(v):
                for kind, key, n in full["dve"]:
                    if kind == "dlt":
                        g = key
                        v.wait_ge(s_act.h, V[("sp", g)])
                        if g >= 2:
                            v.wait_ge(s_pe.h, V[("hsd", g - 2)])
                        inst = v.tensor_tensor(out=dlg[g % 2], in0=sp_r[g % 2],
                                               in1=sp_u[g % 2], op=ALU.add)
                        inst.then_inc(s_dve.h, 1)
                    elif kind == "dx":
                        g = key
                        inst = v.tensor_tensor(out=dxgs[g % 2], in0=dlg[g % 2],
                                               in1=hsb[:, g * T:(g + 1) * T],
                                               op=ALU.mult)
                        inst.then_inc(s_dve.h, 1)
                    elif kind == "cz":
                        g, p = key
                        inst = v.memset(
                            carry[:, g * NST + p * 2:g * NST + p * 2 + 2], 0.0)
                        inst.then_inc(s_dve.h, 1)
                        v.wait_ge(s_dve.h, V[("cz", (g, p))])
                    elif kind == "dbq":
                        g, p = key
                        i = g * NPAIR + p
                        sl = i % NSL
                        if i >= NSL:
                            v.wait_ge(s_pe.h, V[("ymm", divmod(i - NSL, NPAIR))])
                        inst = v.tensor_tensor(
                            out=dBq[sl].rearrange("q (n t) -> q n t", n=2),
                            in0=dxgs[g % 2].unsqueeze(1).broadcast_to(
                                [128, 2, T]),
                            in1=Brep.rearrange("q (n t) -> q n t", n=NST)
                            [:, p * 2:p * 2 + 2, :], op=ALU.mult)
                        inst.then_inc(s_dve.h, 1)
                    elif kind == "scan":
                        g, p = key
                        sl = (g * NPAIR + p) % NSL
                        v.wait_ge(s_act.h, V[("dA", (g, p))])
                        for j in range(2):
                            nn2 = p * 2 + j
                            inst = v.tensor_tensor_scan(
                                out=zq[:, j * T:(j + 1) * T],
                                data0=dAq[sl][:, j * T:(j + 1) * T],
                                data1=dBq[sl][:, j * T:(j + 1) * T],
                                initial=carry[:, g * NST + nn2:g * NST + nn2 + 1],
                                op0=ALU.mult, op1=ALU.add)
                        inst.then_inc(s_dve.h, 1)
                        v.wait_ge(s_dve.h, V[("scan", (g, p))])
                    elif kind == "pq":
                        g, p = key
                        sl = (g * NPAIR + p) % NSL
                        for j in range(2):
                            nn2 = p * 2 + j
                            v.tensor_copy(
                                out=carry[:, g * NST + nn2:g * NST + nn2 + 1],
                                in_=zq[:, j * T + T - 1:j * T + T])
                        inst = v.tensor_tensor(
                            out=dBq[sl], in0=zq,
                            in1=Crep[:, p * 2 * T:(p * 2 + 2) * T], op=ALU.mult)
                        inst.then_inc(s_dve.h, 1)
                    elif kind == "yev":
                        g = key
                        v.wait_ge(s_pe.h, V[("hsd", g)])
                        inst = v.tensor_tensor(out=yb[:, g * T:(g + 1) * T],
                                               in0=pb[4 + g % 2][:, 0:T],
                                               in1=resb[:, g * T:(g + 1) * T],
                                               op=ALU.mult)
                        inst.then_inc(s_dve.h, 1)
                    elif kind == "wdiag":
                        m = key
                        if m >= 6:
                            v.wait_ge(s_pe.h, V[("wconv", m - 6)])
                        if m == 0:
                            v.memset(mx2, 1.0 if cheap else 0.0)
                        if cheap:
                            v.wait_ge(s_act.h, V[("wev", m)])
                            v.memset(hp2[m % 4][:, 2:4], 0.0)
                        for tap in range(KF):
                            inst = v.tensor_scalar(
                                out=dgw[(m % 6) * 3 + tap], in0=maskd,
                                scalar1=cw2_sb[:, m * KF + tap:m * KF + tap + 1],
                                scalar2=None, op0=ALU.mult)
                        inst.then_inc(s_dve.h, 1)
                    elif kind == "wmx":
                        m = key
                        v.wait_ge(s_act.h, V[("wsilu", m)])
                        v.tensor_tensor(out=mxsq[:, 0:WC],
                                        in0=h2A[m % 16][:, 0:WC],
                                        in1=h2A[m % 16][:, 0:WC], op=ALU.mult)
                        inst = v.tensor_tensor(out=mx2[:, 0:WC],
                                               in0=mx2[:, 0:WC],
                                               in1=mxsq[:, 0:WC], op=ALU.max)
                        inst.then_inc(s_dve.h, 1)
                    elif kind == "bred":
                        v.wait_ge(s_pe.h, V[("btr1", 0)])
                        inst = v.tensor_reduce(
                            out=mxq[:, 4:8],
                            in_=pb[7].bitcast(BF16)[:, 0:512].rearrange(
                                "p (a q) -> p a q", a=4),
                            axis=mybir.AxisListType.X, op=ALU.max)
                        inst.then_inc(s_dve.h, 1)
                    elif kind == "bscale":
                        v.wait_ge(s_act.h, V[("bsqrt", 0)])
                        v.tensor_scalar(out=r_t1, in0=r_mx, scalar1=1e-5,
                                        scalar2=None, op0=ALU.max)
                        v.reciprocal(out=r_s, in_=r_t1)
                        inst = v.tensor_scalar(out=r_s, in0=r_s, scalar1=127.0,
                                               scalar2=None, op0=ALU.mult)
                        inst.then_inc(s_dve.h, 1)
                    elif kind == "bcopy":
                        v.wait_ge(s_pe.h, V[("bbc", 0)])
                        inst = v.tensor_copy(out=scf[:, 0:T], in_=pb[7][:, 0:T])
                        inst.then_inc(s_dve.h, 1)

            PE(pe_emit)
            SC(act_emit)
            VE(dve_emit)
            for e, sem in (("pe", s_pe), ("act", s_act), ("dve", s_dve),
                           ("gps", s_gps)):
                tot = base[e] + sum(n for _, _, n in full[e])
                sem.v = tot

            if w is not None:
                def f_sy(sy):
                    for m in range(MT):
                        if m >= 4:
                            sy.wait_ge(s_pe.h, V[("w1", m - 4)])
                        s_spx[m % 4].inc(sy.dma_start(
                            out=W1S[m % 4][:, 0:HK * 128].rearrange(
                                "p (k c2) -> p k c2", c2=128),
                            in_=wq1_h[m, :, :].rearrange("(k p) c2 -> p k c2",
                                                         p=128)))
                    if c is None:
                        for k in range(MT):
                            sl = k % 4
                            if k >= 4:
                                sy.wait_ge(s_pe.h, V[("w2", (1, k - 4))])
                            else:
                                sy.wait_ge(s_act.h, V[("wev", MT - 1)])
                                sy.wait_ge(s_pe.h, V[("wconv", MT - 1)])
                            s_spx[sl].inc(sy.dma_start(
                                out=W1S[sl][:, 0:H], in_=wq2_h[k, :, :]))
                    else:
                        for quad in range(2):
                            for k in range(MT):
                                idx = quad * MT + k
                                sl = idx % 4
                                if idx >= 4:
                                    pidx = idx - 4
                                    sy.wait_ge(
                                        s_pe.h,
                                        V[("w2", (pidx // MT, pidx % MT))])
                                else:
                                    sy.wait_ge(s_pe.h, V[("w1", MT - 1)])
                                s_spx[sl].inc(sy.dma_start(
                                    out=wpC4[sl],
                                    in_=wq2_h[k, :,
                                              quad * 512:(quad + 1) * 512]))
                SY(f_sy)

                def f_gp(g2):
                    for m in range(MT):
                        g2.wait_ge(s_dve.h, V[("wmx", m)])
                        if m >= 2:
                            g2.wait_ge(s_gp[m % 2].h, spill_done[m - 2][1])
                        s_gp[m % 2].inc(g2.dma_start(
                            out=h2sp_h[m, :, 0:WC],
                            in_=h2A[m % 16][:, 0:WC],
                            allow_slow_non_contiguous=(WC < 8)))
                    for k in range(MT):
                        p = k % 4
                        g2.wait_ge(s_gp[k % 2].h, spill_done[k][1])
                        if k >= 4:
                            if c is None:
                                g2.wait_ge(s_pe.h, V[("w2", (1, k - 4))])
                            else:
                                g2.wait_ge(s_pe.h, V[("w2", (0, k - 4))])
                                g2.wait_ge(s_gpx[p].h, splq_done[k - 4][1])
                        s_gpx[p].inc(g2.dma_start(
                            out=H2S[p][:, 0:WC], in_=h2sp_h[k, :, 0:WC],
                            allow_slow_non_contiguous=(WC < 8)))
                        if c is not None and k >= 1:
                            pp = (k - 1) % 4
                            g2.wait_ge(s_act.h, V[("wxq", (0, k - 1))])
                            s_gpx[pp].inc(g2.dma_start(
                                out=h2sp_h[k - 1, :, 0:WC],
                                in_=H2S[pp][:, 0:WC],
                                allow_slow_non_contiguous=(WC < 8)))
                        g2.wait_ge(s_gpx[p].h, ldh2_done[k][1])
                        g2.wait_ge(s_dve.h, V[("bcopy", 0)])
                        if k >= 2:
                            g2.wait_ge(s_act.h, V[("wxq", (0, k - 2))])
                        inst = g2.tensor_tensor(
                            out=scf[:, 512 + (k % 2) * 512:
                                    512 + (k % 2) * 512 + WC],
                            in0=H2S[p][:, 0:WC], in1=scf[:, 0:WC],
                            op=ALU.mult)
                        inst.then_inc(s_gps.h, 1)
                    if c is not None:
                        g2.wait_ge(s_act.h, V[("wxq", (0, MT - 1))])
                        s_gpx[(MT - 1) % 4].inc(g2.dma_start(
                            out=h2sp_h[MT - 1, :, 0:WC],
                            in_=H2S[(MT - 1) % 4][:, 0:WC],
                            allow_slow_non_contiguous=(WC < 8)))
                        for k in range(MT):
                            p = k % 4
                            if k >= 4:
                                g2.wait_ge(s_pe.h, V[("w2", (1, k - 4))])
                            else:
                                g2.wait_ge(s_gpx[p].h, splq_done[28 + p][1])
                                g2.wait_ge(s_pe.h, V[("w2", (0, 28 + p))])
                            s_gpx[p].inc(g2.dma_start(
                                out=H2S[p][:, 0:WC], in_=h2sp_h[k, :, 0:WC],
                                allow_slow_non_contiguous=(WC < 8)))
                GP(f_gp)

        def window_tail(w, n_chunks, bar=True):
            cheap = (w == n_chunks)
            if debug and w == 0:
                dbg_tap("d_of0", ofb[:, 0:HK * T], 128, HK * T, BF16)
                hh = dbg_out("d_h2w0", [MT, 128, T], BF16)
                barrier()

                def f2t(sy):
                    s_spd.inc(sy.dma_start(out=hh, in_=h2sp_h))
                SY(f2t)
                barrier()
            ln_quant_stage(ofb[:, 0:HK * T], True, 'out')
            dve_ln = s_dve.v
            gpd0x = s_gpd.v
            dve0x = s_dve.v
            gp_base = [s_gp[0].v, s_gp[1].v]
            if w == 0:
                src0, dst0, width = 0, 1, T - 1
            elif cheap:
                src0, dst0, width = w * T - 1, 0, 1
            else:
                src0, dst0, width = w * T - 1, 0, T

            def f_gp3(g2):
                g2.wait_ge(s_dve.h, dve_ln)
                for k in range(HK):
                    if k >= 1:
                        g2.wait_ge(s_dve.h, dve0x + k)
                    s_gp[k % 2].inc(g2.dma_start(
                        out=(sp_a if k % 2 == 0 else sp_r[0])[:, dst0:dst0 + width],
                        in_=xsp_h[k * 128:(k + 1) * 128, src0:src0 + width],
                        allow_slow_non_contiguous=(width < 8)))
            GP(f_gp3)

            def f_ve10(v):
                for k in range(HK):
                    v.wait_ge(s_gp[k % 2].h, gp_base[k % 2] + 16 * (k // 2 + 1))
                    s_dve.inc(v.tensor_tensor(
                        out=xpr[:, k * T:(k + 1) * T],
                        in0=scf[:, k * T:(k + 1) * T],
                        in1=sp_a if k % 2 == 0 else sp_r[0], op=ALU.add))
            VE(f_ve10)

            def f_gp4(g2):
                for k in range(HK):
                    g2.wait_ge(s_dve.h, dve0x + k + 1)
                    s_gpd.inc(g2.dma_start(
                        out=out_h[k * 128:(k + 1) * 128, src0:src0 + width],
                        in_=xpr[:, k * T + dst0:k * T + dst0 + width],
                        allow_slow_non_contiguous=(width < 8)))
            GP(f_gp4)
            if bar:
                barrier()

        def pbc(bank, k):
            return pb[bank][:, 0:T].unsqueeze(1).broadcast_to([128, k, T])

        def ln_quant_stage(src_ap, src_is_bf, mode, c=None):
            pe0, act0, dve0 = s_pe.v, s_act.v, s_dve.v
            n_k = HK
            D = n_k * 128
            has_cast = not src_is_bf
            dve_cast = dve0 + 1 if has_cast else dve0
            dve_stat = dve_cast + 1
            act_sq = lambda k: act0 + k + 1
            act_rstd = act0 + n_k + 1
            pe_sum = lambda k: pe0 + k + 1
            pe_done_sums = pe0 + n_k

            def f_sc(s):
                for k in range(n_k):
                    if k >= 2:
                        s.wait_ge(s_pe.h, pe_sum(k - 2))
                    if has_cast and k == 0:
                        s.wait_ge(s_dve.h, dve_cast)
                    srck = (src_ap[:, k * T:(k + 1) * T] if src_is_bf
                            else nb1[:, k * T:(k + 1) * T])
                    s_act.inc(s.activation(out=h2t[k % 2][:, 0:T], in_=srck,
                                           func=AF.Square))
                s.wait_ge(s_dve.h, dve_stat)
                s.activation(out=r_t2, in_=r_t1, func=AF.Ln)
                s_act.inc(s.activation(out=r_r, in_=r_t2, func=AF.Exp,
                                       scale=-0.5))
            SC(f_sc)
            assert s_act.v == act_rstd

            if has_cast:
                def f_vc(v):
                    s_dve.inc(v.tensor_copy(out=nb1[:, 0:n_k * T], in_=src_ap))
                VE(f_vc)

            def f_pe(t):
                for k in range(n_k):
                    t.wait_ge(s_act.h, act_sq(k))
                    o = onesh_a if k % 2 == 0 else onesh_b
                    rhs = (src_ap[:, k * T:(k + 1) * T] if src_is_bf
                           else nb1[:, k * T:(k + 1) * T])
                    t.matmul(pb[6][0:1, 0:T], o, rhs,
                             start=(k == 0), stop=(k == n_k - 1))
                    o2 = onesh_b if k % 2 == 0 else onesh_a
                    inst = t.matmul(pb[7][0:1, 0:T], o2, h2t[k % 2][:, 0:T],
                                    start=(k == 0), stop=(k == n_k - 1))
                    s_pe.inc(inst)
                t.wait_ge(s_dve.h, dve_stat)
                s_pe.inc(t.matmul(pb[4][:, 0:T], ones1, r_m,
                                  start=True, stop=True))
                t.wait_ge(s_act.h, act_rstd)
                s_pe.inc(t.matmul(pb[5][:, 0:T], ones1, r_r,
                                  start=True, stop=True))
            PE(f_pe)
            pe_bc_m = pe_done_sums + 1
            pe_bc_r = pe_done_sums + 2

            def f_ve(v):
                if has_cast:
                    pass
                v.wait_ge(s_pe.h, pe_done_sums)
                v.tensor_scalar(out=r_m, in0=pb[6][0:1, 0:T], scalar1=1.0 / D,
                                scalar2=None, op0=ALU.mult)
                v.tensor_scalar(out=r_t1, in0=pb[7][0:1, 0:T], scalar1=1.0 / D,
                                scalar2=None, op0=ALU.mult)
                v.tensor_tensor(out=r_t2, in0=r_m, in1=r_m, op=ALU.mult)
                v.tensor_tensor(out=r_t1, in0=r_t1, in1=r_t2, op=ALU.subtract)
                s_dve.inc(v.tensor_scalar(out=r_t1, in0=r_t1, scalar1=0.0,
                                          scalar2=EPS, op0=ALU.max, op1=ALU.add))
                assert s_dve.v == dve_stat
                if mode == 'xn1':
                    v.wait_ge(s_pe.h, pe_bc_r)
                    v.tensor_tensor(out=t3(scf[:, 0:HK * T], HK),
                                    in0=t3(src_ap, HK), in1=pbc(4, HK),
                                    op=ALU.subtract)
                    s_dve.inc(v.tensor_tensor(out=t3(nb1[:, 0:HK * T], HK),
                                              in0=t3(scf[:, 0:HK * T], HK),
                                              in1=pbc(5, HK), op=ALU.mult))
                elif mode == 'resid':
                    v.wait_ge(s_pe.h, pe_bc_r)
                    v.tensor_tensor(out=t3(scf[:, 0:HK * T], HK),
                                    in0=t3(src_ap, HK), in1=pbc(4, HK),
                                    op=ALU.subtract)
                    v.tensor_tensor(out=t3(scf[:, 0:HK * T], HK),
                                    in0=t3(scf[:, 0:HK * T], HK),
                                    in1=pbc(5, HK), op=ALU.mult)
                    s_dve.inc(v.tensor_tensor(out=xpr[:, 0:HK * T],
                                              in0=scf[:, 0:HK * T],
                                              in1=xf[:, 0:HK * T], op=ALU.add))
                elif mode == 'out':
                    v.wait_ge(s_pe.h, pe_bc_r)
                    v.tensor_tensor(out=t3(scf[:, 0:HK * T], HK),
                                    in0=t3(src_ap, HK), in1=pbc(4, HK),
                                    op=ALU.subtract)
                    s_dve.inc(v.tensor_tensor(out=t3(scf[:, 0:HK * T], HK),
                                              in0=t3(scf[:, 0:HK * T], HK),
                                              in1=pbc(5, HK), op=ALU.mult))
            VE(f_ve)

            if mode == 'quant':
                quant1_tail(src_ap, c)

        def quant1_tail(src_ap, c):
            pe0, act0, dve0 = s_pe.v, s_act.v, s_dve.v
            pe_ready = s_pe.v

            def f_ve(v):
                v.wait_ge(s_pe.h, pe_ready)
                for k in range(HK):
                    v.tensor_tensor(out=h2t[0][:, 0:T],
                                    in0=src_ap[:, k * T:(k + 1) * T],
                                    in1=pb[4][:, 0:T], op=ALU.subtract)
                    v.tensor_tensor(out=h2t[1][:, 0:T], in0=h2t[0][:, 0:T],
                                    in1=h2t[0][:, 0:T], op=ALU.mult)
                    if k == 0:
                        v.tensor_copy(out=mx2, in_=h2t[1][:, 0:T])
                    else:
                        v.tensor_tensor(out=mx2, in0=mx2, in1=h2t[1][:, 0:T],
                                        op=ALU.max)
                s_dve.inc(v.engine_nop())
            VE(f_ve)
            dve_tree = dve0 + 1
            colmax_to_row(0, dve_tree)
            act_mx = s_act.v
            dve_stat2 = s_dve.v + 1

            def f_sc2(s):
                s.activation(out=r_t1, in_=r_mx, func=AF.Ln)
                s_act.inc(s.activation(out=r_mx, in_=r_t1, func=AF.Exp,
                                       scale=0.5))
            SC(f_sc2)
            act_sqrt = s_act.v

            def f_ve3(v):
                v.wait_ge(s_act.h, act_sqrt)
                v.tensor_tensor(out=r_t1, in0=r_r, in1=r_mx, op=ALU.mult)
                v.tensor_scalar(out=r_t1, in0=r_t1, scalar1=1e-5, scalar2=None,
                                op0=ALU.max)
                v.reciprocal(out=r_s, in_=r_t1)
                v.tensor_scalar(out=r_s, in0=r_s, scalar1=127.0, scalar2=None,
                                op0=ALU.mult)
                v.tensor_tensor(out=r_t2, in0=r_r, in1=r_s, op=ALU.mult)
                v.tensor_tensor(out=r_mx, in0=r_m, in1=r_t2, op=ALU.mult)
                v.tensor_scalar(out=r_mx, in0=r_mx, scalar1=-1.0, scalar2=None,
                                op0=ALU.mult)
                s_dve.inc(v.reciprocal(out=r_is, in_=r_s))
            VE(f_ve3)
            dve_sc = s_dve.v

            def f_pe4(t):
                t.wait_ge(s_dve.h, dve_sc)
                t.matmul(pb[4][:, 0:T], ones1, r_t2, start=True, stop=True)
                t.matmul(pb[5][:, 0:T], ones1, r_mx, start=True, stop=True)
                s_pe.inc(t.matmul(pb[6][:, 0:T], ones1, r_is,
                                  start=True, stop=True))
            PE(f_pe4)
            pe_bc = s_pe.v

            def f_ve4(v):
                v.wait_ge(s_pe.h, pe_bc)
                v.tensor_tensor(out=t3(scf[:, 0:HK * T], HK),
                                in0=t3(src_ap, HK), in1=pbc(4, HK), op=ALU.mult)
                v.tensor_tensor(out=t3(scf[:, 0:HK * T], HK),
                                in0=t3(scf[:, 0:HK * T], HK), in1=pbc(5, HK),
                                op=ALU.add)
                v.tensor_scalar(out=scf[:, 0:HK * T], in0=scf[:, 0:HK * T],
                                scalar1=MAGIC, scalar2=MAGIC,
                                op0=ALU.add, op1=ALU.subtract)
                xq_view = xq1[c % 2].rearrange("p (k t) -> p k t", k=HK)[:, :, 2:]
                v.tensor_tensor(out=xq_view, in0=t3(scf[:, 0:HK * T], HK),
                                in1=pbc(6, HK), op=ALU.mult)
                if c > 0:
                    prev = xq1[(c - 1) % 2]
                    for k in range(HK):
                        inst = v.tensor_copy(
                            out=xq1[c % 2][:, k * (T + 2):k * (T + 2) + 2],
                            in_=prev[:, k * (T + 2) + 512:k * (T + 2) + 514])
                s_dve.inc(v.engine_nop())
            VE(f_ve4)

        def colmax_to_row(mxq_off, dve_ready):
            pe0 = s_pe.v

            def f_pe(t):
                t.wait_ge(s_dve.h, dve_ready)
                for a in range(4):
                    inst = t.transpose(
                        pb[7].bitcast(BF16)[:, a * 128:(a + 1) * 128],
                        mx2[:, a * 128:(a + 1) * 128], idnt[0])
                s_pe.inc(inst)
            PE(f_pe)

            def f_ve(v):
                v.wait_ge(s_pe.h, pe0 + 1)
                s_dve.inc(v.tensor_reduce(
                    out=mxq[:, mxq_off:mxq_off + 4],
                    in_=pb[7].bitcast(BF16)[:, 0:512].rearrange(
                        "p (a q) -> p a q", a=4),
                    axis=mybir.AxisListType.X, op=ALU.max))
            VE(f_ve)
            dve_red = s_dve.v

            def f_pe2(t):
                t.wait_ge(s_dve.h, dve_red)
                for a in range(4):
                    inst = t.transpose(pb[6][0:1, a * 128:(a + 1) * 128],
                                       mxq[:, mxq_off + a:mxq_off + a + 1],
                                       maskd)
                s_pe.inc(inst)
            PE(f_pe2)
            pe_tr2 = s_pe.v

            def f_sc(s):
                s.wait_ge(s_pe.h, pe_tr2)
                for a in range(4):
                    inst = s.activation(out=r_mx[:, a * 128:(a + 1) * 128],
                                        in_=pb[6][0:1, a * 128:(a + 1) * 128],
                                        func=AF.Copy)
                s_act.inc(inst)
            SC(f_sc)

        for c in range(n_chunks):
            def f_gpx(g2, c=c):
                for k in range(HK):
                    s_gpd.inc(g2.dma_start(out=xf[:, k * T:(k + 1) * T],
                                           in_=x_h[k * 128:(k + 1) * 128,
                                                   c * T:(c + 1) * T]))
            GP(f_gpx)
            barrier()

            ln_quant_stage(xf[:, 0:HK * T], True, 'xn1')
            barrier()
            if c == 0:
                dbg_tap("d_xn1", nb1[:, 0:HK * T], 128, HK * T, BF16)

            pe0, act0, dve0, spd0 = s_pe.v, s_act.v, s_dve.v, s_spd.v
            mm_done = {}
            conv_done = {}
            pe_order = []
            for m in range(WMT):
                pe_order.append(("mm", m))
                if m >= 2 and m - 2 < GT:
                    pe_order.append(("conv", m - 2))
            vv = pe0
            for kind, m in pe_order:
                vv += 1
                (mm_done if kind == "mm" else conv_done)[m] = vv
            evac_done = {}
            silu_done = {}
            act_order = []
            for m in range(WMT):
                act_order.append(("evac", m))
                if m >= 1 and m - 1 < GT:
                    act_order.append(("silu", m - 1))
            vv = act0
            for kind, m in act_order:
                vv += 1
                (evac_done if kind == "evac" else silu_done)[m] = vv
            diag_done = {m: dve0 + m + 1 for m in range(GT)}

            sp_base4 = [s.v for s in s_spx]
            S2B = (0, 1, 4, 5)

            def f_sy(sy, c=c):
                for m in range(WMT):
                    if m >= 4:
                        sy.wait_ge(s_pe.h, mm_done[m - 4])
                    s_spx[m % 4].inc(sy.dma_start(
                        out=W1S[m % 4][:, 0:HK * 128].rearrange(
                            "p (k c2) -> p k c2", c2=128),
                        in_=wbf_h[m, :, :].rearrange("(k p) c2 -> p k c2",
                                                     p=128)))
            SY(f_sy)

            def f_pe(t, c=c):
                for kind, m in pe_order:
                    if kind == "mm":
                        t.wait_ge(s_spx[m % 4].h,
                                  sp_base4[m % 4] + 16 * (m // 4 + 1))
                        if m >= 4:
                            t.wait_ge(s_act.h, evac_done[m - 4])
                        bank = pb[S2B[m % 4]]
                        for k in range(HK):
                            inst = t.matmul(bank[:, 0:T],
                                            W1S[m % 4][:, k * 128:(k + 1) * 128],
                                            nb1[:, k * T:(k + 1) * T],
                                            start=(k == 0), stop=(k == HK - 1))
                        s_pe.inc(inst)
                    else:
                        t.wait_ge(s_dve.h, diag_done[m])
                        if m >= 2:
                            t.wait_ge(s_act.h, silu_done[m - 2])
                        bank = pb[2 + m % 2]
                        for tap in range(KM):
                            inst = t.matmul(bank[:, 0:T],
                                            diag[(m % 2) * KM + tap],
                                            hp[m % 2][:, tap:tap + T],
                                            start=(tap == 0),
                                            stop=(tap == KM - 1))
                        s_pe.inc(inst)
            PE(f_pe)

            def f_sc(s, c=c):
                for kind, m in act_order:
                    if kind == "evac":
                        s.wait_ge(s_pe.h, mm_done[m])
                        if m < GT:
                            if m >= 2:
                                s.wait_ge(s_pe.h, conv_done[m - 2])
                                s.wait_ge(s_dve.h, diag_done[m - 2])
                            s_act.inc(s.activation(
                                out=hp[m % 2][:, KM - 1:KM - 1 + T],
                                in_=pb[S2B[m % 4]][:, 0:T], func=AF.Copy))
                        else:
                            g = m - GT
                            s_act.inc(s.activation(
                                out=resb[:, g * T:(g + 1) * T],
                                in_=pb[S2B[m % 4]][:, 0:T], func=AF.Silu))
                    else:
                        s.wait_ge(s_pe.h, conv_done[m])
                        s_act.inc(s.activation(out=hsb[:, m * T:(m + 1) * T],
                                               in_=pb[2 + m % 2][:, 0:T],
                                               func=AF.Silu))
            SC(f_sc)

            def f_ve(v, c=c):
                for m in range(GT):
                    v.wait_ge(s_act.h, evac_done[m])
                    if m >= 2:
                        v.wait_ge(s_pe.h, conv_done[m - 2])
                    if c == 0:
                        v.memset(hp[m % 2][:, 0:KM - 1], 0.0)
                    else:
                        v.tensor_copy(out=hp[m % 2][:, 0:KM - 1],
                                      in_=mt3[:, m * (KM - 1):(m + 1) * (KM - 1)])
                    v.tensor_copy(out=mt3[:, m * (KM - 1):(m + 1) * (KM - 1)],
                                  in_=hp[m % 2][:, T:T + KM - 1])
                    for tap in range(KM):
                        inst = v.tensor_scalar(
                            out=diag[(m % 2) * KM + tap], in0=maskd,
                            scalar1=cw1_sb[:, m * KM + tap:m * KM + tap + 1],
                            scalar2=None, op0=ALU.mult)
                    s_dve.inc(inst)
            VE(f_ve)
            barrier()
            if c == 0:
                dbg_tap("d_hs", hsb[:, 0:GT * T], 128, GT * T, BF16)
                dbg_tap("d_res", resb[:, 0:GT * T], 128, GT * T, BF16)

            pe0 = s_pe.v

            def f_pe(t):
                for k in range(GT):
                    inst = t.matmul(pb[4][0:96, 0:T], wxq[:, k * 96:(k + 1) * 96],
                                    hsb[:, k * T:(k + 1) * T],
                                    start=(k == 0), stop=(k == GT - 1))
                s_pe.inc(inst)
            PE(f_pe)

            def f_ve(v):
                v.wait_ge(s_pe.h, pe0 + 1)
                s_dve.inc(v.tensor_copy(out=xdbq[0:96, 0:T],
                                        in_=pb[4][0:96, 0:T]))
            VE(f_ve)
            dve_xdb = s_dve.v

            def f_gp(g2):
                g2.wait_ge(s_dve.h, dve_xdb)
                s_gpd.inc(g2.dma_start(out=bcr_h, in_=xdbq[64:96, 0:T]))
                g2.wait_ge(s_gpd.h, s_gpd.v)
                for n in range(NST):
                    s_gpd.inc(g2.dma_start(
                        out=Brep[:, n * T:(n + 1) * T],
                        in_=bcr_h[n:n + 1, :].broadcast_to([128, T])))
                    s_gpd.inc(g2.dma_start(
                        out=Crep[:, n * T:(n + 1) * T],
                        in_=bcr_h[NST + n:NST + n + 1, :].broadcast_to([128, T])))
            GP(f_gp)
            barrier()

            ddbg = None
            if debug and c == 0:
                ddbg = dbg_out("d_delta", [128, GT * T], FP16)
                dbg_tap("d_Brep", Brep[:, 0:NST * T], 128, NST * T, BF16)


            merged_stage(c, (c - 1) if c > 0 else None, n_chunks)
            barrier()
            if c > 0:
                window_tail(c - 1, n_chunks, bar=False)
            if c == 0 and debug:
                dbg_tap("d_y", yb[:, 0:GT * T], 128, GT * T, BF16)

            pe0, act0, spd0 = s_pe.v, s_act.v, s_spd.v
            mm_done = {m: pe0 + m + 1 for m in range(HK)}
            evac_done = {m: act0 + m + 1 for m in range(HK)}
            sp_base = [s_sp[0].v, s_sp[1].v]
            spd_done = {}
            for m in range(HK):
                if m % 2 == 0:
                    spd_done[m] = (0, sp_base[0] + 16 * (m // 2 + 1))
                else:
                    spd_done[m] = (1, sp_base[1] + 32 * (m // 2 + 1))

            def lhs_wout(m, k):
                if m % 2 == 0:
                    return wpB[0][:, k * 128:(k + 1) * 128]
                return wpA[k // HK][:, (k % HK) * 128:(k % HK + 1) * 128]

            def f_sy(sy):
                for m in range(HK):
                    if m >= 2:
                        sy.wait_ge(s_pe.h, mm_done[m - 2])
                    if m % 2 == 0:
                        s_sp[0].inc(sy.dma_start(
                            out=wpB[0][:, 0:GT * 128].rearrange(
                                "p (k c2) -> p k c2", c2=128),
                            in_=wob_h[m, :, :].rearrange("(k p) c2 -> p k c2",
                                                         p=128)))
                    else:
                        for half in range(2):
                            s_sp[1].inc(sy.dma_start(
                                out=wpA[half][:, 0:HK * 128].rearrange(
                                    "p (k c2) -> p k c2", c2=128),
                                in_=wob_h[m, half * HK * 128:(half + 1) * HK * 128,
                                          :].rearrange("(k p) c2 -> p k c2",
                                                       p=128)))
            SY(f_sy)

            def f_pe(t):
                for m in range(HK):
                    sl, vv2 = spd_done[m]
                    t.wait_ge(s_sp[sl].h, vv2)
                    if m >= 2:
                        t.wait_ge(s_act.h, evac_done[m - 2])
                    bank = pb[m % 2]
                    for k in range(GT):
                        inst = t.matmul(bank[:, 0:T], lhs_wout(m, k),
                                        yb[:, k * T:(k + 1) * T],
                                        start=(k == 0), stop=(k == GT - 1))
                    s_pe.inc(inst)
            PE(f_pe)

            def f_sc(s):
                for m in range(HK):
                    s.wait_ge(s_pe.h, mm_done[m])
                    s_act.inc(s.activation(out=Brep[:, m * T:(m + 1) * T],
                                           in_=pb[m % 2][:, 0:T], func=AF.Copy))
            SC(f_sc)
            barrier()

            ln_quant_stage(Brep[:, 0:HK * T], True, 'resid')
            barrier()
            if c == 0:
                dbg_tap("d_xp", xpr[:, 0:HK * T], 128, HK * T, F32)

            def f_gp(g2, c=c):
                for k in range(HK):
                    s_gpd.inc(g2.dma_start(
                        out=xsp_h[k * 128:(k + 1) * 128, c * T:(c + 1) * T],
                        in_=xpr[:, k * T:(k + 1) * T]))
            GP(f_gp)

            ln_quant_stage(xpr[:, 0:HK * T], False, 'quant', c=c)
            barrier()
            if c == 0:
                dbg_tap("d_xq1",
                        xq1[0].rearrange("p (k t) -> p k t", k=HK)[:, :, 2:],
                        128, HK * T, BF16)


        merged_stage(None, n_chunks - 1, n_chunks)
        barrier()
        window_tail(n_chunks - 1, n_chunks)
        merged_stage(None, n_chunks, n_chunks)
        barrier()
        window_tail(n_chunks, n_chunks)


        def fin(sy):
            sy.wait_ge(s_gpd.h, s_gpd.v)
            sy.wait_ge(s_spd.h, s_spd.v)
        SY(fin)

    return nc, dbg_handles



def prep_inputs(inputs, core):
    x = inputs["L0_attn_hidden_states"]
    d = {}
    d["x"] = np.ascontiguousarray(np.asarray(x)[core].T)
    d["winT"] = np.ascontiguousarray(np.asarray(inputs["L0_attn_w_in"]).T)
    d["wxT"] = np.ascontiguousarray(np.asarray(inputs["L0_attn_w_xproj"]).T)
    d["wdtT"] = np.ascontiguousarray(np.asarray(inputs["L0_attn_w_dt"]).T)
    d["woutT"] = np.ascontiguousarray(np.asarray(inputs["L0_attn_w_out"]).T)
    d["w1T"] = np.ascontiguousarray(np.asarray(inputs["L0_ffn_w1"]).T)
    d["w2T"] = np.ascontiguousarray(np.asarray(inputs["L0_ffn_w2"]).T)
    d["bdt"] = np.ascontiguousarray(
        np.asarray(inputs["L0_attn_b_dt"]).reshape(GT, 128).T)
    d["alog"] = np.ascontiguousarray(
        np.asarray(inputs["L0_attn_A_log"]).reshape(GT, 128, NST)
        .transpose(1, 0, 2).reshape(128, GT * NST))
    d["dvec"] = np.ascontiguousarray(
        np.asarray(inputs["L0_attn_D"]).reshape(GT, 128).T)
    d["cw1"] = np.ascontiguousarray(
        np.asarray(inputs["L0_attn_w_conv"]).reshape(GT, 128, KM)
        .transpose(1, 0, 2).reshape(128, GT * KM))
    d["cw2"] = np.ascontiguousarray(
        np.asarray(inputs["L0_ffn_conv_w"]).reshape(MT, 128, KF)
        .transpose(1, 0, 2).reshape(128, MT * KF))
    d["cb2"] = np.ascontiguousarray(
        np.asarray(inputs["L0_ffn_conv_b"]).reshape(MT, 128).T)
    return {k: np.ascontiguousarray(np.asarray(v, dtype=np.float32))
            for k, v in d.items()}


def kernel(**inputs):
    nc, _ = build(debug=False)
    in_maps = [prep_inputs(inputs, core) for core in range(8)]
    res = run_bass_kernel_spmd(nc, in_maps, core_ids=list(range(8)))
    outs = [np.asarray(r["out"], np.float32).T for r in res.results]
    return np.stack(outs, axis=0)
